# version 08
# baseline (speedup 1.0000x reference)
import numpy as np
import ml_dtypes
import concourse.bass as bass
import concourse.mybir as mybir
from concourse.bass_utils import run_bass_kernel_spmd

F32 = mybir.dt.float32
BF16 = mybir.dt.bfloat16
I32 = mybir.dt.int32
AF = mybir.ActivationFunctionType
OP = mybir.AluOpType
AXX = mybir.AxisListType.X

D = 1024
KC = 8
NPAIR = 8
L = 4096
TOWN = 2048
NG = 32
NGOWN = 16
FF = 4096
FFC = 32
NEXP = 8
GH = 8
NGRP = FFC // GH
EPS_LN = 1e-5
EPS_ATTN = 1e-6
INV_D = 1.0 / D

DMA_SCRATCH = 16384
SBUF_TOP = 229344


def pack_w1(W):
    din, dout = W.shape
    return np.ascontiguousarray(
        W.reshape(din // 128, 128, dout // 128, 128).transpose(2, 1, 0, 3)
        .reshape(dout // 128, 128, din))


def pack_w2(W, G=GH):
    din, dout = W.shape
    ngb = din // (G * 128)
    return np.ascontiguousarray(
        W.reshape(ngb, G, 128, dout // 128, 128).transpose(0, 3, 2, 1, 4)
        .reshape(ngb, dout // 128, 128, G * 128))


def pp(vec):
    n = vec.shape[0] // 128
    return np.ascontiguousarray(vec.reshape(n, 128).T)


def make_e2():
    e2 = np.zeros((16, NPAIR * 128), np.float32)
    for p in range(NPAIR):
        e2[2 * p, p * 128:p * 128 + 64] = 1.0
        e2[2 * p + 1, p * 128 + 64:p * 128 + 128] = 1.0
    return e2


class Sem:
    def __init__(self, h, name):
        self.h = h
        self.name = name
        self.n = 0


class KB:
    def __init__(self, debug=(), ngl=NG, alim=6, moe_e=NEXP, moe_y=True, gser=False, moe_wb=True, emod=NEXP, hblk=10000):
        self.debug = set(debug)
        self.ngl = ngl
        self.alim = alim
        self.moe_e = moe_e
        self.moe_y = moe_y
        self.gser = gser
        self.moe_wb = moe_wb
        self.emod = emod
        self.hblk = hblk
        nc = bass.Bass(trn_type="TRN2", target_bir_lowering=False,
                       dynamic_dma_scratch_size=DMA_SCRATCH)
        self.nc = nc
        self.base = DMA_SCRATCH + 32
        self.L = {k: [] for k in "TVSGY"}
        self.meta = {k: [] for k in "TVSGY"}
        self._decl_dram()
        self._decl_sbuf()
        self.ps = [nc.alloc_psum_tensor(f"ps{i}", [128, 512], F32) for i in range(8)]
        mk = lambda n: Sem(nc.alloc_semaphore(n), n)
        self.s_lx = [mk("lx0"), mk("lx1"), mk("lx2"), mk("lx3")]
        self.s_w1 = [mk("w1a"), mk("w1b")]
        self.s_w2 = [mk("w2a"), mk("w2b")]
        self.s_aux = mk("aux")
        self.s_sd = mk("sd")
        self.s_dbg = mk("dbg")
        self.s_eb = [mk("eb0"), mk("eb1")]
        self.s_mm = mk("mm")
        self.s_sc = mk("sc")
        self.s_ve = mk("ve")
        self.s_st = [mk("st0"), mk("st1")]
        self.s_bar = mk("bar")
        self.w1_idx = 0
        self.w1_ev = []
        self.w2_idx = 0
        self.w2_ev = []

    def _decl_dram(self):
        nc = self.nc
        dt = lambda n, s: nc.dram_tensor(n, s, F32, kind="ExternalInput").ap()
        self.xs = dt("xs", [L, D])
        self.wq_pk = dt("wq_pk", [KC, 128, D])
        self.wo_pk = dt("wo_pk", [KC, 128, D])
        self.wk_nat = dt("wk_nat", [D, D])
        self.wv_nat = dt("wv_nat", [D, D])
        self.w1_pk = dt("w1_pk", [FFC, 128, D])
        self.w2_pk = dt("w2_pk", [NGRP, KC, 128, GH * 128])
        self.ew1_pk = dt("ew1_pk", [NEXP, FFC, 128, D])
        self.ew2_pk = dt("ew2_pk", [NEXP, NGRP, KC, 128, GH * 128])
        self.bq_pp = dt("bq_pp", [128, KC])
        self.bo_pp = dt("bo_pp", [128, KC])
        self.g1_pp = dt("g1_pp", [128, KC])
        self.be1_pp = dt("be1_pp", [128, KC])
        self.g2_pp = dt("g2_pp", [128, KC])
        self.be2_pp = dt("be2_pp", [128, KC])
        self.b2a_pp = dt("b2a_pp", [128, KC])
        self.b1_pp = dt("b1_pp", [128, FFC])
        self.eb1_pp = dt("eb1_pp", [128, NEXP * FFC])
        self.bk_row = dt("bk_row", [1, D])
        self.bv_row = dt("bv_row", [1, D])
        self.eb2_rows = dt("eb2_rows", [NEXP, D])
        self.ident_c = dt("ident_c", [128, 128])
        self.onescol_c = dt("onescol_c", [128, 1])
        self.onesrow_c = dt("onesrow_c", [1, 128])
        self.e2_c = dt("e2_c", [16, NPAIR * 128])
        self.mkc_c = dt("mkc_c", [NEXP, 2])
        self.nwc_c = dt("nwc_c", [128, 2])
        self.ecol_c = dt("ecol_c", [NEXP, 1])
        self.out = nc.dram_tensor("out", [TOWN, D], F32, kind="ExternalOutput").ap()
        self.dbg = {}
        for name, shape in (("xnbf", [128, KC * TOWN]), ("msb", [128, NPAIR * 128]),
                            ("ksbd", [128, NPAIR * 16]), ("wrow", [NEXP, TOWN]),
                            ("accm", [128, KC * TOWN]), ("x1n", [128, KC * TOWN]),
                            ("accf", [128, KC * TOWN]), ("hall", [128, NGOWN]),
                            ("ktm", [128, 2 * D]), ("mstat", [128, NGOWN]),
                            ("rstat", [128, NGOWN]), ("x1_pre", [128, KC * TOWN]),
                            ("qsb", [128, KC * TOWN]), ("cbf", [128, KC * TOWN])):
            if name in self.debug:
                self.dbg[name] = nc.dram_tensor(
                    "dbg_" + name, shape, F32, kind="ExternalOutput").ap()

    def _sb(self, name, shape, dtype, off, region_end=None):
        per = int(np.prod(shape[1:])) * mybir.dt.size(dtype)
        off = self.base + off
        assert off % 32 == 0, (name, off)
        assert off + per <= SBUF_TOP, (name, off, per, off + per)
        if region_end is not None:
            assert off + per <= self.base + region_end, (name, off + per, region_end)
        return self.nc.alloc_sbuf_tensor_at(name, list(shape), dtype, offset=off)

    def _decl_sbuf(self):
        sb = self._sb
        self.wk_sb = sb("wk_sb", [128, KC * D], BF16, 0, 65536)
        self.wv_sb = sb("wv_sb", [128, KC * D], BF16, 16384, 65536)
        self.kvbias = sb("kvbias", [1, 2 * D], BF16, 32768, 65536)
        self.xnp = sb("xnp", [128, 2 * KC * 128], BF16, 36864, 65536)
        self.stat48 = sb("stat48", [48, 128], F32, 40960, 65536)
        self.kvacc = sb("kvacc", [128, NPAIR * 128], F32, 45056, 65536)
        self.ksacc = sb("ksacc", [128, NPAIR], F32, 49600, 65536)
        self.sq2 = sb("sq2", [128, 2 * D], F32, 49664, 65536)
        self.h_m = [sb("h_lo", [128, GH * TOWN], BF16, 0, 65536),
                    sb("h_hi", [128, GH * TOWN], BF16, 32768, 65536)]
        self.x1_sb = sb("x1_sb", [128, KC * TOWN], F32, 0, 65536)
        self.acc = sb("acc", [128, KC * TOWN], F32, 65536, 131072)
        XB = 131072
        self.xnbf = sb("xnbf", [128, KC * TOWN], BF16, XB, 163840)
        self.xnfm = sb("xnfm", [128, KC * 1024], F32, XB, 163840)
        self.c_bf = sb("c_bf", [128, KC * TOWN], BF16, XB, 163840)
        TR = 163840
        self.xtm_a = sb("xtm_a", [128, 4 * D], F32, TR, 196608)
        self.ktm = sb("ktm", [128, 2 * D], BF16, TR + 16384, 196608)
        self.vtm = sb("vtm", [128, 2 * D], BF16, TR + 20480, 196608)
        self.elu_a = sb("elu_a", [128, 2 * D], F32, TR + 24576, 196608)
        self.hrow = sb("hrow", [1, TOWN], F32, TR, 196608)
        self.h8 = sb("h8", [NEXP, TOWN], F32, TR + 8192, 196608)
        self.wscr = sb("wscr", [NEXP, TOWN], F32, TR + 16384, 196608)
        self.wrow_f = sb("wrow_f", [NEXP, TOWN], F32, TR + 24576, 196608)
        self.wb = sb("wb", [128, 2 * TOWN], BF16, TR, 196608)
        self.wrow_bf = sb("wrow_bf", [NEXP, TOWN], BF16, TR + 8192, 196608)
        self.wtmp = sb("wtmp", [1, 2 * TOWN], BF16, TR + 12288, 196608)
        self.eb2cur = sb("eb2cur", [1, 2 * D], BF16, TR + 20480, 196608)
        self.q_sb = sb("q_sb", [128, KC * TOWN], BF16, TR, 196608)
        self.ln_m = sb("ln_m", [1, TOWN], F32, TR, 196608)
        self.ln_r = sb("ln_r", [1, TOWN], F32, TR + 8192, 196608)
        self.ln_bf = sb("ln_bf", [128, 4 * 512], BF16, TR + 16384, 196608)
        self.ln_t = sb("ln_t", [128, 2 * 512], F32, TR + 20480, 196608)
        self.ln_s1 = sb("ln_s1", [1, 512], F32, TR + 24576, 196608)
        self.ln_s2 = sb("ln_s2", [1, 512], F32, TR + 26624, 196608)
        self.ln_s3 = sb("ln_s3", [1, 512], F32, TR + 28672, 196608)
        self.h_b2 = sb("h_b2", [128, GH * TOWN], BF16, TR, 196608)
        self.xtm_b4 = sb("xtm_b4", [128, 2 * D], F32, TR, 196608)
        self.otm_b4 = sb("otm_b4", [128, 2 * D], F32, TR + 8192, 196608)
        ST = 196608
        self.w1s = sb("w1s", [128, 2 * KC * 128], BF16, ST, 204800)
        self.w2s = sb("w2s", [128, 2 * GH * 128], BF16, ST + 4096, 204800)
        self.zbfb = sb("zbfb", [16, 512], BF16, ST + 4096, 204800)
        self.zb = sb("zb", [128, 2 * 512], BF16, ST + 5120, 204800)
        self.xt_b1 = sb("xt_b1", [128, 512], F32, ST + 6144, 204800)
        SM = 204800
        self.ident = sb("ident", [128, 128], F32, SM)
        self.onescol_bf = sb("onescol_bf", [128, 1], BF16, SM + 512)
        self.onesrow_bf = sb("onesrow_bf", [1, 128], BF16, SM + 544)
        self.onesrow_f = sb("onesrow_f", [1, 128], F32, SM + 800)
        self.bq_sb = sb("bq_sb", [128, KC], F32, SM + 1312)
        self.bo_sb = sb("bo_sb", [128, KC], F32, SM + 1344)
        self.g1_sb = sb("g1_sb", [128, KC], F32, SM + 1376)
        self.be1_sb = sb("be1_sb", [128, KC], F32, SM + 1408)
        self.g2_sb = sb("g2_sb", [128, KC], F32, SM + 1440)
        self.be2_sb = sb("be2_sb", [128, KC], F32, SM + 1472)
        self.b2a_sb = sb("b2a_sb", [128, KC], F32, SM + 1504)
        self.b1_sb = sb("b1_sb", [128, FFC], BF16, SM + 1536)
        self.eb1_sb = sb("eb1_sb", [128, NEXP * FFC], BF16, SM + 1600)
        self.mstat = sb("mstat", [128, NGOWN], F32, SM + 2112)
        self.rstat = sb("rstat", [128, NGOWN], F32, SM + 2176)
        self.ksbd = sb("ksbd", [128, NPAIR * 16], BF16, SM + 2240)
        self.m_sb = sb("m_sb", [128, NPAIR * 128], BF16, SM + 2496)
        self.e2_sb = sb("e2_sb", [16, NPAIR * 128], BF16, SM + 4544)
        self.dum = sb("dum", [128, 8], F32, SM + 6624)
        SZ = SM + 6656
        self.elu_r = sb("elu_r", [128, 512], BF16, SZ)
        self.tpack = sb("tpack", [128, 128], F32, SZ)
        self.xn01 = sb("xn01", [128, 2 * NGOWN], F32, SZ + 512)
        self.hall = sb("hall", [128, NGOWN], F32, SZ + 704)
        self.scol = sb("scol", [128, 8], F32, SZ + 768)
        self.ecol_sb = sb("ecol_sb", [NEXP, 1], F32, SZ + 800)
        self.mkc_sb = sb("mkc_sb", [NEXP, 2], F32, SZ + 832)
        self.nwc_sb = sb("nwc_sb", [128, 2], F32, SM + 6592)
        assert SZ + 1024 <= SBUF_TOP - self.base, (SZ + 1024, SBUF_TOP - self.base)

    def em(self, e, fn):
        self.L[e].append(fn)
        self.meta[e].append(("op", None, 0))

    def wait(self, e, sem, val):
        if val > 0:
            self.L[e].append(lambda x, s=sem.h, v=val: x.wait_ge(s, v))
            self.meta[e].append(("wait", sem.name, val))

    def inc(self, e, sem, n=1):
        prev = self.L[e].pop()
        sem.n += n
        def fn(x, prev=prev, s=sem.h, n=n):
            prev(x).then_inc(s, n)
        self.L[e].append(fn)
        self.meta[e][-1] = ("op", sem.name, n)
        return sem.n

    def static_check(self):
        pos = {e: 0 for e in "TVSGY"}
        sems = {}
        while True:
            progress = False
            done = True
            for e in "TVSGY":
                while pos[e] < len(self.meta[e]):
                    kind, sem, val = self.meta[e][pos[e]]
                    if kind == "wait" and sems.get(sem, 0) < val:
                        break
                    if kind == "op" and sem is not None:
                        sems[sem] = sems.get(sem, 0) + val
                    pos[e] += 1
                    progress = True
                if pos[e] < len(self.meta[e]):
                    done = False
            if done:
                return
            if not progress:
                msg = []
                for e in "TVSGY":
                    if pos[e] < len(self.meta[e]):
                        k, sem, val = self.meta[e][pos[e]]
                        msg.append(f"{e}@{pos[e]}: wait {sem}>={val} (cur {sems.get(sem, 0)})")
                raise RuntimeError("STATIC DEADLOCK: " + "; ".join(msg))

    def chain(self, e, sem):
        v = self.inc(e, sem, 1)
        self.wait(e, sem, v)
        return v

    def vbar(self):
        return self.chain("V", self.s_ve)

    def dma(self, e, out, in_, sem):
        self.em(e, lambda x, o=out, i=in_: x.dma_start(out=o, in_=i))
        return self.inc(e, sem, 16)

    def barrier(self):
        self.em("T", lambda x: x.nop())
        self.inc("T", self.s_bar, 1)
        self.em("V", lambda x: x.memset(self.dum[:, 0:2], 0.0))
        self.inc("V", self.s_bar, 1)
        self.em("S", lambda x: x.memzero(self.dum[:, 2:4]))
        self.inc("S", self.s_bar, 1)
        self.em("G", lambda x: x.memset(self.dum[:, 4:6], 0.0))
        self.inc("G", self.s_bar, 1)
        tgt = self.s_bar.n
        for e in "TVSGY":
            self.wait(e, self.s_bar, tgt)

    def w1_load(self, src):
        i = self.w1_idx
        self.w1_idx += 1
        slot = i % 2
        if i >= 2:
            sem, val = self.w1_ev[i - 2]
            self.wait("G", sem, val)
        if self.gser:
            self.wait("G", self.s_w1[slot], self.s_w1[slot].n)
        v = self.dma("G", self.w1s[:, slot * 1024:(slot + 1) * 1024], src, self.s_w1[slot])
        return slot, v

    def w1t(self, slot, k):
        return self.w1s[:, slot * 1024 + k * 128: slot * 1024 + (k + 1) * 128]

    def w2_load(self, src):
        i = self.w2_idx
        self.w2_idx += 1
        slot = i % 2
        if i >= 2:
            sem, val = self.w2_ev[i - 2]
            self.wait("G", sem, val)
        v = self.dma("G", self.w2s[:, slot * 1024:(slot + 1) * 1024], src, self.s_w2[slot])
        return slot, v

    def w2t(self, slot, j):
        return self.w2s[:, slot * 1024 + j * 128: slot * 1024 + (j + 1) * 128]

    def _dump(self, name, src, cast=False):
        if name not in self.dbg:
            return
        if cast:
            sem = self.s_dbg
            self.dma("G", self.dbg[name], src[:], sem)
        else:
            sem = self.s_sd
            self.wait("Y", sem, sem.n)
            self.dma("Y", self.dbg[name], src[:], sem)
        for e in "TVSG":
            self.wait(e, sem, sem.n)

    def emit_aux(self):
        d = self.dma
        for k in range(KC):
            d("G", self.wk_sb[:, k * D:(k + 1) * D],
              self.wk_nat[k * 128:(k + 1) * 128, :], self.s_aux)
            d("G", self.wv_sb[:, k * D:(k + 1) * D],
              self.wv_nat[k * 128:(k + 1) * 128, :], self.s_aux)
        d("G", self.kvbias[0:1, 0:D], self.bk_row, self.s_aux)
        d("G", self.kvbias[0:1, D:2 * D], self.bv_row, self.s_aux)
        for dst, src in ((self.ident, self.ident_c), (self.onescol_bf, self.onescol_c),
                         (self.onesrow_bf, self.onesrow_c), (self.onesrow_f, self.onesrow_c),
                         (self.bq_sb, self.bq_pp), (self.bo_sb, self.bo_pp),
                         (self.g1_sb, self.g1_pp), (self.be1_sb, self.be1_pp),
                         (self.g2_sb, self.g2_pp), (self.be2_sb, self.be2_pp),
                         (self.b2a_sb, self.b2a_pp), (self.b1_sb, self.b1_pp),
                         (self.eb1_sb, self.eb1_pp), (self.ecol_sb, self.ecol_c),
                         (self.e2_sb, self.e2_c), (self.mkc_sb, self.mkc_c),
                         (self.nwc_sb, self.nwc_c)):
            d("G", dst[:], src, self.s_aux)
        self.aux_tgt = self.s_aux.n
        for e in "TVS":
            self.wait(e, self.s_aux, self.aux_tgt)
        self.em("V", lambda v: v.memset(self.m_sb[:], 0.0))
        self.em("V", lambda v: v.memset(self.ksbd[:], 0.0))
        self.inc("V", self.s_ve, 1)

    def emit_phase_a(self):
        ps = self.ps
        NGL = self.ngl
        assert NGL % 4 == 0
        evT = {}; evKV = {}; evKVmm = {}; evA = {}; evXN = {}
        evK_s = {}; evK_v = {}; evV_v = {}; evDMA = {}; evSQ = {}
        evKmm = {}; evVmm = {}
        evXNw = [0]

        def lhs(c, g):
            if g < NGOWN:
                return self.xnbf[:, c * TOWN + g * 128:c * TOWN + g * 128 + 128]
            return self.xnp[:, ((g % 2) * KC + c) * 128:((g % 2) * KC + c + 1) * 128]

        def stats_sums(g):
            own = g < NGOWN
            bpar = (g // 4) % 2
            j = g % 4
            col = bpar * 4 + j
            sl = g % 4
            xt = self.xtm_a[:, sl * D:(sl + 1) * D]
            if g >= 4:
                self.wait("Y", self.s_mm, evT[g - 4])
            evDMA[g] = self.dma("Y", xt, self.xs[g * 128:(g + 1) * 128, :], self.s_lx[g % 4])
            mc = self.scol[:, col:col + 1]
            ac = self.tpack[:, col:col + 1]
            sq = self.sq2[:, (g % 2) * D:(g % 2 + 1) * D]
            self.wait("V", self.s_lx[g % 4], evDMA[g])
            if g >= 2:
                self.wait("V", self.s_ve, evSQ.get(g - 2, 0))
            self.em("V", lambda v: v.reduce_sum(mc, xt, axis=AXX))
            self.vbar()
            self.em("V", lambda v: v.tensor_scalar_mul(mc, mc, -INV_D))
            v_negm = self.inc("V", self.s_ve, 1)
            self.wait("S", self.s_ve, v_negm)
            if g >= 2:
                self.wait("S", self.s_ve, evSQ.get(g - 2, 0))
            self.em("S", lambda s: s.activation(sq, xt, AF.Square, bias=mc))
            self.chain("S", self.s_sc)
            self.em("S", lambda s: s.add(xt, xt, mc))
            s_sub = self.inc("S", self.s_sc, 1)
            self.wait("V", self.s_sc, s_sub)
            self.em("V", lambda v: v.reduce_sum(ac, sq, axis=AXX))
            evSQ[g] = self.inc("V", self.s_ve, 1)
            self.wait("V", self.s_ve, evSQ[g])
            if own:
                mm_dst = self.mstat[:, g:g + 1]
                self.em("V", lambda v: v.tensor_copy(mm_dst, mc))
                self.vbar()

        def stats_finish(gs):
            bpar = (gs[0] // 4) % 2
            nb = len(gs)
            a = self.tpack[:, bpar * 4:bpar * 4 + nb]
            r0 = self.tpack[:, 8 + bpar * 4:8 + bpar * 4 + nb]
            rcb = self.scol[:, bpar * 4:bpar * 4 + nb]
            self.em("V", lambda v: v.tensor_scalar(a, a, INV_D, EPS_LN, op0=OP.mult, op1=OP.add))
            v_ac = self.inc("V", self.s_ve, 1)
            self.wait("S", self.s_ve, v_ac)
            self.em("S", lambda s: s.activation(r0, a, AF.Ln))
            self.chain("S", self.s_sc)
            self.em("S", lambda s: s.activation(r0, r0, AF.Exp, scale=-0.5))
            s_r0 = self.inc("S", self.s_sc, 1)
            self.wait("V", self.s_sc, s_r0)
            self.em("V", lambda v: v.tensor_mul(rcb, r0, r0))
            self.vbar()
            self.em("V", lambda v: v.tensor_mul(rcb, rcb, a))
            self.vbar()
            self.em("V", lambda v: v.tensor_scalar(rcb, rcb, -0.5, 1.5, op0=OP.mult, op1=OP.add))
            self.vbar()
            self.em("V", lambda v: v.tensor_mul(rcb, rcb, r0))
            v_rcb = self.inc("V", self.s_ve, 1)
            self.wait("S", self.s_ve, v_rcb)
            for j, g in enumerate(gs):
                own = g < NGOWN
                sl = g % 4
                xt = self.xtm_a[:, sl * D:(sl + 1) * D]
                rc = rcb[:, j:j + 1]
                self.em("S", lambda s, xt=xt, rc=rc: s.mul(xt, xt, rc))
                self.chain("S", self.s_sc)
                if own:
                    self.em("S", lambda s, rr=self.rstat[:, g:g + 1], rc=rc: s.copy(rr, rc))
                    self.chain("S", self.s_sc)
                    self.em("S", lambda s, xt=xt, g=g: s.copy(self.xn01[:, g:g + 1], xt[:, 0:1]))
                    self.chain("S", self.s_sc)
                    self.em("S", lambda s, xt=xt, g=g: s.copy(
                        self.xn01[:, NGOWN + g:NGOWN + g + 1], xt[:, 1:2]))
                    self.chain("S", self.s_sc)
                evA[g] = self.s_sc.n

        def elu_all(g):
            rt = self.elu_a[:, (g % 2) * D:(g % 2 + 1) * D]
            kslot = self.ktm[:, (g % 2) * D:(g % 2 + 1) * D]
            vslot = self.vtm[:, (g % 2) * D:(g % 2 + 1) * D]
            self.wait("V", self.s_mm, evKmm[g])
            if g >= 2:
                self.wait("V", self.s_mm, evKVmm.get(g - 2, 0))
            for hh in range(2):
                self.em("V", lambda v, hh=hh: v.tensor_scalar(
                    rt[:, hh * 512:(hh + 1) * 512], ps[hh][:], 0.0, None, op0=OP.max))
            evK_v[g] = self.inc("V", self.s_ve, 1)
            self.wait("S", self.s_mm, evKmm[g])
            self.wait("S", self.s_ve, evK_v[g])
            if g >= 2:
                self.wait("S", self.s_mm, evKVmm.get(g - 2, 0))
            for hh in range(2):
                self.em("S", lambda s, hh=hh: s.activation(
                    kslot[:, hh * 512:(hh + 1) * 512], ps[hh][:], AF.Exp))
            evK_s[g] = self.inc("S", self.s_sc, 1)
            self.wait("V", self.s_sc, evK_s[g])
            for hh in range(2):
                self.em("V", lambda v, hh=hh: v.scalar_tensor_tensor(
                    kslot[:, hh * 512:(hh + 1) * 512], kslot[:, hh * 512:(hh + 1) * 512],
                    1.0, rt[:, hh * 512:(hh + 1) * 512], op0=OP.min, op1=OP.add))
            self.vbar()
            self.wait("V", self.s_mm, evVmm[g])
            for hh in range(2):
                self.em("V", lambda v, hh=hh: v.tensor_copy(
                    vslot[:, hh * 512:(hh + 1) * 512], ps[2 + hh][:]))
            evV_v[g] = self.inc("V", self.s_ve, 1)

        def emit_trans(g):
            sl = g % 4
            xt = self.xtm_a[:, sl * D:(sl + 1) * D]
            self.wait("T", self.s_sc, evA[g])
            for wave in range(2):
                self.wait("T", self.s_sc, evXNw[0])
                for j in range(4):
                    c = wave * 4 + j
                    self.em("T", lambda t, c=c, j=j: t.transpose(
                        ps[4][:, j * 128:(j + 1) * 128], xt[:, c * 128:(c + 1) * 128],
                        self.ident[:]))
                mmv = self.inc("T", self.s_mm, 1)
                if wave == 1:
                    evT[g] = mmv
                self.wait("S", self.s_mm, mmv)
                if (g >= NGOWN) and g >= 2:
                    self.wait("S", self.s_mm, evKVmm.get(g - 2, 0))
                for j in range(4):
                    c = wave * 4 + j
                    self.em("S", lambda s, dst=lhs(c, g), j=j: s.copy(dst, ps[4][:, j * 128:(j + 1) * 128]))
                evXNw[0] = self.inc("S", self.s_sc, 1)
            evXN[g] = evXNw[0]

        def emit_proj(g):
            self.wait("T", self.s_sc, evXN[g])
            if g >= 1:
                self.wait("T", self.s_sc, evK_s[g - 1])
                self.wait("T", self.s_ve, evK_v[g - 1])
            for k in range(KC):
                for hh in range(2):
                    self.em("T", lambda t, k=k, hh=hh, lh=lhs(k, g): t.matmul(
                        ps[hh][:], lh, self.wk_sb[:, k * D + hh * 512:k * D + (hh + 1) * 512],
                        start=(k == 0), stop=False))
            for hh in range(2):
                self.em("T", lambda t, hh=hh: t.matmul(
                    ps[hh][:], self.onesrow_bf[:], self.kvbias[0:1, hh * 512:(hh + 1) * 512],
                    start=False, stop=True))
            evKmm[g] = self.inc("T", self.s_mm, 1)
            if g >= 1:
                self.wait("T", self.s_ve, evV_v[g - 1])
            for k in range(KC):
                for hh in range(2):
                    self.em("T", lambda t, k=k, hh=hh, lh=lhs(k, g): t.matmul(
                        ps[2 + hh][:], lh, self.wv_sb[:, k * D + hh * 512:k * D + (hh + 1) * 512],
                        start=(k == 0), stop=False))
            for hh in range(2):
                self.em("T", lambda t, hh=hh: t.matmul(
                    ps[2 + hh][:], self.onesrow_bf[:],
                    self.kvbias[0:1, D + hh * 512:D + (hh + 1) * 512],
                    start=False, stop=True))
            evVmm[g] = self.inc("T", self.s_mm, 1)

        def emit_kv(g):
            kslot = self.ktm[:, (g % 2) * D:(g % 2 + 1) * D]
            vslot = self.vtm[:, (g % 2) * D:(g % 2 + 1) * D]
            self.wait("T", self.s_ve, evV_v[g])
            if g >= 1:
                self.wait("T", self.s_ve, evKV[g - 1])
            for pr in range(NPAIR):
                b = 5 + pr // 4
                o = pr % 4
                self.em("T", lambda t, pr=pr, b=b, o=o: t.matmul(
                    ps[b][:, o * 128:(o + 1) * 128], kslot[:, pr * 128:(pr + 1) * 128],
                    vslot[:, pr * 128:(pr + 1) * 128], start=True, stop=True))
                self.em("T", lambda t, pr=pr: t.matmul(
                    ps[7][:, pr:pr + 1], kslot[:, pr * 128:(pr + 1) * 128],
                    self.onescol_bf[:], start=True, stop=True))
            evKVmm[g] = self.inc("T", self.s_mm, 1)
            self.wait("V", self.s_mm, evKVmm[g])
            if g >= 1:
                self.wait("V", self.s_ve, evKV[g - 1])
            if g == 0:
                self.em("V", lambda v: v.tensor_copy(self.kvacc[:, 0:512], ps[5][:]))
                self.em("V", lambda v: v.tensor_copy(self.kvacc[:, 512:1024], ps[6][:]))
                self.em("V", lambda v: v.tensor_copy(self.ksacc[:], ps[7][:, 0:NPAIR]))
            else:
                self.em("V", lambda v: v.tensor_add(self.kvacc[:, 0:512], self.kvacc[:, 0:512], ps[5][:]))
                self.em("V", lambda v: v.tensor_add(self.kvacc[:, 512:1024], self.kvacc[:, 512:1024], ps[6][:]))
                self.em("V", lambda v: v.tensor_add(self.ksacc[:], self.ksacc[:], ps[7][:, 0:NPAIR]))
            evKV[g] = self.inc("V", self.s_ve, 1)

        nbat = NGL // 4
        for g in range(4):
            stats_sums(g)
        stats_finish([0, 1, 2, 3])
        for b in range(nbat):
            gs = [4 * b + j for j in range(4)]
            nxt = [4 * (b + 1) + j for j in range(4)] if b + 1 < nbat else []
            for j, g in enumerate(gs):
                emit_trans(g)
                emit_proj(g)
                if nxt:
                    stats_sums(nxt[j])
                elu_all(g)
                emit_kv(g)
            if nxt:
                stats_finish(nxt)
        self.evKV_last = evKV[NGL - 1]

    def emit_phase_a_tail(self):
        ps = self.ps
        self.wait("V", self.s_ve, self.evKV_last)
        for pr in range(NPAIR):
            self.em("V", lambda v, pr=pr: v.tensor_copy(
                self.m_sb[0:64, pr * 128:pr * 128 + 64],
                self.kvacc[0:64, pr * 128:pr * 128 + 64]))
            self.em("V", lambda v, pr=pr: v.tensor_copy(
                self.m_sb[64:128, pr * 128 + 64:pr * 128 + 128],
                self.kvacc[64:128, pr * 128 + 64:pr * 128 + 128]))
            self.em("V", lambda v, pr=pr: v.tensor_copy(
                self.ksbd[0:64, pr * 16 + 2 * pr:pr * 16 + 2 * pr + 1],
                self.ksacc[0:64, pr:pr + 1]))
            self.em("V", lambda v, pr=pr: v.tensor_copy(
                self.ksbd[64:128, pr * 16 + 2 * pr + 1:pr * 16 + 2 * pr + 2],
                self.ksacc[64:128, pr:pr + 1]))
        u = self.hall[:]
        t1 = self.xn01[:, 0:NGOWN]
        t2 = self.xn01[:, NGOWN:2 * NGOWN]
        self.em("V", lambda v: v.tensor_add(u, t1, t2))
        self.vbar()
        for bnd in (64.0, 32.0, 16.0, 8.0):
            self.em("V", lambda v, b=bnd: v.tensor_single_scalar(t1, u, b, OP.is_ge))
            self.vbar()
            self.em("V", lambda v, b=bnd: v.scalar_tensor_tensor(u, t1, -b, u, op0=OP.mult, op1=OP.add))
            self.vbar()
            self.em("V", lambda v, b=bnd: v.tensor_single_scalar(t1, u, -b, OP.is_le))
            self.vbar()
            self.em("V", lambda v, b=bnd: v.scalar_tensor_tensor(u, t1, b, u, op0=OP.mult, op1=OP.add))
            self.vbar()
        self.em("V", lambda v: v.tensor_single_scalar(t1, u, 1.0, OP.is_ge))
        self.vbar()
        for kk in range(2, 8):
            self.em("V", lambda v, k=float(kk): v.tensor_single_scalar(t2, u, k, OP.is_ge))
            self.vbar()
            self.em("V", lambda v: v.tensor_add(t1, t1, t2))
            self.vbar()
        for kk in range(1, 8):
            self.em("V", lambda v, k=float(-kk): v.tensor_single_scalar(t2, u, k, OP.is_le))
            self.vbar()
            self.em("V", lambda v: v.tensor_sub(t1, t1, t2))
            self.vbar()
        self.em("V", lambda v: v.tensor_single_scalar(t2, t1, 0.0, OP.is_lt))
        self.vbar()
        self.em("V", lambda v: v.scalar_tensor_tensor(u, t2, 8.0, t1, op0=OP.mult, op1=OP.add))
        self.vbar()
        self.em("V", lambda v: v.tensor_copy(self.tpack[:, 0:NGOWN], self.mstat[:]))
        self.em("V", lambda v: v.tensor_copy(self.tpack[:, NGOWN:2 * NGOWN], self.rstat[:]))
        self.em("V", lambda v: v.tensor_copy(self.tpack[:, 2 * NGOWN:3 * NGOWN], self.hall[:]))
        vp = self.inc("V", self.s_ve, 1)
        self.wait("T", self.s_ve, vp)
        self.em("T", lambda t: t.transpose(ps[4][:, 0:128], self.tpack[:], self.ident[:]))
        mt = self.inc("T", self.s_mm, 1)
        self.wait("V", self.s_mm, mt)
        self.em("V", lambda v: v.tensor_copy(self.stat48[0:16, :], ps[4][32:48, 0:128]))
        vs = self.inc("V", self.s_ve, 1)
        self.wait("Y", self.s_ve, vs)
        self.dma("Y", self.hrow[0:1, :], self.stat48[0:16, :], self.s_sd)
        for e in range(NEXP):
            self.wait("Y", self.s_sd, self.s_sd.n)
            self.dma("Y", self.h8[e:e + 1, :], self.hrow[0:1, :], self.s_sd)
        sdv = self.s_sd.n
        self.wait("V", self.s_sd, sdv)
        self.em("V", lambda v: v.tensor_scalar(self.h8[:], self.h8[:], self.ecol_sb[:], None,
                                               op0=OP.subtract))
        vd = self.inc("V", self.s_ve, 1)
        self.wait("S", self.s_ve, vd)
        self.em("S", lambda s: s.activation(self.wscr[:], self.h8[:], AF.Abs))
        self.chain("S", self.s_sc)
        self.em("S", lambda s: s.activation(self.wrow_f[:], self.wscr[:], AF.Relu, scale=-1.0, bias=self.mkc_sb[:, 0:1]))
        self.chain("S", self.s_sc)
        self.em("S", lambda s: s.activation(self.wscr[:], self.h8[:], AF.Abs, bias=self.mkc_sb[:, 0:1]))
        self.chain("S", self.s_sc)
        self.em("S", lambda s: s.activation(self.wscr[:], self.wscr[:], AF.Relu, scale=-1.0, bias=self.mkc_sb[:, 0:1]))
        s1 = self.inc("S", self.s_sc, 1)
        self.wait("V", self.s_sc, s1)
        self.em("V", lambda v: v.tensor_add(self.wrow_f[:], self.wrow_f[:], self.wscr[:]))
        v1 = self.inc("V", self.s_ve, 1)
        self.wait("S", self.s_ve, v1)
        self.em("S", lambda s: s.activation(self.wscr[:], self.h8[:], AF.Abs, bias=self.mkc_sb[:, 1:2]))
        self.chain("S", self.s_sc)
        self.em("S", lambda s: s.activation(self.wscr[:], self.wscr[:], AF.Relu, scale=-1.0, bias=self.mkc_sb[:, 0:1]))
        s2 = self.inc("S", self.s_sc, 1)
        self.wait("V", self.s_sc, s2)
        self.em("V", lambda v: v.tensor_add(self.wrow_f[:], self.wrow_f[:], self.wscr[:]))
        self.vbar()
        self.em("V", lambda v: v.tensor_scalar_mul(self.wrow_f[:], self.wrow_f[:], 0.5))
        self.vbar()
        self.em("V", lambda v: v.tensor_copy(self.wrow_bf[:], self.wrow_f[:]))
        self.wrow_ev = self.inc("V", self.s_ve, 1)

    def emit_moe(self):
        ps = self.ps
        regA_ev = [0, 0]
        hEv_prev = [0]
        wbEv = [0, 0]
        wtmpRead = [0, 0]
        hW_last = {}
        yEv_prev = [0]
        ebv_of = {}

        def emit_wb(e):
            sl = e % 2
            self.wait("Y", self.s_ve, self.wrow_ev)
            if e >= 2:
                self.wait("Y", self.s_mm, wtmpRead[sl])
            self.wait("Y", self.s_sd, self.s_sd.n)
            self.dma("Y", self.wtmp[0:1, sl * TOWN:(sl + 1) * TOWN],
                     self.wrow_bf[e:e + 1, :], self.s_sd)
            sdv = self.s_sd.n
            if e >= 2:
                self.wait("G", self.s_mm, wtmpRead[sl])
            ebv_of[e] = self.dma("G", self.eb2cur[0:1, sl * D:(sl + 1) * D],
                                 self.eb2_rows[e:e + 1, :], self.s_eb[sl])
            self.wait("T", self.s_sd, sdv)
            self.wait("T", self.s_sc, hEv_prev[0])
            for tt in range(4):
                self.em("T", lambda t, tt=tt, sl=sl: t.matmul(
                    ps[tt][:], self.onesrow_bf[:],
                    self.wtmp[0:1, sl * TOWN + tt * 512:sl * TOWN + (tt + 1) * 512],
                    start=True, stop=True))
            mmwb = self.inc("T", self.s_mm, 1)
            self.wait("V", self.s_mm, mmwb)
            for tt in range(4):
                self.em("V", lambda v, tt=tt, sl=sl: v.tensor_copy(
                    self.wb[:, sl * TOWN + tt * 512:sl * TOWN + (tt + 1) * 512], ps[tt][:]))
            wbEv[sl] = self.inc("V", self.s_ve, 1)

        def emit_h_block(e, gg, i):
            n = gg * GH + i
            r = gg % 2
            hreg = self.h_m[r]
            slot, wv = self.w1_load(self.ew1_pk[e % self.emod, n])
            self.wait("T", self.s_w1[slot], wv)
            self.wait("T", self.s_sc, hEv_prev[0])
            if gg == 0 and i == 0:
                self.wait("T", self.s_ve, wbEv[e % 2])
            for k in range(KC):
                for tt in range(4):
                    self.em("T", lambda t, k=k, tt=tt, slot=slot: t.matmul(
                        ps[tt][:], self.w1t(slot, k),
                        self.xnbf[:, k * TOWN + tt * 512:k * TOWN + (tt + 1) * 512],
                        start=(k == 0), stop=(k == KC - 1)))
            mmH = self.inc("T", self.s_mm, 1)
            self.w1_ev.append((self.s_mm, mmH))
            self.wait("S", self.s_mm, mmH)
            if regA_ev[r]:
                self.wait("S", self.s_mm, regA_ev[r])
            for tt in range(4):
                self.em("S", lambda s, tt=tt, i=i, hreg=hreg, e=e, n=n: s.activation(
                    hreg[:, i * TOWN + tt * 512:i * TOWN + (tt + 1) * 512], ps[tt][:],
                    AF.Relu, bias=self.eb1_sb[:, e * FFC + n:e * FFC + n + 1]))
            hEv_prev[0] = self.inc("S", self.s_sc, 1)
            self.wait("V", self.s_sc, hEv_prev[0])
            if self.moe_wb:
                self.wait("V", self.s_ve, wbEv[e % 2])
                for tt in range(4):
                    self.em("V", lambda v, tt=tt, i=i, hreg=hreg, e=e: v.tensor_mul(
                        hreg[:, i * TOWN + tt * 512:i * TOWN + (tt + 1) * 512],
                        hreg[:, i * TOWN + tt * 512:i * TOWN + (tt + 1) * 512],
                        self.wb[:, (e % 2) * TOWN + tt * 512:(e % 2) * TOWN + (tt + 1) * 512]))
            else:
                self.em("V", lambda v: v.memset(self.dum[:, 0:2], 0.0))
            hW_last[(e, gg)] = self.inc("V", self.s_ve, 1)

        def emit_y_sub(e, gg, dout):
            r = gg % 2
            hreg = self.h_m[r]
            slot, wv = self.w2_load(self.ew2_pk[e, gg, dout])
            self.wait("T", self.s_w2[slot], wv)
            self.wait("T", self.s_ve, hW_last[(e, gg)])
            self.wait("T", self.s_ve, yEv_prev[0])
            if gg == NGRP - 1 and dout == 0:
                self.wait("T", self.s_eb[e % 2], ebv_of[e])
            last_plain = (gg != NGRP - 1)
            for j in range(GH):
                for tt in range(4):
                    self.em("T", lambda t, j=j, tt=tt, slot=slot, hreg=hreg, lp=last_plain: t.matmul(
                        ps[4 + tt][:], self.w2t(slot, j),
                        hreg[:, j * TOWN + tt * 512:j * TOWN + (tt + 1) * 512],
                        start=(j == 0), stop=(lp and j == GH - 1)))
            if gg == NGRP - 1:
                for tt in range(4):
                    self.em("T", lambda t, tt=tt, e=e, dout=dout: t.matmul(
                        ps[4 + tt][:],
                        self.eb2cur[0:1, (e % 2) * D + dout * 128:(e % 2) * D + (dout + 1) * 128],
                        self.wtmp[0:1, (e % 2) * TOWN + tt * 512:(e % 2) * TOWN + (tt + 1) * 512],
                        start=False, stop=True))
            mmY = self.inc("T", self.s_mm, 1)
            self.w2_ev.append((self.s_mm, mmY))
            if dout == KC - 1:
                regA_ev[r] = mmY
            if gg == NGRP - 1 and dout == GH - 1:
                wtmpRead[e % 2] = mmY
            self.wait("V", self.s_mm, mmY)
            self.wait("V", self.s_ve, yEv_prev[0])
            for tt in range(4):
                dst = self.acc[:, dout * TOWN + tt * 512:dout * TOWN + (tt + 1) * 512]
                if e == 0 and gg == 0:
                    self.em("V", lambda v, dst=dst, tt=tt: v.tensor_copy(dst, ps[4 + tt][:]))
                else:
                    self.em("V", lambda v, dst=dst, tt=tt: v.tensor_add(dst, dst, ps[4 + tt][:]))
            yEv_prev[0] = self.inc("V", self.s_ve, 1)

        prev = None
        nh = [0]
        for e in range(self.moe_e):
            if self.moe_wb:
                emit_wb(e)
            for gg in range(NGRP):
                for i in range(GH):
                    if nh[0] >= self.hblk:
                        break
                    emit_h_block(e, gg, i)
                    nh[0] += 1
                    if prev is not None and self.moe_y:
                        emit_y_sub(prev[0], prev[1], i)
                if nh[0] >= self.hblk:
                    break
                prev = (e, gg)
            if nh[0] >= self.hblk:
                break
        if self.moe_y:
            for i in range(GH):
                emit_y_sub(prev[0], prev[1], i)

    def emit_b1(self):
        ps = self.ps
        evQpar = [0, 0]
        for s in range(2):
            for n in range(KC):
                slot, wv = self.w1_load(self.wq_pk[n])
                self.wait("T", self.s_w1[slot], wv)
                self.wait("T", self.s_ve, evQpar[n % 2])
                for k in range(KC):
                    for sub in range(2):
                        self.em("T", lambda t, k=k, sub=sub, n=n, s=s, slot=slot: t.matmul(
                            ps[2 * (n % 2) + sub][:], self.w1t(slot, k),
                            self.xnbf[:, k * TOWN + s * 1024 + sub * 512:
                                      k * TOWN + s * 1024 + (sub + 1) * 512],
                            start=(k == 0), stop=(k == KC - 1)))
                mmQ = self.inc("T", self.s_mm, 1)
                self.w1_ev.append((self.s_mm, mmQ))
                for sub in range(2):
                    pstile = ps[2 * (n % 2) + sub]
                    qsl = self.q_sb[:, n * TOWN + s * 1024 + sub * 512:
                                    n * TOWN + s * 1024 + (sub + 1) * 512]
                    self.wait("V", self.s_mm, mmQ)
                    self.wait("V", self.s_ve, self.s_ve.n)
                    self.em("V", lambda v, p=pstile, n=n: v.tensor_scalar(
                        self.elu_r[:], p[:], self.bq_sb[:, n:n + 1], 0.0, op0=OP.add, op1=OP.max))
                    vmax = self.vbar()
                    self.wait("S", self.s_mm, mmQ)
                    self.wait("S", self.s_ve, vmax)
                    self.em("S", lambda x, p=pstile, q=qsl, n=n: x.activation(
                        q, p[:], AF.Exp, bias=self.bq_sb[:, n:n + 1]))
                    sv = self.inc("S", self.s_sc, 1)
                    self.wait("V", self.s_sc, sv)
                    self.em("V", lambda v, q=qsl: v.scalar_tensor_tensor(
                        q, q, 1.0, self.elu_r[:], op0=OP.min, op1=OP.add))
                    evQpar[n % 2] = self.inc("V", self.s_ve, 1)
        evQ_last = max(evQpar)
        self._dump_ev = evQ_last

        evDen = 0
        evZB = 0
        evZBp = [0, 0]
        evAOZ = [0, 0]
        evAOZ_last = 0
        evX1par = [0, 0]
        xf_prev = [0, 0]
        xf_evac = [0, 0]
        for s in range(2):
            if s == 1:
                self.wait("S", self.s_mm, max(xf_prev))
            for sub in range(2):
                qcol = s * 1024 + sub * 512
                self.wait("T", self.s_ve, evQ_last)
                self.wait("T", self.s_ve, evDen)
                for p in range(NPAIR):
                    self.em("T", lambda t, p=p, qcol=qcol: t.matmul(
                        ps[4][0:16, :], self.ksbd[:, p * 16:(p + 1) * 16],
                        self.q_sb[:, p * TOWN + qcol:p * TOWN + qcol + 512],
                        start=(p == 0), stop=(p == NPAIR - 1)))
                mmDen = self.inc("T", self.s_mm, 1)
                self.wait("V", self.s_mm, mmDen)
                self.em("V", lambda v: v.tensor_scalar(ps[4][0:16, :], ps[4][0:16, :],
                                                       EPS_ATTN, None, op0=OP.add))
                self.vbar()
                def _recip(v):
                    with self.nc.allow_low_precision("z in bf16 (matmul operand)"):
                        return v.reciprocal(self.zbfb[:], ps[4][0:16, :])
                self.em("V", _recip)
                evDen = self.inc("V", self.s_ve, 1)
                for p in range(NPAIR):
                    zsl = self.zb[:, (p % 2) * 512:(p % 2 + 1) * 512]
                    self.wait("T", self.s_ve, evDen)
                    self.wait("T", self.s_sc, evZB)
                    self.em("T", lambda t, p=p: t.matmul(
                        ps[5][:], self.e2_sb[:, p * 128:(p + 1) * 128], self.zbfb[:],
                        start=True, stop=True))
                    mmZB = self.inc("T", self.s_mm, 1)
                    self.wait("S", self.s_mm, mmZB)
                    self.wait("S", self.s_ve, evAOZ[p % 2] if p > 1 else evAOZ_last)
                    self.em("S", lambda x, zsl=zsl: x.copy(zsl, ps[5][:]))
                    evZB = self.inc("S", self.s_sc, 1)
                    evZBp[p % 2] = evZB
                    self.wait("T", self.s_ve, evAOZ[p % 2])
                    self.em("T", lambda t, p=p, qcol=qcol: t.matmul(
                        ps[6 + p % 2][:], self.m_sb[:, p * 128:(p + 1) * 128],
                        self.q_sb[:, p * TOWN + qcol:p * TOWN + qcol + 512],
                        start=True, stop=True))
                    mmQM = self.inc("T", self.s_mm, 1)
                    self.wait("V", self.s_mm, mmQM)
                    self.wait("V", self.s_sc, evZBp[p % 2])
                    self.em("V", lambda v, p=p, qcol=qcol, zsl=zsl: v.tensor_mul(
                        self.q_sb[:, p * TOWN + qcol:p * TOWN + qcol + 512],
                        ps[6 + p % 2][:], zsl))
                    evAOZ[p % 2] = self.inc("V", self.s_ve, 1)
                evAOZ_last = evAOZ[(NPAIR - 1) % 2]
            aoz_done = max(evAOZ)

            self.wait("T", self.s_sc, evZB)
            self.wait("Y", self.s_ve, aoz_done)
            if s == 1:
                self.wait("S", self.s_ve, max(evX1par))
            for j in range(8):
                gown = s * 8 + j
                for fh in range(2):
                    sl = fh
                    xt = self.xt_b1[:]
                    if xf_prev[0] or xf_prev[1]:
                        self.wait("Y", self.s_mm, max(xf_prev))
                    lxv = self.dma("Y", xt,
                                   self.xs[gown * 128:(gown + 1) * 128, fh * 512:(fh + 1) * 512],
                                   self.s_lx[sl])
                    self.wait("V", self.s_lx[sl], lxv)
                    self.em("V", lambda v, xt=xt, gown=gown: v.tensor_scalar(
                        xt, xt, self.mstat[:, gown:gown + 1], self.rstat[:, gown:gown + 1],
                        op0=OP.add, op1=OP.mult))
                    vn = self.inc("V", self.s_ve, 1)
                    self.wait("T", self.s_ve, vn)
                    self.wait("T", self.s_sc, xf_evac[sl])
                    pb = 4 + fh
                    for cc in range(4):
                        self.em("T", lambda t, xt=xt, cc=cc, pb=pb: t.transpose(
                            ps[pb][:, cc * 128:(cc + 1) * 128], xt[:, cc * 128:(cc + 1) * 128],
                            self.ident[:]))
                    xf_prev[sl] = self.inc("T", self.s_mm, 1)
                    self.wait("S", self.s_mm, xf_prev[sl])
                    for cc in range(4):
                        c = fh * 4 + cc
                        self.em("S", lambda x, c=c, j=j, cc=cc, pb=pb: x.copy(
                            self.xnfm[:, c * 1024 + j * 128:c * 1024 + (j + 1) * 128],
                            ps[pb][:, cc * 128:(cc + 1) * 128]))
                    xf_evac[sl] = self.inc("S", self.s_sc, 1)
            xn_ready = self.s_sc.n

            for n in range(KC):
                slot, wv = self.w1_load(self.wo_pk[n])
                self.wait("T", self.s_w1[slot], wv)
                self.wait("T", self.s_ve, evX1par[n % 2])
                if n == 0:
                    self.wait("T", self.s_ve, aoz_done)
                for k in range(KC):
                    for sub in range(2):
                        self.em("T", lambda t, k=k, sub=sub, n=n, s=s, slot=slot: t.matmul(
                            ps[2 * (n % 2) + sub][:], self.w1t(slot, k),
                            self.q_sb[:, k * TOWN + s * 1024 + sub * 512:
                                      k * TOWN + s * 1024 + (sub + 1) * 512],
                            start=(k == 0), stop=(k == KC - 1)))
                mmWo = self.inc("T", self.s_mm, 1)
                self.w1_ev.append((self.s_mm, mmWo))
                self.wait("V", self.s_mm, mmWo)
                self.wait("V", self.s_sc, xn_ready)
                for sub in range(2):
                    self.em("V", lambda v, n=n, s=s, sub=sub: v.scalar_tensor_tensor(
                        self.x1_sb[:, n * TOWN + s * 1024 + sub * 512:
                                   n * TOWN + s * 1024 + (sub + 1) * 512],
                        ps[2 * (n % 2) + sub][:], self.bo_sb[:, n:n + 1],
                        self.xnfm[:, n * 1024 + sub * 512:n * 1024 + (sub + 1) * 512],
                        op0=OP.add, op1=OP.add))
                evX1par[n % 2] = self.inc("V", self.s_ve, 1)

    def emit_ln(self, which):
        ps = self.ps
        g_sb = self.g1_sb if which == 1 else self.g2_sb
        be_sb = self.be1_sb if which == 1 else self.be2_sb
        evMMpar = [0, 0]
        evRowEvac = 0
        evApply = {}
        evBF = [0, 0]
        for tt in range(4):
            for c in range(KC):
                sl = c % 2
                xsl = self.x1_sb[:, c * TOWN + tt * 512:c * TOWN + (tt + 1) * 512]
                b1 = self.ln_bf[:, sl * 512:(sl + 1) * 512]
                b2 = self.ln_bf[:, (2 + sl) * 512:(3 + sl) * 512]
                self.wait("S", self.s_mm, evMMpar[sl])
                self.em("S", lambda s, b1=b1, x=xsl: s.copy(b1, x))
                self.chain("S", self.s_sc)
                self.em("S", lambda s, b2=b2, x=xsl: s.activation(b2, x, AF.Square))
                vb = self.inc("S", self.s_sc, 1)
                self.wait("T", self.s_sc, vb)
                if c == 0:
                    self.wait("T", self.s_ve, evRowEvac)
                self.em("T", lambda t, b1=b1, c=c: t.matmul(
                    ps[0][0:1, :], self.onescol_bf[:], b1, start=(c == 0), stop=(c == KC - 1)))
                self.em("T", lambda t, b2=b2, c=c: t.matmul(
                    ps[1][0:1, :], self.onescol_bf[:], b2, start=(c == 0), stop=(c == KC - 1)))
                evMMpar[sl] = self.inc("T", self.s_mm, 1)
            mmStat = evMMpar[(KC - 1) % 2]
            mrow = self.ln_m[0:1, tt * 512:(tt + 1) * 512]
            rrow = self.ln_r[0:1, tt * 512:(tt + 1) * 512]
            self.wait("V", self.s_mm, mmStat)
            self.em("V", lambda v, m=mrow: v.tensor_scalar_mul(m, ps[0][0:1, :], INV_D))
            self.em("V", lambda v: v.tensor_scalar(self.ln_s1[:], ps[1][0:1, :], INV_D, EPS_LN,
                                                   op0=OP.mult, op1=OP.add))
            self.vbar()
            self.em("V", lambda v, m=mrow: v.tensor_mul(self.ln_s2[:], m, m))
            self.vbar()
            self.em("V", lambda v: v.tensor_sub(self.ln_s1[:], self.ln_s1[:], self.ln_s2[:]))
            evRowEvac = self.inc("V", self.s_ve, 1)
            self.wait("S", self.s_ve, evRowEvac)
            self.em("S", lambda s: s.activation(self.ln_s2[:], self.ln_s1[:], AF.Ln))
            self.chain("S", self.s_sc)
            self.em("S", lambda s: s.activation(self.ln_s2[:], self.ln_s2[:], AF.Exp, scale=-0.5))
            sv = self.inc("S", self.s_sc, 1)
            self.wait("V", self.s_sc, sv)
            self.em("V", lambda v: v.tensor_mul(self.ln_s3[:], self.ln_s2[:], self.ln_s2[:]))
            self.vbar()
            self.em("V", lambda v: v.tensor_mul(self.ln_s3[:], self.ln_s3[:], self.ln_s1[:]))
            self.vbar()
            self.em("V", lambda v: v.tensor_scalar(self.ln_s3[:], self.ln_s3[:], -0.5, 1.5,
                                                   op0=OP.mult, op1=OP.add))
            self.vbar()
            self.em("V", lambda v, r=rrow: v.tensor_mul(r, self.ln_s3[:], self.ln_s2[:]))
            vr = self.inc("V", self.s_ve, 1)
            self.wait("T", self.s_ve, vr)
            self.wait("T", self.s_ve, evApply.get((tt - 1, KC - 1), 0))
            self.em("T", lambda t, m=mrow: t.matmul(ps[2][:], self.onesrow_f[:], m,
                                                    start=True, stop=True))
            self.em("T", lambda t, r=rrow: t.matmul(ps[3][:], self.onesrow_f[:], r,
                                                    start=True, stop=True))
            mmB = self.inc("T", self.s_mm, 1)
            self.wait("V", self.s_mm, mmB)
            stages = []
            for c in range(KC):
                lt = lambda c: self.ln_t[:, (c % 2) * 512:(c % 2 + 1) * 512]
                xs = self.x1_sb[:, c * TOWN + tt * 512:c * TOWN + (tt + 1) * 512]
                stages.append(("fin", c - 2))
                stages.append(("mul", c - 1))
                stages.append(("sub", c))
            stages += [("fin", KC - 2), ("mul", KC - 1), ("fin", KC - 1)]
            for step, c in stages:
                if c < 0:
                    continue
                sl = c % 2
                lts = self.ln_t[:, sl * 512:(sl + 1) * 512]
                xsl = self.x1_sb[:, c * TOWN + tt * 512:c * TOWN + (tt + 1) * 512]
                if step == "sub":
                    prev = (tt, c - 2) if c >= 2 else (tt - 1, KC - 2 + c)
                    self.wait("V", self.s_ve, evApply.get(prev, 0))
                    self.em("V", lambda v, x=xsl, lts=lts: v.tensor_sub(lts, x, ps[2][:]))
                    self.inc("V", self.s_ve, 1)
                elif step == "mul":
                    self.wait("V", self.s_ve, self.s_ve.n)
                    self.em("V", lambda v, lts=lts: v.tensor_mul(lts, lts, ps[3][:]))
                    self.inc("V", self.s_ve, 1)
                else:
                    self.wait("V", self.s_ve, self.s_ve.n)
                    if which == 1:
                        self.em("V", lambda v, x=xsl, lts=lts, c=c: v.tensor_scalar(
                            x, lts, g_sb[:, c:c + 1], be_sb[:, c:c + 1],
                            op0=OP.mult, op1=OP.add))
                        vx = self.inc("V", self.s_ve, 1)
                        csl = self.c_bf[:, c * TOWN + tt * 512:c * TOWN + (tt + 1) * 512]
                        self.wait("S", self.s_ve, vx)
                        self.em("S", lambda s, csl=csl, x=xsl: s.copy(csl, x))
                        self.chain("S", self.s_sc)
                        evApply[(tt, c)] = vx
                    else:
                        self.em("V", lambda v, lts=lts, c=c: v.tensor_scalar(
                            lts, lts, g_sb[:, c:c + 1], be_sb[:, c:c + 1],
                            op0=OP.mult, op1=OP.add))
                        self.vbar()
                        asl = self.acc[:, c * TOWN + tt * 512:c * TOWN + (tt + 1) * 512]
                        self.em("V", lambda v, a=asl, lts=lts: v.tensor_add(a, a, lts))
                        evApply[(tt, c)] = self.inc("V", self.s_ve, 1)

    def emit_b2(self):
        ps = self.ps
        hEv_prev = 0
        yEv_prev = 0
        regEv = 0
        for gg in range(NGRP):
            for i in range(GH):
                n = gg * GH + i
                slot, wv = self.w1_load(self.w1_pk[n])
                self.wait("T", self.s_w1[slot], wv)
                self.wait("T", self.s_sc, hEv_prev)
                for k in range(KC):
                    for tt in range(4):
                        self.em("T", lambda t, k=k, tt=tt, slot=slot: t.matmul(
                            ps[tt][:], self.w1t(slot, k),
                            self.c_bf[:, k * TOWN + tt * 512:k * TOWN + (tt + 1) * 512],
                            start=(k == 0), stop=(k == KC - 1)))
                mmH = self.inc("T", self.s_mm, 1)
                self.w1_ev.append((self.s_mm, mmH))
                self.wait("S", self.s_mm, mmH)
                if regEv:
                    self.wait("S", self.s_mm, regEv)
                for tt in range(4):
                    self.em("S", lambda s, tt=tt, i=i, n=n: s.activation(
                        self.h_b2[:, i * TOWN + tt * 512:i * TOWN + (tt + 1) * 512],
                        ps[tt][:], AF.Relu, bias=self.b1_sb[:, n:n + 1]))
                hEv_prev = self.inc("S", self.s_sc, 1)
            for dout in range(KC):
                slot, wv = self.w2_load(self.w2_pk[gg, dout])
                self.wait("T", self.s_w2[slot], wv)
                self.wait("T", self.s_sc, hEv_prev)
                self.wait("T", self.s_ve, yEv_prev)
                for j in range(GH):
                    for tt in range(4):
                        self.em("T", lambda t, j=j, tt=tt, slot=slot: t.matmul(
                            ps[4 + tt][:], self.w2t(slot, j),
                            self.h_b2[:, j * TOWN + tt * 512:j * TOWN + (tt + 1) * 512],
                            start=(j == 0), stop=(j == GH - 1)))
                mmY = self.inc("T", self.s_mm, 1)
                self.w2_ev.append((self.s_mm, mmY))
                if dout == KC - 1:
                    regEv = mmY
                self.wait("V", self.s_mm, mmY)
                self.wait("V", self.s_ve, yEv_prev)
                for tt in range(4):
                    xsl = self.x1_sb[:, dout * TOWN + tt * 512:dout * TOWN + (tt + 1) * 512]
                    if gg == 0:
                        self.em("V", lambda v, x=xsl, tt=tt, dout=dout: v.scalar_tensor_tensor(
                            x, ps[4 + tt][:], self.b2a_sb[:, dout:dout + 1], x,
                            op0=OP.add, op1=OP.add))
                    else:
                        self.em("V", lambda v, x=xsl, tt=tt: v.tensor_add(x, x, ps[4 + tt][:]))
                yEv_prev = self.inc("V", self.s_ve, 1)

    def emit_b4(self):
        ps = self.ps
        evAdd = {}
        for j in range(NGOWN):
            par = j % 2
            xt = self.xtm_b4[:, par * D:(par + 1) * D]
            ot = self.otm_b4[:, par * D:(par + 1) * D]
            if j >= 2:
                self.wait("Y", self.s_ve, evAdd[j - 2])
            lxv = self.dma("Y", xt, self.xs[j * 128:(j + 1) * 128, :], self.s_lx[par])
            pbase = 0 if par == 0 else 2
            if j >= 2:
                self.wait("V", self.s_st[par], (j // 2) * 16)
            self.wait("T", self.s_ve, evAdd.get(j - 2, 0))
            for c in range(KC):
                self.em("T", lambda t, c=c, j=j, pbase=pbase: t.transpose(
                    ps[pbase + c // 4][:, (c % 4) * 128:(c % 4 + 1) * 128],
                    self.acc[:, c * TOWN + j * 128:c * TOWN + (j + 1) * 128], self.ident[:]))
            mmT = self.inc("T", self.s_mm, 1)
            self.wait("V", self.s_mm, mmT)
            self.wait("V", self.s_lx[par], lxv)
            for hh in range(2):
                self.em("V", lambda v, ot=ot, xt=xt, hh=hh, pbase=pbase: v.tensor_add(
                    ot[:, hh * 512:(hh + 1) * 512], xt[:, hh * 512:(hh + 1) * 512],
                    ps[pbase + hh][:]))
            evAdd[j] = self.inc("V", self.s_ve, 1)
            self.wait("Y", self.s_ve, evAdd[j])
            self.dma("Y", self.out[j * 128:(j + 1) * 128, :], ot, self.s_st[par])
        self.wait("Y", self.s_st[0], self.s_st[0].n)
        self.wait("Y", self.s_st[1], self.s_st[1].n)
        if self.s_sd.n:
            self.wait("Y", self.s_sd, self.s_sd.n)
        if self.s_dbg.n:
            self.wait("Y", self.s_dbg, self.s_dbg.n)

    def build(self, phases=("A", "MOE", "B1", "LN1", "B2", "LN2", "B4")):
        self.emit_aux()
        if "A0" in phases:
            self.emit_phase_a()
            self.barrier()
        if "A" in phases:
            self.emit_phase_a()
            self.emit_phase_a_tail()
            self.barrier()
            self._dump("xnbf", self.xnbf, cast=True)
            self._dump("msb", self.m_sb, cast=True)
            self._dump("ksbd", self.ksbd, cast=True)
            self._dump("wrow", self.wrow_f)
            self._dump("hall", self.hall)
            self._dump("mstat", self.mstat)
            self._dump("rstat", self.rstat)
            self._dump("ktm", self.ktm, cast=True)
        if "MOE" in phases:
            self.emit_moe()
            self.barrier()
            self._dump("accm", self.acc)
        if "B1" in phases:
            self.emit_b1()
            self.barrier()
            self._dump("x1_pre", self.x1_sb)
            self._dump("qsb", self.q_sb, cast=True)
        if "LN1" in phases:
            self.emit_ln(1)
            self.barrier()
            self._dump("cbf", self.c_bf, cast=True)
        if "B2" in phases:
            self.emit_b2()
            self.barrier()
            self._dump("x1n", self.x1_sb)
        if "LN2" in phases:
            self.emit_ln(2)
            self.barrier()
            self._dump("accf", self.acc)
        if "B4" in phases:
            self.emit_b4()
        else:
            self.barrier()
            fb = self.acc[:, 0:D] if "MOE" in phases else self.kvacc[:]
            self.dma("Y", self.out[0:128, :], fb, self.s_st[0])
            self.wait("Y", self.s_st[0], self.s_st[0].n)
            if self.s_sd.n:
                self.wait("Y", self.s_sd, self.s_sd.n)
            if self.s_dbg.n:
                self.wait("Y", self.s_dbg, self.s_dbg.n)
        nc = self.nc
        with nc.Block() as block:
            @block.tensor
            def _(t):
                for fn in self.L["T"]:
                    fn(t)

            @block.vector
            def _(v):
                for fn in self.L["V"]:
                    fn(v)

            @block.scalar
            def _(s):
                for fn in self.L["S"]:
                    fn(s)

            @block.gpsimd
            def _(g):
                for fn in self.L["G"]:
                    fn(g)

            @block.sync
            def _(y):
                for fn in self.L["Y"]:
                    fn(y)
        return nc


def host_pack(inputs):
    f32 = lambda a: np.ascontiguousarray(np.asarray(a, np.float32))
    shared = {
        "wq_pk": pack_w1(f32(inputs["L0_attn_Wq"])),
        "wo_pk": pack_w1(f32(inputs["L0_attn_Wo"])),
        "wk_nat": f32(inputs["L0_attn_Wk"]),
        "wv_nat": f32(inputs["L0_attn_Wv"]),
        "w1_pk": pack_w1(f32(inputs["L0_attn_W1"])),
        "w2_pk": pack_w2(f32(inputs["L0_attn_W2"])),
        "ew1_pk": np.stack([pack_w1(f32(inputs["L0_ffn_W1"][e])) for e in range(NEXP)]),
        "ew2_pk": np.stack([pack_w2(f32(inputs["L0_ffn_W2"][e])) for e in range(NEXP)]),
        "bq_pp": pp(f32(inputs["L0_attn_bq"])),
        "bo_pp": pp(f32(inputs["L0_attn_bo"])),
        "g1_pp": pp(f32(inputs["L0_attn_g1"])),
        "be1_pp": pp(f32(inputs["L0_attn_beta1"])),
        "g2_pp": pp(f32(inputs["L0_attn_g2"])),
        "be2_pp": pp(f32(inputs["L0_attn_beta2"])),
        "b2a_pp": pp(f32(inputs["L0_attn_b2"])),
        "b1_pp": pp(f32(inputs["L0_attn_b1"])),
        "eb1_pp": np.concatenate([pp(f32(inputs["L0_ffn_b1"][e])) for e in range(NEXP)], axis=1),
        "bk_row": f32(inputs["L0_attn_bk"]).reshape(1, D),
        "bv_row": f32(inputs["L0_attn_bv"]).reshape(1, D),
        "eb2_rows": f32(inputs["L0_ffn_b2"]).reshape(NEXP, D),
        "ident_c": np.eye(128, dtype=np.float32),
        "onescol_c": np.ones((128, 1), np.float32),
        "onesrow_c": np.ones((1, 128), np.float32),
        "e2_c": make_e2(),
        "ecol_c": np.arange(NEXP, dtype=np.float32).reshape(NEXP, 1),
        "mkc_c": np.tile(np.array([[1.0, -7.0]], np.float32), (NEXP, 1)),
        "nwc_c": np.tile(np.array([[-0.5, 1.5]], np.float32), (128, 1)),
    }
    return {k: np.ascontiguousarray(v) for k, v in shared.items()}


def make_in_maps(inputs):
    x = np.ascontiguousarray(np.asarray(inputs["L0_attn_x"], np.float32))
    shared = host_pack(inputs)
    in_maps = []
    for c in range(8):
        seq, half = c // 2, c % 2
        own = x[seq, half * TOWN:(half + 1) * TOWN]
        other = x[seq, (1 - half) * TOWN:(2 - half) * TOWN]
        m = dict(shared)
        m["xs"] = np.ascontiguousarray(np.concatenate([own, other], axis=0))
        in_maps.append(m)
    return in_maps


def kernel(**inputs):
    nc = KB().build()
    in_maps = make_in_maps(inputs)
    res = run_bass_kernel_spmd(nc, in_maps, core_ids=list(range(8)))
    out = np.empty((4, 4096, 1024), np.float32)
    for c in range(8):
        seq, half = c // 2, c % 2
        out[seq, half * TOWN:(half + 1) * TOWN] = res.results[c]["out"]
    return out


# version 09
# speedup vs baseline: 1.0177x; 1.0177x over previous
import numpy as np
import ml_dtypes
import concourse.bass as bass
import concourse.mybir as mybir
from concourse.bass_utils import run_bass_kernel_spmd

F32 = mybir.dt.float32
BF16 = mybir.dt.bfloat16
I32 = mybir.dt.int32
AF = mybir.ActivationFunctionType
OP = mybir.AluOpType
AXX = mybir.AxisListType.X

D = 1024
KC = 8
NPAIR = 8
L = 4096
TOWN = 2048
NG = 32
NGOWN = 16
FF = 4096
FFC = 32
NEXP = 8
GH = 8
NGRP = FFC // GH
EPS_LN = 1e-5
EPS_ATTN = 1e-6
INV_D = 1.0 / D

DMA_SCRATCH = 16384
SBUF_TOP = 229344


def pack_w1(W):
    din, dout = W.shape
    return np.ascontiguousarray(
        W.reshape(din // 128, 128, dout // 128, 128).transpose(2, 1, 0, 3)
        .reshape(dout // 128, 128, din))


def pack_w2(W, G=GH):
    din, dout = W.shape
    ngb = din // (G * 128)
    return np.ascontiguousarray(
        W.reshape(ngb, G, 128, dout // 128, 128).transpose(0, 3, 2, 1, 4)
        .reshape(ngb, dout // 128, 128, G * 128))


def pp(vec):
    n = vec.shape[0] // 128
    return np.ascontiguousarray(vec.reshape(n, 128).T)


def make_e2():
    e2 = np.zeros((16, NPAIR * 128), np.float32)
    for p in range(NPAIR):
        e2[2 * p, p * 128:p * 128 + 64] = 1.0
        e2[2 * p + 1, p * 128 + 64:p * 128 + 128] = 1.0
    return e2


class Sem:
    def __init__(self, h, name):
        self.h = h
        self.name = name
        self.n = 0


class KB:
    def __init__(self, debug=(), ngl=NG, alim=6, moe_e=NEXP, moe_y=True, gser=False, moe_wb=True, emod=NEXP, hblk=10000):
        self.debug = set(debug)
        self.ngl = ngl
        self.alim = alim
        self.moe_e = moe_e
        self.moe_y = moe_y
        self.gser = gser
        self.moe_wb = moe_wb
        self.emod = emod
        self.hblk = hblk
        nc = bass.Bass(trn_type="TRN2", target_bir_lowering=False,
                       dynamic_dma_scratch_size=DMA_SCRATCH)
        self.nc = nc
        self.base = DMA_SCRATCH + 32
        self.L = {k: [] for k in "TVSGY"}
        self.meta = {k: [] for k in "TVSGY"}
        self._decl_dram()
        self._decl_sbuf()
        self.ps = [nc.alloc_psum_tensor(f"ps{i}", [128, 512], F32) for i in range(8)]
        mk = lambda n: Sem(nc.alloc_semaphore(n), n)
        self.s_lx = [mk("lx0"), mk("lx1")]
        self.s_w1 = [mk("w1a"), mk("w1b")]
        self.s_w2 = [mk("w2a"), mk("w2b")]
        self.s_aux = mk("aux")
        self.s_sd = mk("sd")
        self.s_dbg = mk("dbg")
        self.s_eb = [mk("eb0"), mk("eb1")]
        self.s_mm = mk("mm")
        self.s_sc = mk("sc")
        self.s_ve = mk("ve")
        self.s_st = [mk("st0"), mk("st1")]
        self.s_bar = mk("bar")
        self.w1_idx = 0
        self.w1_ev = []
        self.w2_idx = 0
        self.w2_ev = []

    def _decl_dram(self):
        nc = self.nc
        dt = lambda n, s: nc.dram_tensor(n, s, F32, kind="ExternalInput").ap()
        self.xs = dt("xs", [L, D])
        self.wq_pk = dt("wq_pk", [KC, 128, D])
        self.wo_pk = dt("wo_pk", [KC, 128, D])
        self.wk_nat = dt("wk_nat", [D, D])
        self.wv_nat = dt("wv_nat", [D, D])
        self.w1_pk = dt("w1_pk", [FFC, 128, D])
        self.w2_pk = dt("w2_pk", [NGRP, KC, 128, GH * 128])
        self.ew1_pk = dt("ew1_pk", [NEXP, FFC, 128, D])
        self.ew2_pk = dt("ew2_pk", [NEXP, NGRP, KC, 128, GH * 128])
        self.bq_pp = dt("bq_pp", [128, KC])
        self.bo_pp = dt("bo_pp", [128, KC])
        self.g1_pp = dt("g1_pp", [128, KC])
        self.be1_pp = dt("be1_pp", [128, KC])
        self.g2_pp = dt("g2_pp", [128, KC])
        self.be2_pp = dt("be2_pp", [128, KC])
        self.b2a_pp = dt("b2a_pp", [128, KC])
        self.b1_pp = dt("b1_pp", [128, FFC])
        self.eb1_pp = dt("eb1_pp", [128, NEXP * FFC])
        self.bk_row = dt("bk_row", [1, D])
        self.bv_row = dt("bv_row", [1, D])
        self.eb2_rows = dt("eb2_rows", [NEXP, D])
        self.ident_c = dt("ident_c", [128, 128])
        self.onescol_c = dt("onescol_c", [128, 1])
        self.onesrow_c = dt("onesrow_c", [1, 128])
        self.e2_c = dt("e2_c", [16, NPAIR * 128])
        self.mkc_c = dt("mkc_c", [NEXP, 2])
        self.nwc_c = dt("nwc_c", [128, 2])
        self.ecol_c = dt("ecol_c", [NEXP, 1])
        self.out = nc.dram_tensor("out", [TOWN, D], F32, kind="ExternalOutput").ap()
        self.dbg = {}
        for name, shape in (("xnbf", [128, KC * TOWN]), ("msb", [128, NPAIR * 128]),
                            ("ksbd", [128, NPAIR * 16]), ("wrow", [NEXP, TOWN]),
                            ("accm", [128, KC * TOWN]), ("x1n", [128, KC * TOWN]),
                            ("accf", [128, KC * TOWN]), ("hall", [128, NGOWN]),
                            ("ktm", [128, 2 * D]), ("mstat", [128, NGOWN]),
                            ("rstat", [128, NGOWN]), ("x1_pre", [128, KC * TOWN]),
                            ("qsb", [128, KC * TOWN]), ("cbf", [128, KC * TOWN])):
            if name in self.debug:
                self.dbg[name] = nc.dram_tensor(
                    "dbg_" + name, shape, F32, kind="ExternalOutput").ap()

    def _sb(self, name, shape, dtype, off, region_end=None):
        per = int(np.prod(shape[1:])) * mybir.dt.size(dtype)
        off = self.base + off
        assert off % 32 == 0, (name, off)
        assert off + per <= SBUF_TOP, (name, off, per, off + per)
        if region_end is not None:
            assert off + per <= self.base + region_end, (name, off + per, region_end)
        return self.nc.alloc_sbuf_tensor_at(name, list(shape), dtype, offset=off)

    def _decl_sbuf(self):
        sb = self._sb
        self.wk_sb = sb("wk_sb", [128, KC * D], BF16, 0, 65536)
        self.wv_sb = sb("wv_sb", [128, KC * D], BF16, 16384, 65536)
        self.kvbias = sb("kvbias", [1, 2 * D], BF16, 32768, 65536)
        self.xnp = sb("xnp", [128, 2 * KC * 128], BF16, 36864, 65536)
        self.stat48 = sb("stat48", [48, 128], F32, 40960, 65536)
        self.kvacc = sb("kvacc", [128, NPAIR * 128], F32, 45056, 65536)
        self.ksacc = sb("ksacc", [128, NPAIR], F32, 49600, 65536)
        self.h_m = [sb("h_lo", [128, GH * TOWN], BF16, 0, 65536),
                    sb("h_hi", [128, GH * TOWN], BF16, 32768, 65536)]
        self.x1_sb = sb("x1_sb", [128, KC * TOWN], F32, 0, 65536)
        self.acc = sb("acc", [128, KC * TOWN], F32, 65536, 131072)
        XB = 131072
        self.xnbf = sb("xnbf", [128, KC * TOWN], BF16, XB, 163840)
        self.xnfm = sb("xnfm", [128, KC * 1024], F32, XB, 163840)
        self.c_bf = sb("c_bf", [128, KC * TOWN], BF16, XB, 163840)
        TR = 163840
        self.xtm_a = sb("xtm_a", [128, 2 * D], F32, TR, 196608)
        self.ktm = sb("ktm", [128, 2 * D], BF16, TR + 8192, 196608)
        self.vtm = sb("vtm", [128, 2 * D], BF16, TR + 12288, 196608)
        self.elu_a = sb("elu_a", [128, 2 * D], F32, TR + 16384, 196608)
        self.hrow = sb("hrow", [1, TOWN], F32, TR, 196608)
        self.h8 = sb("h8", [NEXP, TOWN], F32, TR + 8192, 196608)
        self.wscr = sb("wscr", [NEXP, TOWN], F32, TR + 16384, 196608)
        self.wrow_f = sb("wrow_f", [NEXP, TOWN], F32, TR + 24576, 196608)
        self.wb = sb("wb", [128, 2 * TOWN], BF16, TR, 196608)
        self.wrow_bf = sb("wrow_bf", [NEXP, TOWN], BF16, TR + 8192, 196608)
        self.wtmp = sb("wtmp", [1, 2 * TOWN], BF16, TR + 12288, 196608)
        self.eb2cur = sb("eb2cur", [1, 2 * D], BF16, TR + 20480, 196608)
        self.q_sb = sb("q_sb", [128, KC * TOWN], BF16, TR, 196608)
        self.ln_m = sb("ln_m", [1, TOWN], F32, TR, 196608)
        self.ln_r = sb("ln_r", [1, TOWN], F32, TR + 8192, 196608)
        self.ln_bf = sb("ln_bf", [128, 4 * 512], BF16, TR + 16384, 196608)
        self.ln_t = sb("ln_t", [128, 2 * 512], F32, TR + 20480, 196608)
        self.ln_s1 = sb("ln_s1", [1, 512], F32, TR + 24576, 196608)
        self.ln_s2 = sb("ln_s2", [1, 512], F32, TR + 26624, 196608)
        self.ln_s3 = sb("ln_s3", [1, 512], F32, TR + 28672, 196608)
        self.h_b2 = sb("h_b2", [128, GH * TOWN], BF16, TR, 196608)
        self.xtm_b4 = sb("xtm_b4", [128, 2 * D], F32, TR, 196608)
        self.otm_b4 = sb("otm_b4", [128, 2 * D], F32, TR + 8192, 196608)
        ST = 196608
        self.w1s = sb("w1s", [128, 2 * KC * 128], BF16, ST, 204800)
        self.w2s = sb("w2s", [128, 2 * GH * 128], BF16, ST + 4096, 204800)
        self.zbfb = sb("zbfb", [16, 512], BF16, ST + 4096, 204800)
        self.zb = sb("zb", [128, 2 * 512], BF16, ST + 5120, 204800)
        self.xt_b1 = sb("xt_b1", [128, 512], F32, ST + 6144, 204800)
        SM = 204800
        self.ident = sb("ident", [128, 128], F32, SM)
        self.onescol_bf = sb("onescol_bf", [128, 1], BF16, SM + 512)
        self.onesrow_bf = sb("onesrow_bf", [1, 128], BF16, SM + 544)
        self.onesrow_f = sb("onesrow_f", [1, 128], F32, SM + 800)
        self.bq_sb = sb("bq_sb", [128, KC], F32, SM + 1312)
        self.bo_sb = sb("bo_sb", [128, KC], F32, SM + 1344)
        self.g1_sb = sb("g1_sb", [128, KC], F32, SM + 1376)
        self.be1_sb = sb("be1_sb", [128, KC], F32, SM + 1408)
        self.g2_sb = sb("g2_sb", [128, KC], F32, SM + 1440)
        self.be2_sb = sb("be2_sb", [128, KC], F32, SM + 1472)
        self.b2a_sb = sb("b2a_sb", [128, KC], F32, SM + 1504)
        self.b1_sb = sb("b1_sb", [128, FFC], BF16, SM + 1536)
        self.eb1_sb = sb("eb1_sb", [128, NEXP * FFC], BF16, SM + 1600)
        self.mstat = sb("mstat", [128, NGOWN], F32, SM + 2112)
        self.rstat = sb("rstat", [128, NGOWN], F32, SM + 2176)
        self.ksbd = sb("ksbd", [128, NPAIR * 16], BF16, SM + 2240)
        self.m_sb = sb("m_sb", [128, NPAIR * 128], BF16, SM + 2496)
        self.e2_sb = sb("e2_sb", [16, NPAIR * 128], BF16, SM + 4544)
        self.dum = sb("dum", [128, 8], F32, SM + 6624)
        SZ = SM + 6656
        self.elu_r = sb("elu_r", [128, 512], BF16, SZ)
        self.tpack = sb("tpack", [128, 128], F32, SZ)
        self.xn01 = sb("xn01", [128, 2 * NGOWN], F32, SZ + 512)
        self.hall = sb("hall", [128, NGOWN], F32, SZ + 704)
        self.scol = sb("scol", [128, 4], F32, SZ + 768)
        self.ecol_sb = sb("ecol_sb", [NEXP, 1], F32, SZ + 800)
        self.mkc_sb = sb("mkc_sb", [NEXP, 2], F32, SZ + 832)
        self.nwc_sb = sb("nwc_sb", [128, 2], F32, SM + 6592)
        assert SZ + 1024 <= SBUF_TOP - self.base, (SZ + 1024, SBUF_TOP - self.base)

    def em(self, e, fn):
        self.L[e].append(fn)
        self.meta[e].append(("op", None, 0))

    def wait(self, e, sem, val):
        if val > 0:
            self.L[e].append(lambda x, s=sem.h, v=val: x.wait_ge(s, v))
            self.meta[e].append(("wait", sem.name, val))

    def inc(self, e, sem, n=1):
        prev = self.L[e].pop()
        sem.n += n
        def fn(x, prev=prev, s=sem.h, n=n):
            prev(x).then_inc(s, n)
        self.L[e].append(fn)
        self.meta[e][-1] = ("op", sem.name, n)
        return sem.n

    def static_check(self):
        pos = {e: 0 for e in "TVSGY"}
        sems = {}
        while True:
            progress = False
            done = True
            for e in "TVSGY":
                while pos[e] < len(self.meta[e]):
                    kind, sem, val = self.meta[e][pos[e]]
                    if kind == "wait" and sems.get(sem, 0) < val:
                        break
                    if kind == "op" and sem is not None:
                        sems[sem] = sems.get(sem, 0) + val
                    pos[e] += 1
                    progress = True
                if pos[e] < len(self.meta[e]):
                    done = False
            if done:
                return
            if not progress:
                msg = []
                for e in "TVSGY":
                    if pos[e] < len(self.meta[e]):
                        k, sem, val = self.meta[e][pos[e]]
                        msg.append(f"{e}@{pos[e]}: wait {sem}>={val} (cur {sems.get(sem, 0)})")
                raise RuntimeError("STATIC DEADLOCK: " + "; ".join(msg))

    def chain(self, e, sem):
        v = self.inc(e, sem, 1)
        self.wait(e, sem, v)
        return v

    def vbar(self):
        return self.chain("V", self.s_ve)

    def dma(self, e, out, in_, sem):
        self.em(e, lambda x, o=out, i=in_: x.dma_start(out=o, in_=i))
        return self.inc(e, sem, 16)

    def barrier(self):
        self.em("T", lambda x: x.nop())
        self.inc("T", self.s_bar, 1)
        self.em("V", lambda x: x.memset(self.dum[:, 0:2], 0.0))
        self.inc("V", self.s_bar, 1)
        self.em("S", lambda x: x.memzero(self.dum[:, 2:4]))
        self.inc("S", self.s_bar, 1)
        self.em("G", lambda x: x.memset(self.dum[:, 4:6], 0.0))
        self.inc("G", self.s_bar, 1)
        tgt = self.s_bar.n
        for e in "TVSGY":
            self.wait(e, self.s_bar, tgt)

    def w1_load(self, src):
        i = self.w1_idx
        self.w1_idx += 1
        slot = i % 2
        if i >= 2:
            sem, val = self.w1_ev[i - 2]
            self.wait("G", sem, val)
        if self.gser:
            self.wait("G", self.s_w1[slot], self.s_w1[slot].n)
        v = self.dma("G", self.w1s[:, slot * 1024:(slot + 1) * 1024], src, self.s_w1[slot])
        return slot, v

    def w1t(self, slot, k):
        return self.w1s[:, slot * 1024 + k * 128: slot * 1024 + (k + 1) * 128]

    def w2_load(self, src):
        i = self.w2_idx
        self.w2_idx += 1
        slot = i % 2
        if i >= 2:
            sem, val = self.w2_ev[i - 2]
            self.wait("G", sem, val)
        v = self.dma("G", self.w2s[:, slot * 1024:(slot + 1) * 1024], src, self.s_w2[slot])
        return slot, v

    def w2t(self, slot, j):
        return self.w2s[:, slot * 1024 + j * 128: slot * 1024 + (j + 1) * 128]

    def _dump(self, name, src, cast=False):
        if name not in self.dbg:
            return
        if cast:
            sem = self.s_dbg
            self.dma("G", self.dbg[name], src[:], sem)
        else:
            sem = self.s_sd
            self.wait("Y", sem, sem.n)
            self.dma("Y", self.dbg[name], src[:], sem)
        for e in "TVSG":
            self.wait(e, sem, sem.n)

    def emit_aux(self):
        d = self.dma
        for k in range(KC):
            d("G", self.wk_sb[:, k * D:(k + 1) * D],
              self.wk_nat[k * 128:(k + 1) * 128, :], self.s_aux)
            d("G", self.wv_sb[:, k * D:(k + 1) * D],
              self.wv_nat[k * 128:(k + 1) * 128, :], self.s_aux)
        d("G", self.kvbias[0:1, 0:D], self.bk_row, self.s_aux)
        d("G", self.kvbias[0:1, D:2 * D], self.bv_row, self.s_aux)
        for dst, src in ((self.ident, self.ident_c), (self.onescol_bf, self.onescol_c),
                         (self.onesrow_bf, self.onesrow_c), (self.onesrow_f, self.onesrow_c),
                         (self.bq_sb, self.bq_pp), (self.bo_sb, self.bo_pp),
                         (self.g1_sb, self.g1_pp), (self.be1_sb, self.be1_pp),
                         (self.g2_sb, self.g2_pp), (self.be2_sb, self.be2_pp),
                         (self.b2a_sb, self.b2a_pp), (self.b1_sb, self.b1_pp),
                         (self.eb1_sb, self.eb1_pp), (self.ecol_sb, self.ecol_c),
                         (self.e2_sb, self.e2_c), (self.mkc_sb, self.mkc_c),
                         (self.nwc_sb, self.nwc_c)):
            d("G", dst[:], src, self.s_aux)
        self.aux_tgt = self.s_aux.n
        for e in "TVS":
            self.wait(e, self.s_aux, self.aux_tgt)
        self.em("V", lambda v: v.memset(self.m_sb[:], 0.0))
        self.em("V", lambda v: v.memset(self.ksbd[:], 0.0))
        self.inc("V", self.s_ve, 1)

    def emit_phase_a(self):
        ps = self.ps
        NGL = self.ngl
        evT = {}; evKV = {}; evKVmm = {}; evA = {}; evXN = {}
        evK_s = {}; evK_v = {}; evV_v = {}; evDMA = {}
        evXNw = 0

        def lhs(c, g):
            own = g < NGOWN
            if own:
                return self.xnbf[:, c * TOWN + g * 128:c * TOWN + g * 128 + 128]
            return self.xnp[:, ((g % 2) * KC + c) * 128:((g % 2) * KC + c + 1) * 128]

        def emit_stats(g, mid_v=None, mid_s=None):
            own = g < NGOWN
            par = g % 2
            xt = self.xtm_a[:, par * D:(par + 1) * D]
            if g >= 2:
                self.wait("Y", self.s_mm, evT[g - 2])
            evDMA[g] = self.dma("Y", xt, self.xs[g * 128:(g + 1) * 128, :], self.s_lx[par])
            mc = self.mstat[:, g:g + 1] if own else self.scol[:, 0:1]
            rc = self.rstat[:, g:g + 1] if own else self.scol[:, 1:2]
            ac = self.scol[:, 2:3]
            sq = self.elu_a[:, par * D:(par + 1) * D]
            self.wait("V", self.s_lx[par], evDMA[g])
            self.em("V", lambda v, xt=xt, mc=mc: v.reduce_sum(mc, xt, axis=AXX))
            self.vbar()
            self.em("V", lambda v, mc=mc: v.tensor_scalar_mul(mc, mc, -INV_D))
            v_negm = self.inc("V", self.s_ve, 1)
            self.wait("S", self.s_ve, v_negm)
            if g >= 2:
                self.wait("S", self.s_mm, evKVmm.get(g - 2, 0))
            self.em("S", lambda s, xt=xt, sq=sq, mc=mc: s.activation(sq, xt, AF.Square, bias=mc))
            self.chain("S", self.s_sc)
            self.em("S", lambda s, xt=xt, mc=mc: s.add(xt, xt, mc))
            s_sub = self.inc("S", self.s_sc, 1)
            if mid_v is not None:
                mid_v()
            self.wait("V", self.s_sc, s_sub)
            self.em("V", lambda v, sq=sq, ac=ac: v.reduce_sum(ac, sq, axis=AXX))
            self.vbar()
            self.em("V", lambda v, ac=ac: v.tensor_scalar(ac, ac, INV_D, EPS_LN, op0=OP.mult, op1=OP.add))
            v_ac = self.inc("V", self.s_ve, 1)
            if mid_s is not None:
                mid_s()
            r0 = self.scol[:, 3:4]
            self.wait("S", self.s_ve, v_ac)
            self.em("S", lambda s, ac=ac, r0=r0: s.activation(r0, ac, AF.Ln))
            self.chain("S", self.s_sc)
            self.em("S", lambda s, r0=r0: s.activation(r0, r0, AF.Exp, scale=-0.5))
            self.chain("S", self.s_sc)
            self.em("S", lambda s, rc=rc, r0=r0: s.mul(rc, r0, r0))
            self.chain("S", self.s_sc)
            self.em("S", lambda s, rc=rc, ac=ac: s.mul(rc, rc, ac))
            self.chain("S", self.s_sc)
            self.em("S", lambda s, rc=rc: s.mul(rc, rc, self.nwc_sb[:, 0:1]))
            self.chain("S", self.s_sc)
            self.em("S", lambda s, rc=rc: s.add(rc, rc, self.nwc_sb[:, 1:2]))
            self.chain("S", self.s_sc)
            self.em("S", lambda s, rc=rc, r0=r0: s.mul(rc, rc, r0))
            self.chain("S", self.s_sc)
            self.em("S", lambda s, xt=xt, rc=rc: s.mul(xt, xt, rc))
            if own:
                self.chain("S", self.s_sc)
                self.em("S", lambda s, xt=xt, g=g: s.copy(self.xn01[:, g:g + 1], xt[:, 0:1]))
                self.chain("S", self.s_sc)
                self.em("S", lambda s, xt=xt, g=g: s.copy(
                    self.xn01[:, NGOWN + g:NGOWN + g + 1], xt[:, 1:2]))
            evA[g] = self.inc("S", self.s_sc, 1)

        def emit_T(g):
            nonlocal evXNw
            own = g < NGOWN
            par = g % 2
            xt = self.xtm_a[:, par * D:(par + 1) * D]
            self.wait("T", self.s_sc, evA[g])
            for wave in range(2):
                self.wait("T", self.s_sc, evXNw)
                for j in range(4):
                    c = wave * 4 + j
                    self.em("T", lambda t, xt=xt, c=c, j=j: t.transpose(
                        ps[4][:, j * 128:(j + 1) * 128], xt[:, c * 128:(c + 1) * 128],
                        self.ident[:]))
                mmv = self.inc("T", self.s_mm, 1)
                if wave == 1:
                    evT[g] = mmv
                self.wait("S", self.s_mm, mmv)
                if (not own) and g >= 2:
                    self.wait("S", self.s_mm, evKVmm.get(g - 2, 0))
                for j in range(4):
                    c = wave * 4 + j
                    self.em("S", lambda s, dst=lhs(c, g), j=j: s.copy(dst, ps[4][:, j * 128:(j + 1) * 128]))
                evXNw = self.inc("S", self.s_sc, 1)
            evXN[g] = evXNw
            self.wait("T", self.s_sc, evXN[g])
            if g >= 1:
                self.wait("T", self.s_sc, evK_s[g - 1])
                self.wait("T", self.s_ve, evK_v[g - 1])
            for k in range(KC):
                for hh in range(2):
                    self.em("T", lambda t, k=k, hh=hh, lh=lhs(k, g): t.matmul(
                        ps[hh][:], lh, self.wk_sb[:, k * D + hh * 512:k * D + (hh + 1) * 512],
                        start=(k == 0), stop=False))
            for hh in range(2):
                self.em("T", lambda t, hh=hh: t.matmul(
                    ps[hh][:], self.onesrow_bf[:], self.kvbias[0:1, hh * 512:(hh + 1) * 512],
                    start=False, stop=True))
            evKmm = self.inc("T", self.s_mm, 1)
            if g >= 1:
                self.wait("T", self.s_ve, evV_v[g - 1])
            for k in range(KC):
                for hh in range(2):
                    self.em("T", lambda t, k=k, hh=hh, lh=lhs(k, g): t.matmul(
                        ps[2 + hh][:], lh, self.wv_sb[:, k * D + hh * 512:k * D + (hh + 1) * 512],
                        start=(k == 0), stop=False))
            for hh in range(2):
                self.em("T", lambda t, hh=hh: t.matmul(
                    ps[2 + hh][:], self.onesrow_bf[:],
                    self.kvbias[0:1, D + hh * 512:D + (hh + 1) * 512],
                    start=False, stop=True))
            evVmm = self.inc("T", self.s_mm, 1)
            return evKmm, evVmm

        def elu_vmax(g, evKmm):
            par = g % 2
            rt = self.elu_a[:, par * D:(par + 1) * D]
            self.wait("V", self.s_mm, evKmm)
            if g >= 2:
                self.wait("V", self.s_mm, evKVmm.get(g - 2, 0))
            for hh in range(2):
                self.em("V", lambda v, hh=hh, rt=rt: v.tensor_scalar(
                    rt[:, hh * 512:(hh + 1) * 512], ps[hh][:], 0.0, None, op0=OP.max))
            evK_v[g] = self.inc("V", self.s_ve, 1)

        def elu_sexp(g, evKmm):
            par = g % 2
            kslot = self.ktm[:, par * D:(par + 1) * D]
            self.wait("S", self.s_mm, evKmm)
            self.wait("S", self.s_ve, evK_v[g])
            if g >= 2:
                self.wait("S", self.s_mm, evKVmm.get(g - 2, 0))
            for hh in range(2):
                self.em("S", lambda s, hh=hh, kslot=kslot: s.activation(
                    kslot[:, hh * 512:(hh + 1) * 512], ps[hh][:], AF.Exp))
            evK_s[g] = self.inc("S", self.s_sc, 1)

        def elu_vtail(g, evVmm):
            par = g % 2
            kslot = self.ktm[:, par * D:(par + 1) * D]
            vslot = self.vtm[:, par * D:(par + 1) * D]
            rt = self.elu_a[:, par * D:(par + 1) * D]
            self.wait("V", self.s_sc, evK_s[g])
            for hh in range(2):
                self.em("V", lambda v, hh=hh, kslot=kslot, rt=rt: v.scalar_tensor_tensor(
                    kslot[:, hh * 512:(hh + 1) * 512], kslot[:, hh * 512:(hh + 1) * 512],
                    1.0, rt[:, hh * 512:(hh + 1) * 512], op0=OP.min, op1=OP.add))
            self.vbar()
            self.wait("V", self.s_mm, evVmm)
            for hh in range(2):
                self.em("V", lambda v, hh=hh, vslot=vslot: v.tensor_copy(
                    vslot[:, hh * 512:(hh + 1) * 512], ps[2 + hh][:]))
            evV_v[g] = self.inc("V", self.s_ve, 1)

        def emit_kv(g):
            par = g % 2
            kslot = self.ktm[:, par * D:(par + 1) * D]
            vslot = self.vtm[:, par * D:(par + 1) * D]
            self.wait("T", self.s_ve, evV_v[g])
            if g >= 1:
                self.wait("T", self.s_ve, evKV[g - 1])
            for pr in range(NPAIR):
                b = 5 + pr // 4
                o = pr % 4
                self.em("T", lambda t, pr=pr, b=b, o=o, kslot=kslot, vslot=vslot: t.matmul(
                    ps[b][:, o * 128:(o + 1) * 128], kslot[:, pr * 128:(pr + 1) * 128],
                    vslot[:, pr * 128:(pr + 1) * 128], start=True, stop=True))
                self.em("T", lambda t, pr=pr, kslot=kslot: t.matmul(
                    ps[7][:, pr:pr + 1], kslot[:, pr * 128:(pr + 1) * 128],
                    self.onescol_bf[:], start=True, stop=True))
            evKVmm[g] = self.inc("T", self.s_mm, 1)
            self.wait("V", self.s_mm, evKVmm[g])
            if g >= 1:
                self.wait("V", self.s_ve, evKV[g - 1])
            if g == 0:
                self.em("V", lambda v: v.tensor_copy(self.kvacc[:, 0:512], ps[5][:]))
                self.em("V", lambda v: v.tensor_copy(self.kvacc[:, 512:1024], ps[6][:]))
                self.em("V", lambda v: v.tensor_copy(self.ksacc[:], ps[7][:, 0:NPAIR]))
            else:
                self.em("V", lambda v: v.tensor_add(self.kvacc[:, 0:512], self.kvacc[:, 0:512], ps[5][:]))
                self.em("V", lambda v: v.tensor_add(self.kvacc[:, 512:1024], self.kvacc[:, 512:1024], ps[6][:]))
                self.em("V", lambda v: v.tensor_add(self.ksacc[:], self.ksacc[:], ps[7][:, 0:NPAIR]))
            evKV[g] = self.inc("V", self.s_ve, 1)

        pend = []
        for g in range(NGL):
            if pend:
                pg, ek, ev = pend.pop(0)
                emit_stats(g, mid_v=lambda pg=pg, ek=ek: elu_vmax(pg, ek),
                           mid_s=lambda pg=pg, ek=ek, ev=ev: (elu_sexp(pg, ek),
                                                             elu_vtail(pg, ev)))
                emit_kv(pg)
            else:
                emit_stats(g)
            ek, ev = emit_T(g)
            pend.append((g, ek, ev))
        pg, ek, ev = pend.pop(0)
        elu_vmax(pg, ek)
        elu_sexp(pg, ek)
        elu_vtail(pg, ev)
        emit_kv(NGL - 1)
        self.evKV_last = evKV[NGL - 1]

    def emit_phase_a_tail(self):
        ps = self.ps
        self.wait("V", self.s_ve, self.evKV_last)
        for pr in range(NPAIR):
            self.em("V", lambda v, pr=pr: v.tensor_copy(
                self.m_sb[0:64, pr * 128:pr * 128 + 64],
                self.kvacc[0:64, pr * 128:pr * 128 + 64]))
            self.em("V", lambda v, pr=pr: v.tensor_copy(
                self.m_sb[64:128, pr * 128 + 64:pr * 128 + 128],
                self.kvacc[64:128, pr * 128 + 64:pr * 128 + 128]))
            self.em("V", lambda v, pr=pr: v.tensor_copy(
                self.ksbd[0:64, pr * 16 + 2 * pr:pr * 16 + 2 * pr + 1],
                self.ksacc[0:64, pr:pr + 1]))
            self.em("V", lambda v, pr=pr: v.tensor_copy(
                self.ksbd[64:128, pr * 16 + 2 * pr + 1:pr * 16 + 2 * pr + 2],
                self.ksacc[64:128, pr:pr + 1]))
        u = self.hall[:]
        t1 = self.xn01[:, 0:NGOWN]
        t2 = self.xn01[:, NGOWN:2 * NGOWN]
        self.em("V", lambda v: v.tensor_add(u, t1, t2))
        self.vbar()
        for bnd in (64.0, 32.0, 16.0, 8.0):
            self.em("V", lambda v, b=bnd: v.tensor_single_scalar(t1, u, b, OP.is_ge))
            self.vbar()
            self.em("V", lambda v, b=bnd: v.scalar_tensor_tensor(u, t1, -b, u, op0=OP.mult, op1=OP.add))
            self.vbar()
            self.em("V", lambda v, b=bnd: v.tensor_single_scalar(t1, u, -b, OP.is_le))
            self.vbar()
            self.em("V", lambda v, b=bnd: v.scalar_tensor_tensor(u, t1, b, u, op0=OP.mult, op1=OP.add))
            self.vbar()
        self.em("V", lambda v: v.tensor_single_scalar(t1, u, 1.0, OP.is_ge))
        self.vbar()
        for kk in range(2, 8):
            self.em("V", lambda v, k=float(kk): v.tensor_single_scalar(t2, u, k, OP.is_ge))
            self.vbar()
            self.em("V", lambda v: v.tensor_add(t1, t1, t2))
            self.vbar()
        for kk in range(1, 8):
            self.em("V", lambda v, k=float(-kk): v.tensor_single_scalar(t2, u, k, OP.is_le))
            self.vbar()
            self.em("V", lambda v: v.tensor_sub(t1, t1, t2))
            self.vbar()
        self.em("V", lambda v: v.tensor_single_scalar(t2, t1, 0.0, OP.is_lt))
        self.vbar()
        self.em("V", lambda v: v.scalar_tensor_tensor(u, t2, 8.0, t1, op0=OP.mult, op1=OP.add))
        self.vbar()
        self.em("V", lambda v: v.tensor_copy(self.tpack[:, 0:NGOWN], self.mstat[:]))
        self.em("V", lambda v: v.tensor_copy(self.tpack[:, NGOWN:2 * NGOWN], self.rstat[:]))
        self.em("V", lambda v: v.tensor_copy(self.tpack[:, 2 * NGOWN:3 * NGOWN], self.hall[:]))
        vp = self.inc("V", self.s_ve, 1)
        self.wait("T", self.s_ve, vp)
        self.em("T", lambda t: t.transpose(ps[4][:, 0:128], self.tpack[:], self.ident[:]))
        mt = self.inc("T", self.s_mm, 1)
        self.wait("V", self.s_mm, mt)
        self.em("V", lambda v: v.tensor_copy(self.stat48[0:16, :], ps[4][32:48, 0:128]))
        vs = self.inc("V", self.s_ve, 1)
        self.wait("Y", self.s_ve, vs)
        self.dma("Y", self.hrow[0:1, :], self.stat48[0:16, :], self.s_sd)
        for e in range(NEXP):
            self.wait("Y", self.s_sd, self.s_sd.n)
            self.dma("Y", self.h8[e:e + 1, :], self.hrow[0:1, :], self.s_sd)
        sdv = self.s_sd.n
        self.wait("V", self.s_sd, sdv)
        self.em("V", lambda v: v.tensor_scalar(self.h8[:], self.h8[:], self.ecol_sb[:], None,
                                               op0=OP.subtract))
        vd = self.inc("V", self.s_ve, 1)
        self.wait("S", self.s_ve, vd)
        self.em("S", lambda s: s.activation(self.wscr[:], self.h8[:], AF.Abs))
        self.chain("S", self.s_sc)
        self.em("S", lambda s: s.activation(self.wrow_f[:], self.wscr[:], AF.Relu, scale=-1.0, bias=self.mkc_sb[:, 0:1]))
        self.chain("S", self.s_sc)
        self.em("S", lambda s: s.activation(self.wscr[:], self.h8[:], AF.Abs, bias=self.mkc_sb[:, 0:1]))
        self.chain("S", self.s_sc)
        self.em("S", lambda s: s.activation(self.wscr[:], self.wscr[:], AF.Relu, scale=-1.0, bias=self.mkc_sb[:, 0:1]))
        s1 = self.inc("S", self.s_sc, 1)
        self.wait("V", self.s_sc, s1)
        self.em("V", lambda v: v.tensor_add(self.wrow_f[:], self.wrow_f[:], self.wscr[:]))
        v1 = self.inc("V", self.s_ve, 1)
        self.wait("S", self.s_ve, v1)
        self.em("S", lambda s: s.activation(self.wscr[:], self.h8[:], AF.Abs, bias=self.mkc_sb[:, 1:2]))
        self.chain("S", self.s_sc)
        self.em("S", lambda s: s.activation(self.wscr[:], self.wscr[:], AF.Relu, scale=-1.0, bias=self.mkc_sb[:, 0:1]))
        s2 = self.inc("S", self.s_sc, 1)
        self.wait("V", self.s_sc, s2)
        self.em("V", lambda v: v.tensor_add(self.wrow_f[:], self.wrow_f[:], self.wscr[:]))
        self.vbar()
        self.em("V", lambda v: v.tensor_scalar_mul(self.wrow_f[:], self.wrow_f[:], 0.5))
        self.vbar()
        self.em("V", lambda v: v.tensor_copy(self.wrow_bf[:], self.wrow_f[:]))
        self.wrow_ev = self.inc("V", self.s_ve, 1)

    def emit_moe(self):
        ps = self.ps
        regA_ev = [0, 0]
        hEv_prev = [0]
        wbEv = [0, 0]
        wtmpRead = [0, 0]
        hW_last = {}
        yEv_prev = [0]
        ebv_of = {}

        def emit_wb(e):
            sl = e % 2
            self.wait("Y", self.s_ve, self.wrow_ev)
            if e >= 2:
                self.wait("Y", self.s_mm, wtmpRead[sl])
            self.wait("Y", self.s_sd, self.s_sd.n)
            self.dma("Y", self.wtmp[0:1, sl * TOWN:(sl + 1) * TOWN],
                     self.wrow_bf[e:e + 1, :], self.s_sd)
            sdv = self.s_sd.n
            if e >= 2:
                self.wait("G", self.s_mm, wtmpRead[sl])
            ebv_of[e] = self.dma("G", self.eb2cur[0:1, sl * D:(sl + 1) * D],
                                 self.eb2_rows[e:e + 1, :], self.s_eb[sl])
            self.wait("T", self.s_sd, sdv)
            self.wait("T", self.s_sc, hEv_prev[0])
            for tt in range(4):
                self.em("T", lambda t, tt=tt, sl=sl: t.matmul(
                    ps[tt][:], self.onesrow_bf[:],
                    self.wtmp[0:1, sl * TOWN + tt * 512:sl * TOWN + (tt + 1) * 512],
                    start=True, stop=True))
            mmwb = self.inc("T", self.s_mm, 1)
            self.wait("V", self.s_mm, mmwb)
            for tt in range(4):
                self.em("V", lambda v, tt=tt, sl=sl: v.tensor_copy(
                    self.wb[:, sl * TOWN + tt * 512:sl * TOWN + (tt + 1) * 512], ps[tt][:]))
            wbEv[sl] = self.inc("V", self.s_ve, 1)

        def emit_h_block(e, gg, i):
            n = gg * GH + i
            r = gg % 2
            hreg = self.h_m[r]
            slot, wv = self.w1_load(self.ew1_pk[e % self.emod, n])
            self.wait("T", self.s_w1[slot], wv)
            self.wait("T", self.s_sc, hEv_prev[0])
            if gg == 0 and i == 0:
                self.wait("T", self.s_ve, wbEv[e % 2])
            for k in range(KC):
                for tt in range(4):
                    self.em("T", lambda t, k=k, tt=tt, slot=slot: t.matmul(
                        ps[tt][:], self.w1t(slot, k),
                        self.xnbf[:, k * TOWN + tt * 512:k * TOWN + (tt + 1) * 512],
                        start=(k == 0), stop=(k == KC - 1)))
            mmH = self.inc("T", self.s_mm, 1)
            self.w1_ev.append((self.s_mm, mmH))
            self.wait("S", self.s_mm, mmH)
            if regA_ev[r]:
                self.wait("S", self.s_mm, regA_ev[r])
            for tt in range(4):
                self.em("S", lambda s, tt=tt, i=i, hreg=hreg, e=e, n=n: s.activation(
                    hreg[:, i * TOWN + tt * 512:i * TOWN + (tt + 1) * 512], ps[tt][:],
                    AF.Relu, bias=self.eb1_sb[:, e * FFC + n:e * FFC + n + 1]))
            hEv_prev[0] = self.inc("S", self.s_sc, 1)
            self.wait("V", self.s_sc, hEv_prev[0])
            if self.moe_wb:
                self.wait("V", self.s_ve, wbEv[e % 2])
                for tt in range(4):
                    self.em("V", lambda v, tt=tt, i=i, hreg=hreg, e=e: v.tensor_mul(
                        hreg[:, i * TOWN + tt * 512:i * TOWN + (tt + 1) * 512],
                        hreg[:, i * TOWN + tt * 512:i * TOWN + (tt + 1) * 512],
                        self.wb[:, (e % 2) * TOWN + tt * 512:(e % 2) * TOWN + (tt + 1) * 512]))
            else:
                self.em("V", lambda v: v.memset(self.dum[:, 0:2], 0.0))
            hW_last[(e, gg)] = self.inc("V", self.s_ve, 1)

        def emit_y_sub(e, gg, dout):
            r = gg % 2
            hreg = self.h_m[r]
            slot, wv = self.w2_load(self.ew2_pk[e, gg, dout])
            self.wait("T", self.s_w2[slot], wv)
            self.wait("T", self.s_ve, hW_last[(e, gg)])
            self.wait("T", self.s_ve, yEv_prev[0])
            if gg == NGRP - 1 and dout == 0:
                self.wait("T", self.s_eb[e % 2], ebv_of[e])
            last_plain = (gg != NGRP - 1)
            for j in range(GH):
                for tt in range(4):
                    self.em("T", lambda t, j=j, tt=tt, slot=slot, hreg=hreg, lp=last_plain: t.matmul(
                        ps[4 + tt][:], self.w2t(slot, j),
                        hreg[:, j * TOWN + tt * 512:j * TOWN + (tt + 1) * 512],
                        start=(j == 0), stop=(lp and j == GH - 1)))
            if gg == NGRP - 1:
                for tt in range(4):
                    self.em("T", lambda t, tt=tt, e=e, dout=dout: t.matmul(
                        ps[4 + tt][:],
                        self.eb2cur[0:1, (e % 2) * D + dout * 128:(e % 2) * D + (dout + 1) * 128],
                        self.wtmp[0:1, (e % 2) * TOWN + tt * 512:(e % 2) * TOWN + (tt + 1) * 512],
                        start=False, stop=True))
            mmY = self.inc("T", self.s_mm, 1)
            self.w2_ev.append((self.s_mm, mmY))
            if dout == KC - 1:
                regA_ev[r] = mmY
            if gg == NGRP - 1 and dout == GH - 1:
                wtmpRead[e % 2] = mmY
            self.wait("V", self.s_mm, mmY)
            self.wait("V", self.s_ve, yEv_prev[0])
            for tt in range(4):
                dst = self.acc[:, dout * TOWN + tt * 512:dout * TOWN + (tt + 1) * 512]
                if e == 0 and gg == 0:
                    self.em("V", lambda v, dst=dst, tt=tt: v.tensor_copy(dst, ps[4 + tt][:]))
                else:
                    self.em("V", lambda v, dst=dst, tt=tt: v.tensor_add(dst, dst, ps[4 + tt][:]))
            yEv_prev[0] = self.inc("V", self.s_ve, 1)

        prev = None
        nh = [0]
        for e in range(self.moe_e):
            if self.moe_wb:
                emit_wb(e)
            for gg in range(NGRP):
                for i in range(GH):
                    if nh[0] >= self.hblk:
                        break
                    emit_h_block(e, gg, i)
                    nh[0] += 1
                    if prev is not None and self.moe_y:
                        emit_y_sub(prev[0], prev[1], i)
                if nh[0] >= self.hblk:
                    break
                prev = (e, gg)
            if nh[0] >= self.hblk:
                break
        if self.moe_y:
            for i in range(GH):
                emit_y_sub(prev[0], prev[1], i)

    def emit_b1(self):
        ps = self.ps
        evQpar = [0, 0]
        evQs = [0, 0]
        evDen = [0]
        evZB = [0]
        evZBp = [0, 0]
        evAOZ = [0, 0]
        evAOZ_last = [0]
        aoz_done = [0, 0]
        evX1par = [0, 0]
        xf_prev = [0, 0]
        xf_evac = [0, 0]
        xn_ready = [0, 0]

        def qproj_n(s, n):
            slot, wv = self.w1_load(self.wq_pk[n])
            self.wait("T", self.s_w1[slot], wv)
            self.wait("T", self.s_ve, evQpar[n % 2])
            for k in range(KC):
                for sub in range(2):
                    self.em("T", lambda t, k=k, sub=sub, n=n, s=s, slot=slot: t.matmul(
                        ps[2 * (n % 2) + sub][:], self.w1t(slot, k),
                        self.xnbf[:, k * TOWN + s * 1024 + sub * 512:
                                  k * TOWN + s * 1024 + (sub + 1) * 512],
                        start=(k == 0), stop=(k == KC - 1)))
            mmQ = self.inc("T", self.s_mm, 1)
            self.w1_ev.append((self.s_mm, mmQ))
            for sub in range(2):
                pstile = ps[2 * (n % 2) + sub]
                qsl = self.q_sb[:, n * TOWN + s * 1024 + sub * 512:
                                n * TOWN + s * 1024 + (sub + 1) * 512]
                self.wait("V", self.s_mm, mmQ)
                self.wait("V", self.s_ve, self.s_ve.n)
                self.em("V", lambda v, p=pstile, n=n: v.tensor_scalar(
                    self.elu_r[:], p[:], self.bq_sb[:, n:n + 1], 0.0, op0=OP.add, op1=OP.max))
                vmax = self.vbar()
                self.wait("S", self.s_mm, mmQ)
                self.wait("S", self.s_ve, vmax)
                self.em("S", lambda x, p=pstile, q=qsl, n=n: x.activation(
                    q, p[:], AF.Exp, bias=self.bq_sb[:, n:n + 1]))
                sv = self.inc("S", self.s_sc, 1)
                self.wait("V", self.s_sc, sv)
                self.em("V", lambda v, q=qsl: v.scalar_tensor_tensor(
                    q, q, 1.0, self.elu_r[:], op0=OP.min, op1=OP.add))
                evQpar[n % 2] = self.inc("V", self.s_ve, 1)
            evQs[s] = evQpar[n % 2]

        def mix_sub(s, sub):
            qcol = s * 1024 + sub * 512
            self.wait("T", self.s_ve, evQs[s])
            self.wait("T", self.s_ve, evDen[0])
            if s == 1 and sub == 0:
                self.wait("T", self.s_sc, max(xf_evac))
                self.wait("S", self.s_mm, max(xf_prev))
            for p in range(NPAIR):
                self.em("T", lambda t, p=p, qcol=qcol: t.matmul(
                    ps[4][0:16, :], self.ksbd[:, p * 16:(p + 1) * 16],
                    self.q_sb[:, p * TOWN + qcol:p * TOWN + qcol + 512],
                    start=(p == 0), stop=(p == NPAIR - 1)))
            mmDen = self.inc("T", self.s_mm, 1)
            self.wait("V", self.s_mm, mmDen)
            self.em("V", lambda v: v.tensor_scalar(ps[4][0:16, :], ps[4][0:16, :],
                                                   EPS_ATTN, None, op0=OP.add))
            self.vbar()

            def _recip(v):
                with self.nc.allow_low_precision("z in bf16 (matmul operand)"):
                    return v.reciprocal(self.zbfb[:], ps[4][0:16, :])
            self.em("V", _recip)
            evDen[0] = self.inc("V", self.s_ve, 1)
            for p in range(NPAIR):
                zsl = self.zb[:, (p % 2) * 512:(p % 2 + 1) * 512]
                self.wait("T", self.s_ve, evDen[0])
                self.wait("T", self.s_sc, evZB[0])
                self.em("T", lambda t, p=p: t.matmul(
                    ps[5][:], self.e2_sb[:, p * 128:(p + 1) * 128], self.zbfb[:],
                    start=True, stop=True))
                mmZB = self.inc("T", self.s_mm, 1)
                self.wait("S", self.s_mm, mmZB)
                self.wait("S", self.s_ve, evAOZ[p % 2] if p > 1 else evAOZ_last[0])
                self.em("S", lambda x, zsl=zsl: x.copy(zsl, ps[5][:]))
                evZB[0] = self.inc("S", self.s_sc, 1)
                evZBp[p % 2] = evZB[0]
                self.wait("T", self.s_ve, evAOZ[p % 2])
                self.em("T", lambda t, p=p, qcol=qcol: t.matmul(
                    ps[6 + p % 2][:], self.m_sb[:, p * 128:(p + 1) * 128],
                    self.q_sb[:, p * TOWN + qcol:p * TOWN + qcol + 512],
                    start=True, stop=True))
                mmQM = self.inc("T", self.s_mm, 1)
                self.wait("V", self.s_mm, mmQM)
                self.wait("V", self.s_sc, evZBp[p % 2])
                self.em("V", lambda v, p=p, qcol=qcol, zsl=zsl: v.tensor_mul(
                    self.q_sb[:, p * TOWN + qcol:p * TOWN + qcol + 512],
                    ps[6 + p % 2][:], zsl))
                evAOZ[p % 2] = self.inc("V", self.s_ve, 1)
            evAOZ_last[0] = evAOZ[(NPAIR - 1) % 2]
            aoz_done[s] = max(evAOZ)

        def xnrec(s):
            self.wait("T", self.s_sc, evZB[0])
            self.wait("Y", self.s_ve, aoz_done[s])
            if s == 1:
                self.wait("S", self.s_ve, max(evX1par))
            for j in range(8):
                gown = s * 8 + j
                for fh in range(2):
                    sl = fh
                    xt = self.xt_b1[:]
                    if xf_prev[0] or xf_prev[1]:
                        self.wait("Y", self.s_mm, max(xf_prev))
                    lxv = self.dma("Y", xt,
                                   self.xs[gown * 128:(gown + 1) * 128, fh * 512:(fh + 1) * 512],
                                   self.s_lx[sl])
                    self.wait("V", self.s_lx[sl], lxv)
                    self.em("V", lambda v, xt=xt, gown=gown: v.tensor_scalar(
                        xt, xt, self.mstat[:, gown:gown + 1], self.rstat[:, gown:gown + 1],
                        op0=OP.add, op1=OP.mult))
                    vn = self.inc("V", self.s_ve, 1)
                    self.wait("T", self.s_ve, vn)
                    self.wait("T", self.s_sc, xf_evac[sl])
                    pb = 4 + fh
                    for cc in range(4):
                        self.em("T", lambda t, xt=xt, cc=cc, pb=pb: t.transpose(
                            ps[pb][:, cc * 128:(cc + 1) * 128], xt[:, cc * 128:(cc + 1) * 128],
                            self.ident[:]))
                    xf_prev[sl] = self.inc("T", self.s_mm, 1)
                    self.wait("S", self.s_mm, xf_prev[sl])
                    for cc in range(4):
                        c = fh * 4 + cc
                        self.em("S", lambda x, c=c, j=j, cc=cc, pb=pb: x.copy(
                            self.xnfm[:, c * 1024 + j * 128:c * 1024 + (j + 1) * 128],
                            ps[pb][:, cc * 128:(cc + 1) * 128]))
                    xf_evac[sl] = self.inc("S", self.s_sc, 1)
            xn_ready[s] = self.s_sc.n

        def wo_n(s, n):
            slot, wv = self.w1_load(self.wo_pk[n])
            self.wait("T", self.s_w1[slot], wv)
            self.wait("T", self.s_ve, evX1par[n % 2])
            if n == 0:
                self.wait("T", self.s_ve, aoz_done[s])
            for k in range(KC):
                for sub in range(2):
                    self.em("T", lambda t, k=k, sub=sub, n=n, s=s, slot=slot: t.matmul(
                        ps[2 * (n % 2) + sub][:], self.w1t(slot, k),
                        self.q_sb[:, k * TOWN + s * 1024 + sub * 512:
                                  k * TOWN + s * 1024 + (sub + 1) * 512],
                        start=(k == 0), stop=(k == KC - 1)))
            mmWo = self.inc("T", self.s_mm, 1)
            self.w1_ev.append((self.s_mm, mmWo))
            self.wait("V", self.s_mm, mmWo)
            self.wait("V", self.s_sc, xn_ready[s])
            for sub in range(2):
                self.em("V", lambda v, n=n, s=s, sub=sub: v.scalar_tensor_tensor(
                    self.x1_sb[:, n * TOWN + s * 1024 + sub * 512:
                               n * TOWN + s * 1024 + (sub + 1) * 512],
                    ps[2 * (n % 2) + sub][:], self.bo_sb[:, n:n + 1],
                    self.xnfm[:, n * 1024 + sub * 512:n * 1024 + (sub + 1) * 512],
                    op0=OP.add, op1=OP.add))
            evX1par[n % 2] = self.inc("V", self.s_ve, 1)

        for n in range(KC):
            qproj_n(0, n)
        for n in range(KC):
            qproj_n(1, n)
            if n == 3:
                mix_sub(0, 0)
            if n == 7:
                mix_sub(0, 1)
        xnrec(0)
        for n in range(KC):
            wo_n(0, n)
            if n == 3:
                mix_sub(1, 0)
            if n == 7:
                mix_sub(1, 1)
        xnrec(1)
        for n in range(KC):
            wo_n(1, n)

    def emit_ln(self, which):
        ps = self.ps
        g_sb = self.g1_sb if which == 1 else self.g2_sb
        be_sb = self.be1_sb if which == 1 else self.be2_sb
        evMMpar = [0, 0]
        evRowEvac = 0
        evApply = {}
        evBF = [0, 0]
        for tt in range(4):
            for c in range(KC):
                sl = c % 2
                xsl = self.x1_sb[:, c * TOWN + tt * 512:c * TOWN + (tt + 1) * 512]
                b1 = self.ln_bf[:, sl * 512:(sl + 1) * 512]
                b2 = self.ln_bf[:, (2 + sl) * 512:(3 + sl) * 512]
                self.wait("S", self.s_mm, evMMpar[sl])
                self.em("S", lambda s, b1=b1, x=xsl: s.copy(b1, x))
                self.chain("S", self.s_sc)
                self.em("S", lambda s, b2=b2, x=xsl: s.activation(b2, x, AF.Square))
                vb = self.inc("S", self.s_sc, 1)
                self.wait("T", self.s_sc, vb)
                if c == 0:
                    self.wait("T", self.s_ve, evRowEvac)
                self.em("T", lambda t, b1=b1, c=c: t.matmul(
                    ps[0][0:1, :], self.onescol_bf[:], b1, start=(c == 0), stop=(c == KC - 1)))
                self.em("T", lambda t, b2=b2, c=c: t.matmul(
                    ps[1][0:1, :], self.onescol_bf[:], b2, start=(c == 0), stop=(c == KC - 1)))
                evMMpar[sl] = self.inc("T", self.s_mm, 1)
            mmStat = evMMpar[(KC - 1) % 2]
            mrow = self.ln_m[0:1, tt * 512:(tt + 1) * 512]
            rrow = self.ln_r[0:1, tt * 512:(tt + 1) * 512]
            self.wait("V", self.s_mm, mmStat)
            self.em("V", lambda v, m=mrow: v.tensor_scalar_mul(m, ps[0][0:1, :], INV_D))
            self.em("V", lambda v: v.tensor_scalar(self.ln_s1[:], ps[1][0:1, :], INV_D, EPS_LN,
                                                   op0=OP.mult, op1=OP.add))
            self.vbar()
            self.em("V", lambda v, m=mrow: v.tensor_mul(self.ln_s2[:], m, m))
            self.vbar()
            self.em("V", lambda v: v.tensor_sub(self.ln_s1[:], self.ln_s1[:], self.ln_s2[:]))
            evRowEvac = self.inc("V", self.s_ve, 1)
            self.wait("S", self.s_ve, evRowEvac)
            self.em("S", lambda s: s.activation(self.ln_s2[:], self.ln_s1[:], AF.Ln))
            self.chain("S", self.s_sc)
            self.em("S", lambda s: s.activation(self.ln_s2[:], self.ln_s2[:], AF.Exp, scale=-0.5))
            sv = self.inc("S", self.s_sc, 1)
            self.wait("V", self.s_sc, sv)
            self.em("V", lambda v: v.tensor_mul(self.ln_s3[:], self.ln_s2[:], self.ln_s2[:]))
            self.vbar()
            self.em("V", lambda v: v.tensor_mul(self.ln_s3[:], self.ln_s3[:], self.ln_s1[:]))
            self.vbar()
            self.em("V", lambda v: v.tensor_scalar(self.ln_s3[:], self.ln_s3[:], -0.5, 1.5,
                                                   op0=OP.mult, op1=OP.add))
            self.vbar()
            self.em("V", lambda v, r=rrow: v.tensor_mul(r, self.ln_s3[:], self.ln_s2[:]))
            vr = self.inc("V", self.s_ve, 1)
            self.wait("T", self.s_ve, vr)
            self.wait("T", self.s_ve, evApply.get((tt - 1, KC - 1), 0))
            self.em("T", lambda t, m=mrow: t.matmul(ps[2][:], self.onesrow_f[:], m,
                                                    start=True, stop=True))
            self.em("T", lambda t, r=rrow: t.matmul(ps[3][:], self.onesrow_f[:], r,
                                                    start=True, stop=True))
            mmB = self.inc("T", self.s_mm, 1)
            self.wait("V", self.s_mm, mmB)
            stages = []
            for c in range(KC):
                lt = lambda c: self.ln_t[:, (c % 2) * 512:(c % 2 + 1) * 512]
                xs = self.x1_sb[:, c * TOWN + tt * 512:c * TOWN + (tt + 1) * 512]
                stages.append(("fin", c - 2))
                stages.append(("mul", c - 1))
                stages.append(("sub", c))
            stages += [("fin", KC - 2), ("mul", KC - 1), ("fin", KC - 1)]
            for step, c in stages:
                if c < 0:
                    continue
                sl = c % 2
                lts = self.ln_t[:, sl * 512:(sl + 1) * 512]
                xsl = self.x1_sb[:, c * TOWN + tt * 512:c * TOWN + (tt + 1) * 512]
                if step == "sub":
                    prev = (tt, c - 2) if c >= 2 else (tt - 1, KC - 2 + c)
                    self.wait("V", self.s_ve, evApply.get(prev, 0))
                    self.em("V", lambda v, x=xsl, lts=lts: v.tensor_sub(lts, x, ps[2][:]))
                    self.inc("V", self.s_ve, 1)
                elif step == "mul":
                    self.wait("V", self.s_ve, self.s_ve.n)
                    self.em("V", lambda v, lts=lts: v.tensor_mul(lts, lts, ps[3][:]))
                    self.inc("V", self.s_ve, 1)
                else:
                    self.wait("V", self.s_ve, self.s_ve.n)
                    if which == 1:
                        self.em("V", lambda v, x=xsl, lts=lts, c=c: v.tensor_scalar(
                            x, lts, g_sb[:, c:c + 1], be_sb[:, c:c + 1],
                            op0=OP.mult, op1=OP.add))
                        vx = self.inc("V", self.s_ve, 1)
                        csl = self.c_bf[:, c * TOWN + tt * 512:c * TOWN + (tt + 1) * 512]
                        self.wait("S", self.s_ve, vx)
                        self.em("S", lambda s, csl=csl, x=xsl: s.copy(csl, x))
                        self.chain("S", self.s_sc)
                        evApply[(tt, c)] = vx
                    else:
                        self.em("V", lambda v, lts=lts, c=c: v.tensor_scalar(
                            lts, lts, g_sb[:, c:c + 1], be_sb[:, c:c + 1],
                            op0=OP.mult, op1=OP.add))
                        self.vbar()
                        asl = self.acc[:, c * TOWN + tt * 512:c * TOWN + (tt + 1) * 512]
                        self.em("V", lambda v, a=asl, lts=lts: v.tensor_add(a, a, lts))
                        evApply[(tt, c)] = self.inc("V", self.s_ve, 1)

    def emit_b2(self):
        ps = self.ps
        hEv_prev = 0
        yEv_prev = 0
        regEv = 0
        for gg in range(NGRP):
            for i in range(GH):
                n = gg * GH + i
                slot, wv = self.w1_load(self.w1_pk[n])
                self.wait("T", self.s_w1[slot], wv)
                self.wait("T", self.s_sc, hEv_prev)
                for k in range(KC):
                    for tt in range(4):
                        self.em("T", lambda t, k=k, tt=tt, slot=slot: t.matmul(
                            ps[tt][:], self.w1t(slot, k),
                            self.c_bf[:, k * TOWN + tt * 512:k * TOWN + (tt + 1) * 512],
                            start=(k == 0), stop=(k == KC - 1)))
                mmH = self.inc("T", self.s_mm, 1)
                self.w1_ev.append((self.s_mm, mmH))
                self.wait("S", self.s_mm, mmH)
                if regEv:
                    self.wait("S", self.s_mm, regEv)
                for tt in range(4):
                    self.em("S", lambda s, tt=tt, i=i, n=n: s.activation(
                        self.h_b2[:, i * TOWN + tt * 512:i * TOWN + (tt + 1) * 512],
                        ps[tt][:], AF.Relu, bias=self.b1_sb[:, n:n + 1]))
                hEv_prev = self.inc("S", self.s_sc, 1)
            for dout in range(KC):
                slot, wv = self.w2_load(self.w2_pk[gg, dout])
                self.wait("T", self.s_w2[slot], wv)
                self.wait("T", self.s_sc, hEv_prev)
                self.wait("T", self.s_ve, yEv_prev)
                for j in range(GH):
                    for tt in range(4):
                        self.em("T", lambda t, j=j, tt=tt, slot=slot: t.matmul(
                            ps[4 + tt][:], self.w2t(slot, j),
                            self.h_b2[:, j * TOWN + tt * 512:j * TOWN + (tt + 1) * 512],
                            start=(j == 0), stop=(j == GH - 1)))
                mmY = self.inc("T", self.s_mm, 1)
                self.w2_ev.append((self.s_mm, mmY))
                if dout == KC - 1:
                    regEv = mmY
                self.wait("V", self.s_mm, mmY)
                self.wait("V", self.s_ve, yEv_prev)
                for tt in range(4):
                    xsl = self.x1_sb[:, dout * TOWN + tt * 512:dout * TOWN + (tt + 1) * 512]
                    if gg == 0:
                        self.em("V", lambda v, x=xsl, tt=tt, dout=dout: v.scalar_tensor_tensor(
                            x, ps[4 + tt][:], self.b2a_sb[:, dout:dout + 1], x,
                            op0=OP.add, op1=OP.add))
                    else:
                        self.em("V", lambda v, x=xsl, tt=tt: v.tensor_add(x, x, ps[4 + tt][:]))
                yEv_prev = self.inc("V", self.s_ve, 1)

    def emit_b4(self):
        ps = self.ps
        evAdd = {}
        for j in range(NGOWN):
            par = j % 2
            xt = self.xtm_b4[:, par * D:(par + 1) * D]
            ot = self.otm_b4[:, par * D:(par + 1) * D]
            if j >= 2:
                self.wait("Y", self.s_ve, evAdd[j - 2])
            lxv = self.dma("Y", xt, self.xs[j * 128:(j + 1) * 128, :], self.s_lx[par])
            pbase = 0 if par == 0 else 2
            if j >= 2:
                self.wait("V", self.s_st[par], (j // 2) * 16)
            self.wait("T", self.s_ve, evAdd.get(j - 2, 0))
            for c in range(KC):
                self.em("T", lambda t, c=c, j=j, pbase=pbase: t.transpose(
                    ps[pbase + c // 4][:, (c % 4) * 128:(c % 4 + 1) * 128],
                    self.acc[:, c * TOWN + j * 128:c * TOWN + (j + 1) * 128], self.ident[:]))
            mmT = self.inc("T", self.s_mm, 1)
            self.wait("V", self.s_mm, mmT)
            self.wait("V", self.s_lx[par], lxv)
            for hh in range(2):
                self.em("V", lambda v, ot=ot, xt=xt, hh=hh, pbase=pbase: v.tensor_add(
                    ot[:, hh * 512:(hh + 1) * 512], xt[:, hh * 512:(hh + 1) * 512],
                    ps[pbase + hh][:]))
            evAdd[j] = self.inc("V", self.s_ve, 1)
            self.wait("Y", self.s_ve, evAdd[j])
            self.dma("Y", self.out[j * 128:(j + 1) * 128, :], ot, self.s_st[par])
        self.wait("Y", self.s_st[0], self.s_st[0].n)
        self.wait("Y", self.s_st[1], self.s_st[1].n)
        if self.s_sd.n:
            self.wait("Y", self.s_sd, self.s_sd.n)
        if self.s_dbg.n:
            self.wait("Y", self.s_dbg, self.s_dbg.n)

    def build(self, phases=("A", "MOE", "B1", "LN1", "B2", "LN2", "B4")):
        self.emit_aux()
        if "A0" in phases:
            self.emit_phase_a()
            self.barrier()
        if "A" in phases:
            self.emit_phase_a()
            self.emit_phase_a_tail()
            self.barrier()
            self._dump("xnbf", self.xnbf, cast=True)
            self._dump("msb", self.m_sb, cast=True)
            self._dump("ksbd", self.ksbd, cast=True)
            self._dump("wrow", self.wrow_f)
            self._dump("hall", self.hall)
            self._dump("mstat", self.mstat)
            self._dump("rstat", self.rstat)
            self._dump("ktm", self.ktm, cast=True)
        if "MOE" in phases:
            self.emit_moe()
            self.barrier()
            self._dump("accm", self.acc)
        if "B1" in phases:
            self.emit_b1()
            self.barrier()
            self._dump("x1_pre", self.x1_sb)
            self._dump("qsb", self.q_sb, cast=True)
        if "LN1" in phases:
            self.emit_ln(1)
            self.barrier()
            self._dump("cbf", self.c_bf, cast=True)
        if "B2" in phases:
            self.emit_b2()
            self.barrier()
            self._dump("x1n", self.x1_sb)
        if "LN2" in phases:
            self.emit_ln(2)
            self.barrier()
            self._dump("accf", self.acc)
        if "B4" in phases:
            self.emit_b4()
        else:
            self.barrier()
            fb = self.acc[:, 0:D] if "MOE" in phases else self.kvacc[:]
            self.dma("Y", self.out[0:128, :], fb, self.s_st[0])
            self.wait("Y", self.s_st[0], self.s_st[0].n)
            if self.s_sd.n:
                self.wait("Y", self.s_sd, self.s_sd.n)
            if self.s_dbg.n:
                self.wait("Y", self.s_dbg, self.s_dbg.n)
        nc = self.nc
        with nc.Block() as block:
            @block.tensor
            def _(t):
                for fn in self.L["T"]:
                    fn(t)

            @block.vector
            def _(v):
                for fn in self.L["V"]:
                    fn(v)

            @block.scalar
            def _(s):
                for fn in self.L["S"]:
                    fn(s)

            @block.gpsimd
            def _(g):
                for fn in self.L["G"]:
                    fn(g)

            @block.sync
            def _(y):
                for fn in self.L["Y"]:
                    fn(y)
        return nc


def host_pack(inputs):
    f32 = lambda a: np.ascontiguousarray(np.asarray(a, np.float32))
    shared = {
        "wq_pk": pack_w1(f32(inputs["L0_attn_Wq"])),
        "wo_pk": pack_w1(f32(inputs["L0_attn_Wo"])),
        "wk_nat": f32(inputs["L0_attn_Wk"]),
        "wv_nat": f32(inputs["L0_attn_Wv"]),
        "w1_pk": pack_w1(f32(inputs["L0_attn_W1"])),
        "w2_pk": pack_w2(f32(inputs["L0_attn_W2"])),
        "ew1_pk": np.stack([pack_w1(f32(inputs["L0_ffn_W1"][e])) for e in range(NEXP)]),
        "ew2_pk": np.stack([pack_w2(f32(inputs["L0_ffn_W2"][e])) for e in range(NEXP)]),
        "bq_pp": pp(f32(inputs["L0_attn_bq"])),
        "bo_pp": pp(f32(inputs["L0_attn_bo"])),
        "g1_pp": pp(f32(inputs["L0_attn_g1"])),
        "be1_pp": pp(f32(inputs["L0_attn_beta1"])),
        "g2_pp": pp(f32(inputs["L0_attn_g2"])),
        "be2_pp": pp(f32(inputs["L0_attn_beta2"])),
        "b2a_pp": pp(f32(inputs["L0_attn_b2"])),
        "b1_pp": pp(f32(inputs["L0_attn_b1"])),
        "eb1_pp": np.concatenate([pp(f32(inputs["L0_ffn_b1"][e])) for e in range(NEXP)], axis=1),
        "bk_row": f32(inputs["L0_attn_bk"]).reshape(1, D),
        "bv_row": f32(inputs["L0_attn_bv"]).reshape(1, D),
        "eb2_rows": f32(inputs["L0_ffn_b2"]).reshape(NEXP, D),
        "ident_c": np.eye(128, dtype=np.float32),
        "onescol_c": np.ones((128, 1), np.float32),
        "onesrow_c": np.ones((1, 128), np.float32),
        "e2_c": make_e2(),
        "ecol_c": np.arange(NEXP, dtype=np.float32).reshape(NEXP, 1),
        "mkc_c": np.tile(np.array([[1.0, -7.0]], np.float32), (NEXP, 1)),
        "nwc_c": np.tile(np.array([[-0.5, 1.5]], np.float32), (128, 1)),
    }
    return {k: np.ascontiguousarray(v) for k, v in shared.items()}


def make_in_maps(inputs):
    x = np.ascontiguousarray(np.asarray(inputs["L0_attn_x"], np.float32))
    shared = host_pack(inputs)
    in_maps = []
    for c in range(8):
        seq, half = c // 2, c % 2
        own = x[seq, half * TOWN:(half + 1) * TOWN]
        other = x[seq, (1 - half) * TOWN:(2 - half) * TOWN]
        m = dict(shared)
        m["xs"] = np.ascontiguousarray(np.concatenate([own, other], axis=0))
        in_maps.append(m)
    return in_maps


def kernel(**inputs):
    nc = KB().build()
    in_maps = make_in_maps(inputs)
    res = run_bass_kernel_spmd(nc, in_maps, core_ids=list(range(8)))
    out = np.empty((4, 4096, 1024), np.float32)
    for c in range(8):
        seq, half = c // 2, c % 2
        out[seq, half * TOWN:(half + 1) * TOWN] = res.results[c]["out"]
    return out


# version 10
# speedup vs baseline: 1.0382x; 1.0201x over previous
import numpy as np
import ml_dtypes
import concourse.bass as bass
import concourse.mybir as mybir
from concourse.bass_utils import run_bass_kernel_spmd

F32 = mybir.dt.float32
BF16 = mybir.dt.bfloat16
I32 = mybir.dt.int32
AF = mybir.ActivationFunctionType
OP = mybir.AluOpType
AXX = mybir.AxisListType.X

D = 1024
KC = 8
NPAIR = 8
L = 4096
TOWN = 2048
NG = 32
NGOWN = 16
FF = 4096
FFC = 32
NEXP = 8
GH = 8
NGRP = FFC // GH
EPS_LN = 1e-5
EPS_ATTN = 1e-6
INV_D = 1.0 / D

DMA_SCRATCH = 16384
SBUF_TOP = 229344


def pack_w1(W):
    din, dout = W.shape
    return np.ascontiguousarray(
        W.reshape(din // 128, 128, dout // 128, 128).transpose(2, 1, 0, 3)
        .reshape(dout // 128, 128, din))


def pack_w2(W, G=GH):
    din, dout = W.shape
    ngb = din // (G * 128)
    return np.ascontiguousarray(
        W.reshape(ngb, G, 128, dout // 128, 128).transpose(0, 3, 2, 1, 4)
        .reshape(ngb, dout // 128, 128, G * 128))


def pp(vec):
    n = vec.shape[0] // 128
    return np.ascontiguousarray(vec.reshape(n, 128).T)


def make_e2():
    e2 = np.zeros((16, NPAIR * 128), np.float32)
    for p in range(NPAIR):
        e2[2 * p, p * 128:p * 128 + 64] = 1.0
        e2[2 * p + 1, p * 128 + 64:p * 128 + 128] = 1.0
    return e2


class Sem:
    def __init__(self, h, name):
        self.h = h
        self.name = name
        self.n = 0


class KB:
    def __init__(self, debug=(), ngl=NG, alim=6, moe_e=NEXP, moe_y=True, gser=False, moe_wb=True, emod=NEXP, hblk=10000):
        self.debug = set(debug)
        self.ngl = ngl
        self.alim = alim
        self.moe_e = moe_e
        self.moe_y = moe_y
        self.gser = gser
        self.moe_wb = moe_wb
        self.emod = emod
        self.hblk = hblk
        nc = bass.Bass(trn_type="TRN2", target_bir_lowering=False,
                       dynamic_dma_scratch_size=DMA_SCRATCH)
        self.nc = nc
        self.base = DMA_SCRATCH + 32
        self.L = {k: [] for k in "TVSGY"}
        self.meta = {k: [] for k in "TVSGY"}
        self._decl_dram()
        self._decl_sbuf()
        self.ps = [nc.alloc_psum_tensor(f"ps{i}", [128, 512], F32) for i in range(8)]
        mk = lambda n: Sem(nc.alloc_semaphore(n), n)
        self.s_lx = [mk("lx0"), mk("lx1")]
        self.s_w1 = [mk("w1a"), mk("w1b")]
        self.s_w2 = [mk("w2a"), mk("w2b")]
        self.s_aux = mk("aux")
        self.s_sd = mk("sd")
        self.s_dbg = mk("dbg")
        self.s_eb = [mk("eb0"), mk("eb1")]
        self.s_mm = mk("mm")
        self.s_sc = mk("sc")
        self.s_ve = mk("ve")
        self.s_st = [mk("st0"), mk("st1")]
        self.s_bar = mk("bar")
        self.w1_idx = 0
        self.w1_ev = []
        self.w2_idx = 0
        self.w2_ev = []

    def _decl_dram(self):
        nc = self.nc
        dt = lambda n, s: nc.dram_tensor(n, s, F32, kind="ExternalInput").ap()
        self.xs = dt("xs", [L, D])
        self.wq_pk = dt("wq_pk", [KC, 128, D])
        self.wo_pk = dt("wo_pk", [KC, 128, D])
        self.wk_nat = dt("wk_nat", [D, D])
        self.wv_nat = dt("wv_nat", [D, D])
        self.w1_pk = dt("w1_pk", [FFC, 128, D])
        self.w2_pk = dt("w2_pk", [NGRP, KC, 128, GH * 128])
        self.ew1_pk = dt("ew1_pk", [NEXP, FFC, 128, D])
        self.ew2_pk = dt("ew2_pk", [NEXP, NGRP, KC, 128, GH * 128])
        self.bq_pp = dt("bq_pp", [128, KC])
        self.bo_pp = dt("bo_pp", [128, KC])
        self.g1_pp = dt("g1_pp", [128, KC])
        self.be1_pp = dt("be1_pp", [128, KC])
        self.g2_pp = dt("g2_pp", [128, KC])
        self.be2_pp = dt("be2_pp", [128, KC])
        self.b2a_pp = dt("b2a_pp", [128, KC])
        self.b1_pp = dt("b1_pp", [128, FFC])
        self.eb1_pp = dt("eb1_pp", [128, NEXP * FFC])
        self.bk_row = dt("bk_row", [1, D])
        self.bv_row = dt("bv_row", [1, D])
        self.eb2_rows = dt("eb2_rows", [NEXP, D])
        self.ident_c = dt("ident_c", [128, 128])
        self.onescol_c = dt("onescol_c", [128, 1])
        self.onesrow_c = dt("onesrow_c", [1, 128])
        self.e2_c = dt("e2_c", [16, NPAIR * 128])
        self.mkc_c = dt("mkc_c", [NEXP, 2])
        self.nwc_c = dt("nwc_c", [128, 2])
        self.ecol_c = dt("ecol_c", [NEXP, 1])
        self.out = nc.dram_tensor("out", [TOWN, D], F32, kind="ExternalOutput").ap()
        self.dbg = {}
        for name, shape in (("xnbf", [128, KC * TOWN]), ("msb", [128, NPAIR * 128]),
                            ("ksbd", [128, NPAIR * 16]), ("wrow", [NEXP, TOWN]),
                            ("accm", [128, KC * TOWN]), ("x1n", [128, KC * TOWN]),
                            ("accf", [128, KC * TOWN]), ("hall", [128, NGOWN]),
                            ("ktm", [128, 2 * D]), ("mstat", [128, NGOWN]),
                            ("rstat", [128, NGOWN]), ("x1_pre", [128, KC * TOWN]),
                            ("qsb", [128, KC * TOWN]), ("cbf", [128, KC * TOWN])):
            if name in self.debug:
                self.dbg[name] = nc.dram_tensor(
                    "dbg_" + name, shape, F32, kind="ExternalOutput").ap()

    def _sb(self, name, shape, dtype, off, region_end=None):
        per = int(np.prod(shape[1:])) * mybir.dt.size(dtype)
        off = self.base + off
        assert off % 32 == 0, (name, off)
        assert off + per <= SBUF_TOP, (name, off, per, off + per)
        if region_end is not None:
            assert off + per <= self.base + region_end, (name, off + per, region_end)
        return self.nc.alloc_sbuf_tensor_at(name, list(shape), dtype, offset=off)

    def _decl_sbuf(self):
        sb = self._sb
        self.wk_sb = sb("wk_sb", [128, KC * D], BF16, 0, 65536)
        self.wv_sb = sb("wv_sb", [128, KC * D], BF16, 16384, 65536)
        self.kvbias = sb("kvbias", [1, 2 * D], BF16, 32768, 65536)
        self.xnp = sb("xnp", [128, 2 * KC * 128], BF16, 36864, 65536)
        self.stat48 = sb("stat48", [48, 128], F32, 40960, 65536)
        self.kvacc = sb("kvacc", [128, NPAIR * 128], F32, 45056, 65536)
        self.ksacc = sb("ksacc", [128, NPAIR], F32, 49600, 65536)
        self.h_m = [sb("h_lo", [128, GH * TOWN], BF16, 0, 65536),
                    sb("h_hi", [128, GH * TOWN], BF16, 32768, 65536)]
        self.x1_sb = sb("x1_sb", [128, KC * TOWN], F32, 0, 65536)
        self.acc = sb("acc", [128, KC * TOWN], F32, 65536, 131072)
        XB = 131072
        self.xnbf = sb("xnbf", [128, KC * TOWN], BF16, XB, 163840)
        self.xnfm = sb("xnfm", [128, KC * 1024], F32, XB, 163840)
        self.c_bf = sb("c_bf", [128, KC * TOWN], BF16, XB, 163840)
        TR = 163840
        self.xtm_a = sb("xtm_a", [128, 2 * D], F32, TR, 196608)
        self.ktm = sb("ktm", [128, 2 * D], BF16, TR + 8192, 196608)
        self.vtm = sb("vtm", [128, 2 * D], BF16, TR + 12288, 196608)
        self.elu_a = sb("elu_a", [128, 2 * D], F32, TR + 16384, 196608)
        self.hrow = sb("hrow", [1, TOWN], F32, TR, 196608)
        self.h8 = sb("h8", [NEXP, TOWN], F32, TR + 8192, 196608)
        self.wscr = sb("wscr", [NEXP, TOWN], F32, TR + 16384, 196608)
        self.wrow_f = sb("wrow_f", [NEXP, TOWN], F32, TR + 24576, 196608)
        self.wb = sb("wb", [128, 2 * TOWN], BF16, TR, 196608)
        self.wrow_bf = sb("wrow_bf", [NEXP, TOWN], BF16, TR + 8192, 196608)
        self.wtmp = sb("wtmp", [1, 2 * TOWN], BF16, TR + 12288, 196608)
        self.eb2cur = sb("eb2cur", [1, 2 * D], BF16, TR + 20480, 196608)
        self.q_sb = sb("q_sb", [128, KC * TOWN], BF16, TR, 196608)
        self.ln_m = sb("ln_m", [1, TOWN], F32, TR, 196608)
        self.ln_r = sb("ln_r", [1, TOWN], F32, TR + 8192, 196608)
        self.ln_bf = sb("ln_bf", [128, 4 * 512], BF16, TR + 16384, 196608)
        self.ln_t = sb("ln_t", [128, 2 * 512], F32, TR + 20480, 196608)
        self.ln_s1 = sb("ln_s1", [1, 512], F32, TR + 24576, 196608)
        self.ln_s2 = sb("ln_s2", [1, 512], F32, TR + 26624, 196608)
        self.ln_s3 = sb("ln_s3", [1, 512], F32, TR + 28672, 196608)
        self.h_b2 = sb("h_b2", [128, GH * TOWN], BF16, TR, 196608)
        self.xtm_b4 = sb("xtm_b4", [128, 4 * D], F32, TR, 196608)
        self.otm_b4 = sb("otm_b4", [128, 4 * D], F32, TR + 16384, 196608)
        ST = 196608
        self.w1s = sb("w1s", [128, 2 * KC * 128], BF16, ST, 204800)
        self.w2s = sb("w2s", [128, 2 * GH * 128], BF16, ST + 4096, 204800)
        self.zbfb = sb("zbfb", [16, 512], BF16, ST + 4096, 204800)
        self.zb = sb("zb", [128, 2 * 512], BF16, ST + 5120, 204800)
        self.xt_b1 = sb("xt_b1", [128, 512], F32, ST + 6144, 204800)
        SM = 204800
        self.ident = sb("ident", [128, 128], F32, SM)
        self.onescol_bf = sb("onescol_bf", [128, 1], BF16, SM + 512)
        self.onesrow_bf = sb("onesrow_bf", [1, 128], BF16, SM + 544)
        self.onesrow_f = sb("onesrow_f", [1, 128], F32, SM + 800)
        self.bq_sb = sb("bq_sb", [128, KC], F32, SM + 1312)
        self.bo_sb = sb("bo_sb", [128, KC], F32, SM + 1344)
        self.g1_sb = sb("g1_sb", [128, KC], F32, SM + 1376)
        self.be1_sb = sb("be1_sb", [128, KC], F32, SM + 1408)
        self.g2_sb = sb("g2_sb", [128, KC], F32, SM + 1440)
        self.be2_sb = sb("be2_sb", [128, KC], F32, SM + 1472)
        self.b2a_sb = sb("b2a_sb", [128, KC], F32, SM + 1504)
        self.b1_sb = sb("b1_sb", [128, FFC], BF16, SM + 1536)
        self.eb1_sb = sb("eb1_sb", [128, NEXP * FFC], BF16, SM + 1600)
        self.mstat = sb("mstat", [128, NGOWN], F32, SM + 2112)
        self.rstat = sb("rstat", [128, NGOWN], F32, SM + 2176)
        self.ksbd = sb("ksbd", [128, NPAIR * 16], BF16, SM + 2240)
        self.m_sb = sb("m_sb", [128, NPAIR * 128], BF16, SM + 2496)
        self.e2_sb = sb("e2_sb", [16, NPAIR * 128], BF16, SM + 4544)
        self.dum = sb("dum", [128, 8], F32, SM + 6624)
        SZ = SM + 6656
        self.elu_r = sb("elu_r", [128, 512], BF16, SZ)
        self.tpack = sb("tpack", [128, 128], F32, SZ)
        self.xn01 = sb("xn01", [128, 2 * NGOWN], F32, SZ + 512)
        self.hall = sb("hall", [128, NGOWN], F32, SZ + 704)
        self.scol = sb("scol", [128, 4], F32, SZ + 768)
        self.ecol_sb = sb("ecol_sb", [NEXP, 1], F32, SZ + 800)
        self.mkc_sb = sb("mkc_sb", [NEXP, 2], F32, SZ + 832)
        self.nwc_sb = sb("nwc_sb", [128, 2], F32, SM + 6592)
        assert SZ + 1024 <= SBUF_TOP - self.base, (SZ + 1024, SBUF_TOP - self.base)

    def em(self, e, fn):
        self.L[e].append(fn)
        self.meta[e].append(("op", None, 0))

    def wait(self, e, sem, val):
        if val > 0:
            self.L[e].append(lambda x, s=sem.h, v=val: x.wait_ge(s, v))
            self.meta[e].append(("wait", sem.name, val))

    def inc(self, e, sem, n=1):
        prev = self.L[e].pop()
        sem.n += n
        def fn(x, prev=prev, s=sem.h, n=n):
            prev(x).then_inc(s, n)
        self.L[e].append(fn)
        self.meta[e][-1] = ("op", sem.name, n)
        return sem.n

    def static_check(self):
        pos = {e: 0 for e in "TVSGY"}
        sems = {}
        while True:
            progress = False
            done = True
            for e in "TVSGY":
                while pos[e] < len(self.meta[e]):
                    kind, sem, val = self.meta[e][pos[e]]
                    if kind == "wait" and sems.get(sem, 0) < val:
                        break
                    if kind == "op" and sem is not None:
                        sems[sem] = sems.get(sem, 0) + val
                    pos[e] += 1
                    progress = True
                if pos[e] < len(self.meta[e]):
                    done = False
            if done:
                return
            if not progress:
                msg = []
                for e in "TVSGY":
                    if pos[e] < len(self.meta[e]):
                        k, sem, val = self.meta[e][pos[e]]
                        msg.append(f"{e}@{pos[e]}: wait {sem}>={val} (cur {sems.get(sem, 0)})")
                raise RuntimeError("STATIC DEADLOCK: " + "; ".join(msg))

    def chain(self, e, sem):
        v = self.inc(e, sem, 1)
        self.wait(e, sem, v)
        return v

    def vbar(self):
        return self.chain("V", self.s_ve)

    def dma(self, e, out, in_, sem):
        self.em(e, lambda x, o=out, i=in_: x.dma_start(out=o, in_=i))
        return self.inc(e, sem, 16)

    def barrier(self):
        self.em("T", lambda x: x.nop())
        self.inc("T", self.s_bar, 1)
        self.em("V", lambda x: x.memset(self.dum[:, 0:2], 0.0))
        self.inc("V", self.s_bar, 1)
        self.em("S", lambda x: x.memzero(self.dum[:, 2:4]))
        self.inc("S", self.s_bar, 1)
        self.em("G", lambda x: x.memset(self.dum[:, 4:6], 0.0))
        self.inc("G", self.s_bar, 1)
        tgt = self.s_bar.n
        for e in "TVSGY":
            self.wait(e, self.s_bar, tgt)

    def w1_load(self, src):
        i = self.w1_idx
        self.w1_idx += 1
        slot = i % 2
        if i >= 2:
            sem, val = self.w1_ev[i - 2]
            self.wait("G", sem, val)
        if self.gser:
            self.wait("G", self.s_w1[slot], self.s_w1[slot].n)
        v = self.dma("G", self.w1s[:, slot * 1024:(slot + 1) * 1024], src, self.s_w1[slot])
        return slot, v

    def w1t(self, slot, k):
        return self.w1s[:, slot * 1024 + k * 128: slot * 1024 + (k + 1) * 128]

    def w2_load(self, src):
        i = self.w2_idx
        self.w2_idx += 1
        slot = i % 2
        if i >= 2:
            sem, val = self.w2_ev[i - 2]
            self.wait("G", sem, val)
        v = self.dma("G", self.w2s[:, slot * 1024:(slot + 1) * 1024], src, self.s_w2[slot])
        return slot, v

    def w2t(self, slot, j):
        return self.w2s[:, slot * 1024 + j * 128: slot * 1024 + (j + 1) * 128]

    def _dump(self, name, src, cast=False):
        if name not in self.dbg:
            return
        if cast:
            sem = self.s_dbg
            self.dma("G", self.dbg[name], src[:], sem)
        else:
            sem = self.s_sd
            self.wait("Y", sem, sem.n)
            self.dma("Y", self.dbg[name], src[:], sem)
        for e in "TVSG":
            self.wait(e, sem, sem.n)

    def emit_aux(self):
        d = self.dma
        for k in range(KC):
            d("G", self.wk_sb[:, k * D:(k + 1) * D],
              self.wk_nat[k * 128:(k + 1) * 128, :], self.s_aux)
            d("G", self.wv_sb[:, k * D:(k + 1) * D],
              self.wv_nat[k * 128:(k + 1) * 128, :], self.s_aux)
        d("G", self.kvbias[0:1, 0:D], self.bk_row, self.s_aux)
        d("G", self.kvbias[0:1, D:2 * D], self.bv_row, self.s_aux)
        for dst, src in ((self.ident, self.ident_c), (self.onescol_bf, self.onescol_c),
                         (self.onesrow_bf, self.onesrow_c), (self.onesrow_f, self.onesrow_c),
                         (self.bq_sb, self.bq_pp), (self.bo_sb, self.bo_pp),
                         (self.g1_sb, self.g1_pp), (self.be1_sb, self.be1_pp),
                         (self.g2_sb, self.g2_pp), (self.be2_sb, self.be2_pp),
                         (self.b2a_sb, self.b2a_pp), (self.b1_sb, self.b1_pp),
                         (self.eb1_sb, self.eb1_pp), (self.ecol_sb, self.ecol_c),
                         (self.e2_sb, self.e2_c), (self.mkc_sb, self.mkc_c),
                         (self.nwc_sb, self.nwc_c)):
            d("G", dst[:], src, self.s_aux)
        self.aux_tgt = self.s_aux.n
        for e in "TVS":
            self.wait(e, self.s_aux, self.aux_tgt)
        self.em("V", lambda v: v.memset(self.m_sb[:], 0.0))
        self.em("V", lambda v: v.memset(self.ksbd[:], 0.0))
        self.inc("V", self.s_ve, 1)

    def emit_phase_a(self):
        ps = self.ps
        NGL = self.ngl
        evT = {}; evKV = {}; evKVmm = {}; evA = {}; evXN = {}
        evK_s = {}; evK_v = {}; evV_v = {}; evDMA = {}
        evXNw = 0

        def lhs(c, g):
            own = g < NGOWN
            if own:
                return self.xnbf[:, c * TOWN + g * 128:c * TOWN + g * 128 + 128]
            return self.xnp[:, ((g % 2) * KC + c) * 128:((g % 2) * KC + c + 1) * 128]

        def emit_stats(g, mid_v=None, mid_s=None):
            own = g < NGOWN
            par = g % 2
            xt = self.xtm_a[:, par * D:(par + 1) * D]
            if g >= 2:
                self.wait("Y", self.s_mm, evT[g - 2])
            evDMA[g] = self.dma("Y", xt, self.xs[g * 128:(g + 1) * 128, :], self.s_lx[par])
            mc = self.mstat[:, g:g + 1] if own else self.scol[:, 0:1]
            rc = self.rstat[:, g:g + 1] if own else self.scol[:, 1:2]
            ac = self.scol[:, 2:3]
            sq = self.elu_a[:, par * D:(par + 1) * D]
            self.wait("V", self.s_lx[par], evDMA[g])
            self.em("V", lambda v, xt=xt, mc=mc: v.reduce_sum(mc, xt, axis=AXX))
            self.vbar()
            self.em("V", lambda v, mc=mc: v.tensor_scalar_mul(mc, mc, -INV_D))
            v_negm = self.inc("V", self.s_ve, 1)
            self.wait("S", self.s_ve, v_negm)
            if g >= 2:
                self.wait("S", self.s_mm, evKVmm.get(g - 2, 0))
            self.em("S", lambda s, xt=xt, sq=sq, mc=mc: s.activation(sq, xt, AF.Square, bias=mc))
            self.chain("S", self.s_sc)
            self.em("S", lambda s, xt=xt, mc=mc: s.add(xt, xt, mc))
            s_sub = self.inc("S", self.s_sc, 1)
            if mid_v is not None:
                mid_v()
            self.wait("V", self.s_sc, s_sub)
            self.em("V", lambda v, sq=sq, ac=ac: v.reduce_sum(ac, sq, axis=AXX))
            self.vbar()
            self.em("V", lambda v, ac=ac: v.tensor_scalar(ac, ac, INV_D, EPS_LN, op0=OP.mult, op1=OP.add))
            v_ac = self.inc("V", self.s_ve, 1)
            if mid_s is not None:
                mid_s()
            r0 = self.scol[:, 3:4]
            self.wait("S", self.s_ve, v_ac)
            self.em("S", lambda s, ac=ac, r0=r0: s.activation(r0, ac, AF.Ln))
            self.chain("S", self.s_sc)
            self.em("S", lambda s, r0=r0: s.activation(r0, r0, AF.Exp, scale=-0.5))
            self.chain("S", self.s_sc)
            self.em("S", lambda s, rc=rc, r0=r0: s.mul(rc, r0, r0))
            self.chain("S", self.s_sc)
            self.em("S", lambda s, rc=rc, ac=ac: s.mul(rc, rc, ac))
            self.chain("S", self.s_sc)
            self.em("S", lambda s, rc=rc: s.mul(rc, rc, self.nwc_sb[:, 0:1]))
            self.chain("S", self.s_sc)
            self.em("S", lambda s, rc=rc: s.add(rc, rc, self.nwc_sb[:, 1:2]))
            self.chain("S", self.s_sc)
            self.em("S", lambda s, rc=rc, r0=r0: s.mul(rc, rc, r0))
            self.chain("S", self.s_sc)
            self.em("S", lambda s, xt=xt, rc=rc: s.mul(xt, xt, rc))
            if own:
                self.chain("S", self.s_sc)
                self.em("S", lambda s, xt=xt, g=g: s.copy(self.xn01[:, g:g + 1], xt[:, 0:1]))
                self.chain("S", self.s_sc)
                self.em("S", lambda s, xt=xt, g=g: s.copy(
                    self.xn01[:, NGOWN + g:NGOWN + g + 1], xt[:, 1:2]))
            evA[g] = self.inc("S", self.s_sc, 1)

        def emit_T(g):
            nonlocal evXNw
            own = g < NGOWN
            par = g % 2
            xt = self.xtm_a[:, par * D:(par + 1) * D]
            self.wait("T", self.s_sc, evA[g])
            for wave in range(2):
                self.wait("T", self.s_sc, evXNw)
                for j in range(4):
                    c = wave * 4 + j
                    self.em("T", lambda t, xt=xt, c=c, j=j: t.transpose(
                        ps[4][:, j * 128:(j + 1) * 128], xt[:, c * 128:(c + 1) * 128],
                        self.ident[:]))
                mmv = self.inc("T", self.s_mm, 1)
                if wave == 1:
                    evT[g] = mmv
                self.wait("S", self.s_mm, mmv)
                if (not own) and g >= 2:
                    self.wait("S", self.s_mm, evKVmm.get(g - 2, 0))
                for j in range(4):
                    c = wave * 4 + j
                    self.em("S", lambda s, dst=lhs(c, g), j=j: s.copy(dst, ps[4][:, j * 128:(j + 1) * 128]))
                evXNw = self.inc("S", self.s_sc, 1)
            evXN[g] = evXNw
            self.wait("T", self.s_sc, evXN[g])
            if g >= 1:
                self.wait("T", self.s_sc, evK_s[g - 1])
                self.wait("T", self.s_ve, evK_v[g - 1])
            for k in range(KC):
                for hh in range(2):
                    self.em("T", lambda t, k=k, hh=hh, lh=lhs(k, g): t.matmul(
                        ps[hh][:], lh, self.wk_sb[:, k * D + hh * 512:k * D + (hh + 1) * 512],
                        start=(k == 0), stop=False))
            for hh in range(2):
                self.em("T", lambda t, hh=hh: t.matmul(
                    ps[hh][:], self.onesrow_bf[:], self.kvbias[0:1, hh * 512:(hh + 1) * 512],
                    start=False, stop=True))
            evKmm = self.inc("T", self.s_mm, 1)
            if g >= 1:
                self.wait("T", self.s_ve, evV_v[g - 1])
            for k in range(KC):
                for hh in range(2):
                    self.em("T", lambda t, k=k, hh=hh, lh=lhs(k, g): t.matmul(
                        ps[2 + hh][:], lh, self.wv_sb[:, k * D + hh * 512:k * D + (hh + 1) * 512],
                        start=(k == 0), stop=False))
            for hh in range(2):
                self.em("T", lambda t, hh=hh: t.matmul(
                    ps[2 + hh][:], self.onesrow_bf[:],
                    self.kvbias[0:1, D + hh * 512:D + (hh + 1) * 512],
                    start=False, stop=True))
            evVmm = self.inc("T", self.s_mm, 1)
            return evKmm, evVmm

        def elu_vmax(g, evKmm):
            par = g % 2
            rt = self.elu_a[:, par * D:(par + 1) * D]
            self.wait("V", self.s_mm, evKmm)
            if g >= 2:
                self.wait("V", self.s_mm, evKVmm.get(g - 2, 0))
            for hh in range(2):
                self.em("V", lambda v, hh=hh, rt=rt: v.tensor_scalar(
                    rt[:, hh * 512:(hh + 1) * 512], ps[hh][:], 0.0, None, op0=OP.max))
            evK_v[g] = self.inc("V", self.s_ve, 1)

        def elu_sexp(g, evKmm):
            par = g % 2
            kslot = self.ktm[:, par * D:(par + 1) * D]
            self.wait("S", self.s_mm, evKmm)
            self.wait("S", self.s_ve, evK_v[g])
            if g >= 2:
                self.wait("S", self.s_mm, evKVmm.get(g - 2, 0))
            for hh in range(2):
                self.em("S", lambda s, hh=hh, kslot=kslot: s.activation(
                    kslot[:, hh * 512:(hh + 1) * 512], ps[hh][:], AF.Exp))
            evK_s[g] = self.inc("S", self.s_sc, 1)

        def elu_vtail(g, evVmm):
            par = g % 2
            kslot = self.ktm[:, par * D:(par + 1) * D]
            vslot = self.vtm[:, par * D:(par + 1) * D]
            rt = self.elu_a[:, par * D:(par + 1) * D]
            self.wait("V", self.s_sc, evK_s[g])
            for hh in range(2):
                self.em("V", lambda v, hh=hh, kslot=kslot, rt=rt: v.scalar_tensor_tensor(
                    kslot[:, hh * 512:(hh + 1) * 512], kslot[:, hh * 512:(hh + 1) * 512],
                    1.0, rt[:, hh * 512:(hh + 1) * 512], op0=OP.min, op1=OP.add))
            self.vbar()
            self.wait("V", self.s_mm, evVmm)
            for hh in range(2):
                self.em("V", lambda v, hh=hh, vslot=vslot: v.tensor_copy(
                    vslot[:, hh * 512:(hh + 1) * 512], ps[2 + hh][:]))
            evV_v[g] = self.inc("V", self.s_ve, 1)

        def emit_kv(g):
            par = g % 2
            kslot = self.ktm[:, par * D:(par + 1) * D]
            vslot = self.vtm[:, par * D:(par + 1) * D]
            self.wait("T", self.s_ve, evV_v[g])
            if g >= 1:
                self.wait("T", self.s_ve, evKV[g - 1])
            for pr in range(NPAIR):
                b = 5 + pr // 4
                o = pr % 4
                self.em("T", lambda t, pr=pr, b=b, o=o, kslot=kslot, vslot=vslot: t.matmul(
                    ps[b][:, o * 128:(o + 1) * 128], kslot[:, pr * 128:(pr + 1) * 128],
                    vslot[:, pr * 128:(pr + 1) * 128], start=True, stop=True))
                self.em("T", lambda t, pr=pr, kslot=kslot: t.matmul(
                    ps[7][:, pr:pr + 1], kslot[:, pr * 128:(pr + 1) * 128],
                    self.onescol_bf[:], start=True, stop=True))
            evKVmm[g] = self.inc("T", self.s_mm, 1)
            self.wait("V", self.s_mm, evKVmm[g])
            if g >= 1:
                self.wait("V", self.s_ve, evKV[g - 1])
            if g == 0:
                self.em("V", lambda v: v.tensor_copy(self.kvacc[:, 0:512], ps[5][:]))
                self.em("V", lambda v: v.tensor_copy(self.kvacc[:, 512:1024], ps[6][:]))
                self.em("V", lambda v: v.tensor_copy(self.ksacc[:], ps[7][:, 0:NPAIR]))
            else:
                self.em("V", lambda v: v.tensor_add(self.kvacc[:, 0:512], self.kvacc[:, 0:512], ps[5][:]))
                self.em("V", lambda v: v.tensor_add(self.kvacc[:, 512:1024], self.kvacc[:, 512:1024], ps[6][:]))
                self.em("V", lambda v: v.tensor_add(self.ksacc[:], self.ksacc[:], ps[7][:, 0:NPAIR]))
            evKV[g] = self.inc("V", self.s_ve, 1)

        pend = []
        for g in range(NGL):
            if pend:
                pg, ek, ev = pend.pop(0)
                emit_stats(g, mid_v=lambda pg=pg, ek=ek: elu_vmax(pg, ek),
                           mid_s=lambda pg=pg, ek=ek, ev=ev: (elu_sexp(pg, ek),
                                                             elu_vtail(pg, ev)))
                emit_kv(pg)
            else:
                emit_stats(g)
            ek, ev = emit_T(g)
            pend.append((g, ek, ev))
        pg, ek, ev = pend.pop(0)
        elu_vmax(pg, ek)
        elu_sexp(pg, ek)
        elu_vtail(pg, ev)
        emit_kv(NGL - 1)
        self.evKV_last = evKV[NGL - 1]

    def emit_phase_a_tail(self):
        ps = self.ps
        self.wait("V", self.s_ve, self.evKV_last)
        for pr in range(NPAIR):
            self.em("V", lambda v, pr=pr: v.tensor_copy(
                self.m_sb[0:64, pr * 128:pr * 128 + 64],
                self.kvacc[0:64, pr * 128:pr * 128 + 64]))
            self.em("V", lambda v, pr=pr: v.tensor_copy(
                self.m_sb[64:128, pr * 128 + 64:pr * 128 + 128],
                self.kvacc[64:128, pr * 128 + 64:pr * 128 + 128]))
            self.em("V", lambda v, pr=pr: v.tensor_copy(
                self.ksbd[0:64, pr * 16 + 2 * pr:pr * 16 + 2 * pr + 1],
                self.ksacc[0:64, pr:pr + 1]))
            self.em("V", lambda v, pr=pr: v.tensor_copy(
                self.ksbd[64:128, pr * 16 + 2 * pr + 1:pr * 16 + 2 * pr + 2],
                self.ksacc[64:128, pr:pr + 1]))
        u = self.hall[:]
        t1 = self.xn01[:, 0:NGOWN]
        t2 = self.xn01[:, NGOWN:2 * NGOWN]
        self.em("V", lambda v: v.tensor_add(u, t1, t2))
        self.vbar()
        for bnd in (64.0, 32.0, 16.0, 8.0):
            self.em("V", lambda v, b=bnd: v.tensor_single_scalar(t1, u, b, OP.is_ge))
            self.vbar()
            self.em("V", lambda v, b=bnd: v.scalar_tensor_tensor(u, t1, -b, u, op0=OP.mult, op1=OP.add))
            self.vbar()
            self.em("V", lambda v, b=bnd: v.tensor_single_scalar(t1, u, -b, OP.is_le))
            self.vbar()
            self.em("V", lambda v, b=bnd: v.scalar_tensor_tensor(u, t1, b, u, op0=OP.mult, op1=OP.add))
            self.vbar()
        self.em("V", lambda v: v.tensor_single_scalar(t1, u, 1.0, OP.is_ge))
        self.vbar()
        for kk in range(2, 8):
            self.em("V", lambda v, k=float(kk): v.tensor_single_scalar(t2, u, k, OP.is_ge))
            self.vbar()
            self.em("V", lambda v: v.tensor_add(t1, t1, t2))
            self.vbar()
        for kk in range(1, 8):
            self.em("V", lambda v, k=float(-kk): v.tensor_single_scalar(t2, u, k, OP.is_le))
            self.vbar()
            self.em("V", lambda v: v.tensor_sub(t1, t1, t2))
            self.vbar()
        self.em("V", lambda v: v.tensor_single_scalar(t2, t1, 0.0, OP.is_lt))
        self.vbar()
        self.em("V", lambda v: v.scalar_tensor_tensor(u, t2, 8.0, t1, op0=OP.mult, op1=OP.add))
        self.vbar()
        self.em("V", lambda v: v.tensor_copy(self.tpack[:, 0:NGOWN], self.mstat[:]))
        self.em("V", lambda v: v.tensor_copy(self.tpack[:, NGOWN:2 * NGOWN], self.rstat[:]))
        self.em("V", lambda v: v.tensor_copy(self.tpack[:, 2 * NGOWN:3 * NGOWN], self.hall[:]))
        vp = self.inc("V", self.s_ve, 1)
        self.wait("T", self.s_ve, vp)
        self.em("T", lambda t: t.transpose(ps[4][:, 0:128], self.tpack[:], self.ident[:]))
        mt = self.inc("T", self.s_mm, 1)
        self.wait("V", self.s_mm, mt)
        self.em("V", lambda v: v.tensor_copy(self.stat48[0:16, :], ps[4][32:48, 0:128]))
        vs = self.inc("V", self.s_ve, 1)
        self.wait("Y", self.s_ve, vs)
        self.dma("Y", self.hrow[0:1, :], self.stat48[0:16, :], self.s_sd)
        for e in range(NEXP):
            self.wait("Y", self.s_sd, self.s_sd.n)
            self.dma("Y", self.h8[e:e + 1, :], self.hrow[0:1, :], self.s_sd)
        sdv = self.s_sd.n
        self.wait("V", self.s_sd, sdv)
        self.em("V", lambda v: v.tensor_scalar(self.h8[:], self.h8[:], self.ecol_sb[:], None,
                                               op0=OP.subtract))
        vd = self.inc("V", self.s_ve, 1)
        self.wait("S", self.s_ve, vd)
        self.em("S", lambda s: s.activation(self.wscr[:], self.h8[:], AF.Abs))
        self.chain("S", self.s_sc)
        self.em("S", lambda s: s.activation(self.wrow_f[:], self.wscr[:], AF.Relu, scale=-1.0, bias=self.mkc_sb[:, 0:1]))
        self.chain("S", self.s_sc)
        self.em("S", lambda s: s.activation(self.wscr[:], self.h8[:], AF.Abs, bias=self.mkc_sb[:, 0:1]))
        self.chain("S", self.s_sc)
        self.em("S", lambda s: s.activation(self.wscr[:], self.wscr[:], AF.Relu, scale=-1.0, bias=self.mkc_sb[:, 0:1]))
        s1 = self.inc("S", self.s_sc, 1)
        self.wait("V", self.s_sc, s1)
        self.em("V", lambda v: v.tensor_add(self.wrow_f[:], self.wrow_f[:], self.wscr[:]))
        v1 = self.inc("V", self.s_ve, 1)
        self.wait("S", self.s_ve, v1)
        self.em("S", lambda s: s.activation(self.wscr[:], self.h8[:], AF.Abs, bias=self.mkc_sb[:, 1:2]))
        self.chain("S", self.s_sc)
        self.em("S", lambda s: s.activation(self.wscr[:], self.wscr[:], AF.Relu, scale=-1.0, bias=self.mkc_sb[:, 0:1]))
        s2 = self.inc("S", self.s_sc, 1)
        self.wait("V", self.s_sc, s2)
        self.em("V", lambda v: v.tensor_add(self.wrow_f[:], self.wrow_f[:], self.wscr[:]))
        self.vbar()
        self.em("V", lambda v: v.tensor_scalar_mul(self.wrow_f[:], self.wrow_f[:], 0.5))
        self.vbar()
        self.em("V", lambda v: v.tensor_copy(self.wrow_bf[:], self.wrow_f[:]))
        self.wrow_ev = self.inc("V", self.s_ve, 1)

    def emit_moe(self):
        ps = self.ps
        regA_ev = [0, 0]
        hEv_prev = [0]
        wbEv = [0, 0]
        wtmpRead = [0, 0]
        hW_last = {}
        yEv_prev = [0]
        ebv_of = {}

        def emit_wb(e):
            sl = e % 2
            self.wait("Y", self.s_ve, self.wrow_ev)
            if e >= 2:
                self.wait("Y", self.s_mm, wtmpRead[sl])
            self.wait("Y", self.s_sd, self.s_sd.n)
            self.dma("Y", self.wtmp[0:1, sl * TOWN:(sl + 1) * TOWN],
                     self.wrow_bf[e:e + 1, :], self.s_sd)
            sdv = self.s_sd.n
            if e >= 2:
                self.wait("G", self.s_mm, wtmpRead[sl])
            ebv_of[e] = self.dma("G", self.eb2cur[0:1, sl * D:(sl + 1) * D],
                                 self.eb2_rows[e:e + 1, :], self.s_eb[sl])
            self.wait("T", self.s_sd, sdv)
            self.wait("T", self.s_sc, hEv_prev[0])
            for tt in range(4):
                self.em("T", lambda t, tt=tt, sl=sl: t.matmul(
                    ps[tt][:], self.onesrow_bf[:],
                    self.wtmp[0:1, sl * TOWN + tt * 512:sl * TOWN + (tt + 1) * 512],
                    start=True, stop=True))
            mmwb = self.inc("T", self.s_mm, 1)
            self.wait("V", self.s_mm, mmwb)
            for tt in range(4):
                self.em("V", lambda v, tt=tt, sl=sl: v.tensor_copy(
                    self.wb[:, sl * TOWN + tt * 512:sl * TOWN + (tt + 1) * 512], ps[tt][:]))
            wbEv[sl] = self.inc("V", self.s_ve, 1)

        def emit_h_block(e, gg, i):
            n = gg * GH + i
            r = gg % 2
            hreg = self.h_m[r]
            slot, wv = self.w1_load(self.ew1_pk[e % self.emod, n])
            self.wait("T", self.s_w1[slot], wv)
            self.wait("T", self.s_sc, hEv_prev[0])
            if gg == 0 and i == 0:
                self.wait("T", self.s_ve, wbEv[e % 2])
            for k in range(KC):
                for tt in range(4):
                    self.em("T", lambda t, k=k, tt=tt, slot=slot: t.matmul(
                        ps[tt][:], self.w1t(slot, k),
                        self.xnbf[:, k * TOWN + tt * 512:k * TOWN + (tt + 1) * 512],
                        start=(k == 0), stop=(k == KC - 1)))
            mmH = self.inc("T", self.s_mm, 1)
            self.w1_ev.append((self.s_mm, mmH))
            self.wait("S", self.s_mm, mmH)
            if regA_ev[r]:
                self.wait("S", self.s_mm, regA_ev[r])
            for tt in range(4):
                self.em("S", lambda s, tt=tt, i=i, hreg=hreg, e=e, n=n: s.activation(
                    hreg[:, i * TOWN + tt * 512:i * TOWN + (tt + 1) * 512], ps[tt][:],
                    AF.Relu, bias=self.eb1_sb[:, e * FFC + n:e * FFC + n + 1]))
            hEv_prev[0] = self.inc("S", self.s_sc, 1)
            self.wait("V", self.s_sc, hEv_prev[0])
            if self.moe_wb:
                self.wait("V", self.s_ve, wbEv[e % 2])
                for tt in range(4):
                    self.em("V", lambda v, tt=tt, i=i, hreg=hreg, e=e: v.tensor_mul(
                        hreg[:, i * TOWN + tt * 512:i * TOWN + (tt + 1) * 512],
                        hreg[:, i * TOWN + tt * 512:i * TOWN + (tt + 1) * 512],
                        self.wb[:, (e % 2) * TOWN + tt * 512:(e % 2) * TOWN + (tt + 1) * 512]))
            else:
                self.em("V", lambda v: v.memset(self.dum[:, 0:2], 0.0))
            hW_last[(e, gg)] = self.inc("V", self.s_ve, 1)

        def emit_y_sub(e, gg, dout):
            r = gg % 2
            hreg = self.h_m[r]
            slot, wv = self.w2_load(self.ew2_pk[e, gg, dout])
            self.wait("T", self.s_w2[slot], wv)
            self.wait("T", self.s_ve, hW_last[(e, gg)])
            self.wait("T", self.s_ve, yEv_prev[0])
            if gg == NGRP - 1 and dout == 0:
                self.wait("T", self.s_eb[e % 2], ebv_of[e])
            last_plain = (gg != NGRP - 1)
            for j in range(GH):
                for tt in range(4):
                    self.em("T", lambda t, j=j, tt=tt, slot=slot, hreg=hreg, lp=last_plain: t.matmul(
                        ps[4 + tt][:], self.w2t(slot, j),
                        hreg[:, j * TOWN + tt * 512:j * TOWN + (tt + 1) * 512],
                        start=(j == 0), stop=(lp and j == GH - 1)))
            if gg == NGRP - 1:
                for tt in range(4):
                    self.em("T", lambda t, tt=tt, e=e, dout=dout: t.matmul(
                        ps[4 + tt][:],
                        self.eb2cur[0:1, (e % 2) * D + dout * 128:(e % 2) * D + (dout + 1) * 128],
                        self.wtmp[0:1, (e % 2) * TOWN + tt * 512:(e % 2) * TOWN + (tt + 1) * 512],
                        start=False, stop=True))
            mmY = self.inc("T", self.s_mm, 1)
            self.w2_ev.append((self.s_mm, mmY))
            if dout == KC - 1:
                regA_ev[r] = mmY
            if gg == NGRP - 1 and dout == GH - 1:
                wtmpRead[e % 2] = mmY
            self.wait("V", self.s_mm, mmY)
            self.wait("V", self.s_ve, yEv_prev[0])
            for tt in range(4):
                dst = self.acc[:, dout * TOWN + tt * 512:dout * TOWN + (tt + 1) * 512]
                if e == 0 and gg == 0:
                    self.em("V", lambda v, dst=dst, tt=tt: v.tensor_copy(dst, ps[4 + tt][:]))
                else:
                    self.em("V", lambda v, dst=dst, tt=tt: v.tensor_add(dst, dst, ps[4 + tt][:]))
            yEv_prev[0] = self.inc("V", self.s_ve, 1)

        prev = None
        nh = [0]
        for e in range(self.moe_e):
            if self.moe_wb:
                emit_wb(e)
            for gg in range(NGRP):
                for i in range(GH):
                    if nh[0] >= self.hblk:
                        break
                    emit_h_block(e, gg, i)
                    nh[0] += 1
                    if prev is not None and self.moe_y:
                        emit_y_sub(prev[0], prev[1], i)
                if nh[0] >= self.hblk:
                    break
                prev = (e, gg)
            if nh[0] >= self.hblk:
                break
        if self.moe_y:
            for i in range(GH):
                emit_y_sub(prev[0], prev[1], i)

    def emit_b1(self):
        ps = self.ps
        evQpar = [0, 0]
        evQs = [0, 0]
        evDen = [0]
        evZB = [0]
        evZBp = [0, 0]
        evAOZ = [0, 0]
        evAOZ_last = [0]
        aoz_done = [0, 0]
        evX1par = [0, 0]
        xf_prev = [0, 0]
        xf_evac = [0, 0]
        xn_ready = [0, 0]

        def qproj_n(s, n):
            slot, wv = self.w1_load(self.wq_pk[n])
            self.wait("T", self.s_w1[slot], wv)
            self.wait("T", self.s_ve, evQpar[n % 2])
            for k in range(KC):
                for sub in range(2):
                    self.em("T", lambda t, k=k, sub=sub, n=n, s=s, slot=slot: t.matmul(
                        ps[2 * (n % 2) + sub][:], self.w1t(slot, k),
                        self.xnbf[:, k * TOWN + s * 1024 + sub * 512:
                                  k * TOWN + s * 1024 + (sub + 1) * 512],
                        start=(k == 0), stop=(k == KC - 1)))
            mmQ = self.inc("T", self.s_mm, 1)
            self.w1_ev.append((self.s_mm, mmQ))
            for sub in range(2):
                pstile = ps[2 * (n % 2) + sub]
                qsl = self.q_sb[:, n * TOWN + s * 1024 + sub * 512:
                                n * TOWN + s * 1024 + (sub + 1) * 512]
                self.wait("V", self.s_mm, mmQ)
                self.wait("V", self.s_ve, self.s_ve.n)
                self.em("V", lambda v, p=pstile, n=n: v.tensor_scalar(
                    self.elu_r[:], p[:], self.bq_sb[:, n:n + 1], 0.0, op0=OP.add, op1=OP.max))
                vmax = self.vbar()
                self.wait("S", self.s_mm, mmQ)
                self.wait("S", self.s_ve, vmax)
                self.em("S", lambda x, p=pstile, q=qsl, n=n: x.activation(
                    q, p[:], AF.Exp, bias=self.bq_sb[:, n:n + 1]))
                sv = self.inc("S", self.s_sc, 1)
                self.wait("V", self.s_sc, sv)
                self.em("V", lambda v, q=qsl: v.scalar_tensor_tensor(
                    q, q, 1.0, self.elu_r[:], op0=OP.min, op1=OP.add))
                evQpar[n % 2] = self.inc("V", self.s_ve, 1)
            evQs[s] = evQpar[n % 2]

        def mix_sub(s, sub):
            qcol = s * 1024 + sub * 512
            self.wait("T", self.s_ve, evQs[s])
            self.wait("T", self.s_ve, evDen[0])
            if s == 1 and sub == 0:
                self.wait("T", self.s_sc, max(xf_evac))
                self.wait("S", self.s_mm, max(xf_prev))
            for p in range(NPAIR):
                self.em("T", lambda t, p=p, qcol=qcol: t.matmul(
                    ps[4][0:16, :], self.ksbd[:, p * 16:(p + 1) * 16],
                    self.q_sb[:, p * TOWN + qcol:p * TOWN + qcol + 512],
                    start=(p == 0), stop=(p == NPAIR - 1)))
            mmDen = self.inc("T", self.s_mm, 1)
            self.wait("V", self.s_mm, mmDen)
            self.em("V", lambda v: v.tensor_scalar(ps[4][0:16, :], ps[4][0:16, :],
                                                   EPS_ATTN, None, op0=OP.add))
            self.vbar()

            def _recip(v):
                with self.nc.allow_low_precision("z in bf16 (matmul operand)"):
                    return v.reciprocal(self.zbfb[:], ps[4][0:16, :])
            self.em("V", _recip)
            evDen[0] = self.inc("V", self.s_ve, 1)
            for p in range(NPAIR):
                zsl = self.zb[:, (p % 2) * 512:(p % 2 + 1) * 512]
                self.wait("T", self.s_ve, evDen[0])
                self.wait("T", self.s_sc, evZB[0])
                self.em("T", lambda t, p=p: t.matmul(
                    ps[5][:], self.e2_sb[:, p * 128:(p + 1) * 128], self.zbfb[:],
                    start=True, stop=True))
                mmZB = self.inc("T", self.s_mm, 1)
                self.wait("S", self.s_mm, mmZB)
                self.wait("S", self.s_ve, evAOZ[p % 2] if p > 1 else evAOZ_last[0])
                self.em("S", lambda x, zsl=zsl: x.copy(zsl, ps[5][:]))
                evZB[0] = self.inc("S", self.s_sc, 1)
                evZBp[p % 2] = evZB[0]
                self.wait("T", self.s_ve, evAOZ[p % 2])
                self.em("T", lambda t, p=p, qcol=qcol: t.matmul(
                    ps[6 + p % 2][:], self.m_sb[:, p * 128:(p + 1) * 128],
                    self.q_sb[:, p * TOWN + qcol:p * TOWN + qcol + 512],
                    start=True, stop=True))
                mmQM = self.inc("T", self.s_mm, 1)
                self.wait("V", self.s_mm, mmQM)
                self.wait("V", self.s_sc, evZBp[p % 2])
                self.em("V", lambda v, p=p, qcol=qcol, zsl=zsl: v.tensor_mul(
                    self.q_sb[:, p * TOWN + qcol:p * TOWN + qcol + 512],
                    ps[6 + p % 2][:], zsl))
                evAOZ[p % 2] = self.inc("V", self.s_ve, 1)
            evAOZ_last[0] = evAOZ[(NPAIR - 1) % 2]
            aoz_done[s] = max(evAOZ)

        def xnrec(s):
            self.wait("T", self.s_sc, evZB[0])
            self.wait("Y", self.s_ve, aoz_done[s])
            if s == 1:
                self.wait("S", self.s_ve, max(evX1par))
            for j in range(8):
                gown = s * 8 + j
                for fh in range(2):
                    sl = fh
                    xt = self.xt_b1[:]
                    if xf_prev[0] or xf_prev[1]:
                        self.wait("Y", self.s_mm, max(xf_prev))
                    lxv = self.dma("Y", xt,
                                   self.xs[gown * 128:(gown + 1) * 128, fh * 512:(fh + 1) * 512],
                                   self.s_lx[sl])
                    self.wait("V", self.s_lx[sl], lxv)
                    self.em("V", lambda v, xt=xt, gown=gown: v.tensor_scalar(
                        xt, xt, self.mstat[:, gown:gown + 1], self.rstat[:, gown:gown + 1],
                        op0=OP.add, op1=OP.mult))
                    vn = self.inc("V", self.s_ve, 1)
                    self.wait("T", self.s_ve, vn)
                    self.wait("T", self.s_sc, xf_evac[sl])
                    pb = 4 + fh
                    for cc in range(4):
                        self.em("T", lambda t, xt=xt, cc=cc, pb=pb: t.transpose(
                            ps[pb][:, cc * 128:(cc + 1) * 128], xt[:, cc * 128:(cc + 1) * 128],
                            self.ident[:]))
                    xf_prev[sl] = self.inc("T", self.s_mm, 1)
                    self.wait("S", self.s_mm, xf_prev[sl])
                    for cc in range(4):
                        c = fh * 4 + cc
                        self.em("S", lambda x, c=c, j=j, cc=cc, pb=pb: x.copy(
                            self.xnfm[:, c * 1024 + j * 128:c * 1024 + (j + 1) * 128],
                            ps[pb][:, cc * 128:(cc + 1) * 128]))
                    xf_evac[sl] = self.inc("S", self.s_sc, 1)
            xn_ready[s] = self.s_sc.n

        def wo_n(s, n):
            slot, wv = self.w1_load(self.wo_pk[n])
            self.wait("T", self.s_w1[slot], wv)
            self.wait("T", self.s_ve, evX1par[n % 2])
            if n == 0:
                self.wait("T", self.s_ve, aoz_done[s])
            for k in range(KC):
                for sub in range(2):
                    self.em("T", lambda t, k=k, sub=sub, n=n, s=s, slot=slot: t.matmul(
                        ps[2 * (n % 2) + sub][:], self.w1t(slot, k),
                        self.q_sb[:, k * TOWN + s * 1024 + sub * 512:
                                  k * TOWN + s * 1024 + (sub + 1) * 512],
                        start=(k == 0), stop=(k == KC - 1)))
            mmWo = self.inc("T", self.s_mm, 1)
            self.w1_ev.append((self.s_mm, mmWo))
            self.wait("V", self.s_mm, mmWo)
            self.wait("V", self.s_sc, xn_ready[s])
            for sub in range(2):
                self.em("V", lambda v, n=n, s=s, sub=sub: v.scalar_tensor_tensor(
                    self.x1_sb[:, n * TOWN + s * 1024 + sub * 512:
                               n * TOWN + s * 1024 + (sub + 1) * 512],
                    ps[2 * (n % 2) + sub][:], self.bo_sb[:, n:n + 1],
                    self.xnfm[:, n * 1024 + sub * 512:n * 1024 + (sub + 1) * 512],
                    op0=OP.add, op1=OP.add))
            evX1par[n % 2] = self.inc("V", self.s_ve, 1)

        for n in range(KC):
            qproj_n(0, n)
        for n in range(KC):
            qproj_n(1, n)
            if n == 3:
                mix_sub(0, 0)
            if n == 7:
                mix_sub(0, 1)
        xnrec(0)
        for n in range(KC):
            wo_n(0, n)
            if n == 3:
                mix_sub(1, 0)
            if n == 7:
                mix_sub(1, 1)
        xnrec(1)
        for n in range(KC):
            wo_n(1, n)

    def emit_ln(self, which):
        ps = self.ps
        g_sb = self.g1_sb if which == 1 else self.g2_sb
        be_sb = self.be1_sb if which == 1 else self.be2_sb
        evMMpar = [0, 0]
        evRowEvac = 0
        evApply = {}
        evBF = [0, 0]
        for tt in range(4):
            for c in range(KC):
                sl = c % 2
                xsl = self.x1_sb[:, c * TOWN + tt * 512:c * TOWN + (tt + 1) * 512]
                b1 = self.ln_bf[:, sl * 512:(sl + 1) * 512]
                b2 = self.ln_bf[:, (2 + sl) * 512:(3 + sl) * 512]
                self.wait("S", self.s_mm, evMMpar[sl])
                self.em("S", lambda s, b1=b1, x=xsl: s.copy(b1, x))
                self.chain("S", self.s_sc)
                self.em("S", lambda s, b2=b2, x=xsl: s.activation(b2, x, AF.Square))
                vb = self.inc("S", self.s_sc, 1)
                self.wait("T", self.s_sc, vb)
                if c == 0:
                    self.wait("T", self.s_ve, evRowEvac)
                self.em("T", lambda t, b1=b1, c=c: t.matmul(
                    ps[0][0:1, :], self.onescol_bf[:], b1, start=(c == 0), stop=(c == KC - 1)))
                self.em("T", lambda t, b2=b2, c=c: t.matmul(
                    ps[1][0:1, :], self.onescol_bf[:], b2, start=(c == 0), stop=(c == KC - 1)))
                evMMpar[sl] = self.inc("T", self.s_mm, 1)
            mmStat = evMMpar[(KC - 1) % 2]
            mrow = self.ln_m[0:1, tt * 512:(tt + 1) * 512]
            rrow = self.ln_r[0:1, tt * 512:(tt + 1) * 512]
            self.wait("V", self.s_mm, mmStat)
            self.em("V", lambda v, m=mrow: v.tensor_scalar_mul(m, ps[0][0:1, :], INV_D))
            self.em("V", lambda v: v.tensor_scalar(self.ln_s1[:], ps[1][0:1, :], INV_D, EPS_LN,
                                                   op0=OP.mult, op1=OP.add))
            self.vbar()
            self.em("V", lambda v, m=mrow: v.tensor_mul(self.ln_s2[:], m, m))
            self.vbar()
            self.em("V", lambda v: v.tensor_sub(self.ln_s1[:], self.ln_s1[:], self.ln_s2[:]))
            evRowEvac = self.inc("V", self.s_ve, 1)
            self.wait("S", self.s_ve, evRowEvac)
            self.em("S", lambda s: s.activation(self.ln_s2[:], self.ln_s1[:], AF.Ln))
            self.chain("S", self.s_sc)
            self.em("S", lambda s: s.activation(self.ln_s2[:], self.ln_s2[:], AF.Exp, scale=-0.5))
            sv = self.inc("S", self.s_sc, 1)
            self.wait("V", self.s_sc, sv)
            self.em("V", lambda v: v.tensor_mul(self.ln_s3[:], self.ln_s2[:], self.ln_s2[:]))
            self.vbar()
            self.em("V", lambda v: v.tensor_mul(self.ln_s3[:], self.ln_s3[:], self.ln_s1[:]))
            self.vbar()
            self.em("V", lambda v: v.tensor_scalar(self.ln_s3[:], self.ln_s3[:], -0.5, 1.5,
                                                   op0=OP.mult, op1=OP.add))
            self.vbar()
            self.em("V", lambda v, r=rrow: v.tensor_mul(r, self.ln_s3[:], self.ln_s2[:]))
            vr = self.inc("V", self.s_ve, 1)
            self.wait("T", self.s_ve, vr)
            self.wait("T", self.s_ve, evApply.get((tt - 1, KC - 1), 0))
            self.em("T", lambda t, m=mrow: t.matmul(ps[2][:], self.onesrow_f[:], m,
                                                    start=True, stop=True))
            self.em("T", lambda t, r=rrow: t.matmul(ps[3][:], self.onesrow_f[:], r,
                                                    start=True, stop=True))
            mmB = self.inc("T", self.s_mm, 1)
            self.wait("V", self.s_mm, mmB)
            stages = []
            for c in range(KC):
                lt = lambda c: self.ln_t[:, (c % 2) * 512:(c % 2 + 1) * 512]
                xs = self.x1_sb[:, c * TOWN + tt * 512:c * TOWN + (tt + 1) * 512]
                stages.append(("fin", c - 2))
                stages.append(("mul", c - 1))
                stages.append(("sub", c))
            stages += [("fin", KC - 2), ("mul", KC - 1), ("fin", KC - 1)]
            for step, c in stages:
                if c < 0:
                    continue
                sl = c % 2
                lts = self.ln_t[:, sl * 512:(sl + 1) * 512]
                xsl = self.x1_sb[:, c * TOWN + tt * 512:c * TOWN + (tt + 1) * 512]
                if step == "sub":
                    prev = (tt, c - 2) if c >= 2 else (tt - 1, KC - 2 + c)
                    self.wait("V", self.s_ve, evApply.get(prev, 0))
                    self.em("V", lambda v, x=xsl, lts=lts: v.tensor_sub(lts, x, ps[2][:]))
                    self.inc("V", self.s_ve, 1)
                elif step == "mul":
                    self.wait("V", self.s_ve, self.s_ve.n)
                    self.em("V", lambda v, lts=lts: v.tensor_mul(lts, lts, ps[3][:]))
                    self.inc("V", self.s_ve, 1)
                else:
                    self.wait("V", self.s_ve, self.s_ve.n)
                    if which == 1:
                        self.em("V", lambda v, x=xsl, lts=lts, c=c: v.tensor_scalar(
                            x, lts, g_sb[:, c:c + 1], be_sb[:, c:c + 1],
                            op0=OP.mult, op1=OP.add))
                        vx = self.inc("V", self.s_ve, 1)
                        csl = self.c_bf[:, c * TOWN + tt * 512:c * TOWN + (tt + 1) * 512]
                        self.wait("S", self.s_ve, vx)
                        self.em("S", lambda s, csl=csl, x=xsl: s.copy(csl, x))
                        self.chain("S", self.s_sc)
                        evApply[(tt, c)] = vx
                    else:
                        self.em("V", lambda v, lts=lts, c=c: v.tensor_scalar(
                            lts, lts, g_sb[:, c:c + 1], be_sb[:, c:c + 1],
                            op0=OP.mult, op1=OP.add))
                        self.vbar()
                        asl = self.acc[:, c * TOWN + tt * 512:c * TOWN + (tt + 1) * 512]
                        self.em("V", lambda v, a=asl, lts=lts: v.tensor_add(a, a, lts))
                        evApply[(tt, c)] = self.inc("V", self.s_ve, 1)

    def emit_b2(self):
        ps = self.ps
        hprev_tt = [0] * 4
        hlast_tt = [0] * 4
        mmY_tt = [0] * 4
        yadd_tt = [0] * 4
        for gg in range(NGRP):
            for i in range(GH):
                n = gg * GH + i
                slot, wv = self.w1_load(self.w1_pk[n])
                self.wait("T", self.s_w1[slot], wv)
                mmH_tt = [0] * 4
                for tt in range(4):
                    self.wait("T", self.s_sc, hprev_tt[tt])
                    if i == 0:
                        self.wait("T", self.s_mm, mmY_tt[tt])
                    for k in range(KC):
                        self.em("T", lambda t, k=k, tt=tt, slot=slot: t.matmul(
                            ps[tt][:], self.w1t(slot, k),
                            self.c_bf[:, k * TOWN + tt * 512:k * TOWN + (tt + 1) * 512],
                            start=(k == 0), stop=(k == KC - 1)))
                    mmH_tt[tt] = self.inc("T", self.s_mm, 1)
                self.w1_ev.append((self.s_mm, mmH_tt[3]))
                for tt in range(4):
                    self.wait("S", self.s_mm, mmH_tt[tt])
                    self.em("S", lambda s, tt=tt, i=i, n=n: s.activation(
                        self.h_b2[:, i * TOWN + tt * 512:i * TOWN + (tt + 1) * 512],
                        ps[tt][:], AF.Relu, bias=self.b1_sb[:, n:n + 1]))
                    hprev_tt[tt] = self.inc("S", self.s_sc, 1)
                hlast_tt = list(hprev_tt)
            for dout in range(KC):
                slot, wv = self.w2_load(self.w2_pk[gg, dout])
                self.wait("T", self.s_w2[slot], wv)
                mmYc = [0] * 4
                for tt in range(4):
                    self.wait("T", self.s_sc, hlast_tt[tt])
                    self.wait("T", self.s_ve, yadd_tt[tt])
                    for j in range(GH):
                        self.em("T", lambda t, j=j, tt=tt, slot=slot: t.matmul(
                            ps[4 + tt][:], self.w2t(slot, j),
                            self.h_b2[:, j * TOWN + tt * 512:j * TOWN + (tt + 1) * 512],
                            start=(j == 0), stop=(j == GH - 1)))
                    mmYc[tt] = self.inc("T", self.s_mm, 1)
                self.w2_ev.append((self.s_mm, mmYc[3]))
                if dout == KC - 1:
                    mmY_tt = mmYc
                for tt in range(4):
                    xsl = self.x1_sb[:, dout * TOWN + tt * 512:dout * TOWN + (tt + 1) * 512]
                    self.wait("V", self.s_mm, mmYc[tt])
                    self.wait("V", self.s_ve, yadd_tt[tt])
                    if gg == 0:
                        self.em("V", lambda v, x=xsl, tt=tt, dout=dout: v.scalar_tensor_tensor(
                            x, ps[4 + tt][:], self.b2a_sb[:, dout:dout + 1], x,
                            op0=OP.add, op1=OP.add))
                    else:
                        self.em("V", lambda v, x=xsl, tt=tt: v.tensor_add(x, x, ps[4 + tt][:]))
                    yadd_tt[tt] = self.inc("V", self.s_ve, 1)

    def emit_b4(self):
        ps = self.ps
        evAdd = {}
        ldsem = [self.s_eb[0], self.s_eb[1], self.s_w1[0], self.s_w1[1]]
        stsem = [self.s_st[0], self.s_st[1], self.s_lx[0], self.s_lx[1]]
        lxv = {}
        stv = {}
        for j in range(NGOWN):
            sl = j % 4
            par = j % 2
            xt = self.xtm_b4[:, sl * D:(sl + 1) * D]
            ot = self.otm_b4[:, sl * D:(sl + 1) * D]
            if j >= 4:
                self.wait("G", self.s_ve, evAdd[j - 4])
            lxv[j] = self.dma("G", xt, self.xs[j * 128:(j + 1) * 128, :], ldsem[sl])
            pbase = 0 if par == 0 else 2
            self.wait("T", self.s_ve, evAdd.get(j - 2, 0))
            for c in range(KC):
                self.em("T", lambda t, c=c, j=j, pbase=pbase: t.transpose(
                    ps[pbase + c // 4][:, (c % 4) * 128:(c % 4 + 1) * 128],
                    self.acc[:, c * TOWN + j * 128:c * TOWN + (j + 1) * 128], self.ident[:]))
            mmT = self.inc("T", self.s_mm, 1)
            self.wait("V", self.s_mm, mmT)
            self.wait("V", ldsem[sl], lxv[j])
            if j >= 4:
                self.wait("V", stsem[sl], stv[j - 4])
            for hh in range(2):
                self.em("V", lambda v, ot=ot, xt=xt, hh=hh, pbase=pbase: v.tensor_add(
                    ot[:, hh * 512:(hh + 1) * 512], xt[:, hh * 512:(hh + 1) * 512],
                    ps[pbase + hh][:]))
            evAdd[j] = self.inc("V", self.s_ve, 1)
            self.wait("Y", self.s_ve, evAdd[j])
            stv[j] = self.dma("Y", self.out[j * 128:(j + 1) * 128, :], ot, stsem[sl])
        for s in stsem:
            self.wait("Y", s, s.n)
        for s in ldsem:
            self.wait("Y", s, s.n)
        if self.s_sd.n:
            self.wait("Y", self.s_sd, self.s_sd.n)
        if self.s_dbg.n:
            self.wait("Y", self.s_dbg, self.s_dbg.n)

    def build(self, phases=("A", "MOE", "B1", "LN1", "B2", "LN2", "B4")):
        self.emit_aux()
        if "A0" in phases:
            self.emit_phase_a()
            self.barrier()
        if "A" in phases:
            self.emit_phase_a()
            self.emit_phase_a_tail()
            self.barrier()
            self._dump("xnbf", self.xnbf, cast=True)
            self._dump("msb", self.m_sb, cast=True)
            self._dump("ksbd", self.ksbd, cast=True)
            self._dump("wrow", self.wrow_f)
            self._dump("hall", self.hall)
            self._dump("mstat", self.mstat)
            self._dump("rstat", self.rstat)
            self._dump("ktm", self.ktm, cast=True)
        if "MOE" in phases:
            self.emit_moe()
            self.barrier()
            self._dump("accm", self.acc)
        if "B1" in phases:
            self.emit_b1()
            self.barrier()
            self._dump("x1_pre", self.x1_sb)
            self._dump("qsb", self.q_sb, cast=True)
        if "LN1" in phases:
            self.emit_ln(1)
            self.barrier()
            self._dump("cbf", self.c_bf, cast=True)
        if "B2" in phases:
            self.emit_b2()
            self.barrier()
            self._dump("x1n", self.x1_sb)
        if "LN2" in phases:
            self.emit_ln(2)
            self.barrier()
            self._dump("accf", self.acc)
        if "B4" in phases:
            self.emit_b4()
        else:
            self.barrier()
            fb = self.acc[:, 0:D] if "MOE" in phases else self.kvacc[:]
            self.dma("Y", self.out[0:128, :], fb, self.s_st[0])
            self.wait("Y", self.s_st[0], self.s_st[0].n)
            if self.s_sd.n:
                self.wait("Y", self.s_sd, self.s_sd.n)
            if self.s_dbg.n:
                self.wait("Y", self.s_dbg, self.s_dbg.n)
        nc = self.nc
        with nc.Block() as block:
            @block.tensor
            def _(t):
                for fn in self.L["T"]:
                    fn(t)

            @block.vector
            def _(v):
                for fn in self.L["V"]:
                    fn(v)

            @block.scalar
            def _(s):
                for fn in self.L["S"]:
                    fn(s)

            @block.gpsimd
            def _(g):
                for fn in self.L["G"]:
                    fn(g)

            @block.sync
            def _(y):
                for fn in self.L["Y"]:
                    fn(y)
        return nc


def host_pack(inputs):
    f32 = lambda a: np.ascontiguousarray(np.asarray(a, np.float32))
    shared = {
        "wq_pk": pack_w1(f32(inputs["L0_attn_Wq"])),
        "wo_pk": pack_w1(f32(inputs["L0_attn_Wo"])),
        "wk_nat": f32(inputs["L0_attn_Wk"]),
        "wv_nat": f32(inputs["L0_attn_Wv"]),
        "w1_pk": pack_w1(f32(inputs["L0_attn_W1"])),
        "w2_pk": pack_w2(f32(inputs["L0_attn_W2"])),
        "ew1_pk": np.stack([pack_w1(f32(inputs["L0_ffn_W1"][e])) for e in range(NEXP)]),
        "ew2_pk": np.stack([pack_w2(f32(inputs["L0_ffn_W2"][e])) for e in range(NEXP)]),
        "bq_pp": pp(f32(inputs["L0_attn_bq"])),
        "bo_pp": pp(f32(inputs["L0_attn_bo"])),
        "g1_pp": pp(f32(inputs["L0_attn_g1"])),
        "be1_pp": pp(f32(inputs["L0_attn_beta1"])),
        "g2_pp": pp(f32(inputs["L0_attn_g2"])),
        "be2_pp": pp(f32(inputs["L0_attn_beta2"])),
        "b2a_pp": pp(f32(inputs["L0_attn_b2"])),
        "b1_pp": pp(f32(inputs["L0_attn_b1"])),
        "eb1_pp": np.concatenate([pp(f32(inputs["L0_ffn_b1"][e])) for e in range(NEXP)], axis=1),
        "bk_row": f32(inputs["L0_attn_bk"]).reshape(1, D),
        "bv_row": f32(inputs["L0_attn_bv"]).reshape(1, D),
        "eb2_rows": f32(inputs["L0_ffn_b2"]).reshape(NEXP, D),
        "ident_c": np.eye(128, dtype=np.float32),
        "onescol_c": np.ones((128, 1), np.float32),
        "onesrow_c": np.ones((1, 128), np.float32),
        "e2_c": make_e2(),
        "ecol_c": np.arange(NEXP, dtype=np.float32).reshape(NEXP, 1),
        "mkc_c": np.tile(np.array([[1.0, -7.0]], np.float32), (NEXP, 1)),
        "nwc_c": np.tile(np.array([[-0.5, 1.5]], np.float32), (128, 1)),
    }
    return {k: np.ascontiguousarray(v) for k, v in shared.items()}


def make_in_maps(inputs):
    x = np.ascontiguousarray(np.asarray(inputs["L0_attn_x"], np.float32))
    shared = host_pack(inputs)
    in_maps = []
    for c in range(8):
        seq, half = c // 2, c % 2
        own = x[seq, half * TOWN:(half + 1) * TOWN]
        other = x[seq, (1 - half) * TOWN:(2 - half) * TOWN]
        m = dict(shared)
        m["xs"] = np.ascontiguousarray(np.concatenate([own, other], axis=0))
        in_maps.append(m)
    return in_maps


def kernel(**inputs):
    nc = KB().build()
    in_maps = make_in_maps(inputs)
    res = run_bass_kernel_spmd(nc, in_maps, core_ids=list(range(8)))
    out = np.empty((4, 4096, 1024), np.float32)
    for c in range(8):
        seq, half = c // 2, c % 2
        out[seq, half * TOWN:(half + 1) * TOWN] = res.results[c]["out"]
    return out


# version 11
# speedup vs baseline: 1.0677x; 1.0285x over previous
import numpy as np
import ml_dtypes
import concourse.bass as bass
import concourse.mybir as mybir
from concourse.bass_utils import run_bass_kernel_spmd

F32 = mybir.dt.float32
BF16 = mybir.dt.bfloat16
I32 = mybir.dt.int32
AF = mybir.ActivationFunctionType
OP = mybir.AluOpType
AXX = mybir.AxisListType.X

D = 1024
KC = 8
NPAIR = 8
L = 4096
TOWN = 2048
NG = 32
NGOWN = 16
FF = 4096
FFC = 32
NEXP = 8
GH = 8
NGRP = FFC // GH
EPS_LN = 1e-5
EPS_ATTN = 1e-6
INV_D = 1.0 / D

DMA_SCRATCH = 16384
SBUF_TOP = 229344


def pack_w1(W):
    din, dout = W.shape
    return np.ascontiguousarray(
        W.reshape(din // 128, 128, dout // 128, 128).transpose(2, 1, 0, 3)
        .reshape(dout // 128, 128, din))


def pack_w2(W, G=GH):
    din, dout = W.shape
    ngb = din // (G * 128)
    return np.ascontiguousarray(
        W.reshape(ngb, G, 128, dout // 128, 128).transpose(0, 3, 2, 1, 4)
        .reshape(ngb, dout // 128, 128, G * 128))


def pp(vec):
    n = vec.shape[0] // 128
    return np.ascontiguousarray(vec.reshape(n, 128).T)


def make_e2():
    e2 = np.zeros((16, NPAIR * 128), np.float32)
    for p in range(NPAIR):
        e2[2 * p, p * 128:p * 128 + 64] = 1.0
        e2[2 * p + 1, p * 128 + 64:p * 128 + 128] = 1.0
    return e2


class Sem:
    def __init__(self, h, name):
        self.h = h
        self.name = name
        self.n = 0


class KB:
    def __init__(self, debug=(), ngl=NG, alim=6, moe_e=NEXP, moe_y=True, gser=False, moe_wb=True, emod=NEXP, hblk=10000):
        self.debug = set(debug)
        self.ngl = ngl
        self.alim = alim
        self.moe_e = moe_e
        self.moe_y = moe_y
        self.gser = gser
        self.moe_wb = moe_wb
        self.emod = emod
        self.hblk = hblk
        nc = bass.Bass(trn_type="TRN2", target_bir_lowering=False,
                       dynamic_dma_scratch_size=DMA_SCRATCH)
        self.nc = nc
        self.base = DMA_SCRATCH + 32
        self.L = {k: [] for k in "TVSGY"}
        self.meta = {k: [] for k in "TVSGY"}
        self._decl_dram()
        self._decl_sbuf()
        self.ps = [nc.alloc_psum_tensor(f"ps{i}", [128, 512], F32) for i in range(8)]
        mk = lambda n: Sem(nc.alloc_semaphore(n), n)
        self.s_lx = [mk("lx0"), mk("lx1")]
        self.s_w1 = [mk("w1a"), mk("w1b")]
        self.s_w2 = [mk("w2a"), mk("w2b")]
        self.s_aux = mk("aux")
        self.s_sd = mk("sd")
        self.s_dbg = mk("dbg")
        self.s_eb = [mk("eb0"), mk("eb1")]
        self.s_mm = mk("mm")
        self.s_sc = mk("sc")
        self.s_ve = mk("ve")
        self.s_st = [mk("st0"), mk("st1")]
        self.s_bar = mk("bar")
        self.w1_idx = 0
        self.w1_ev = []
        self.w2_idx = 0
        self.w2_ev = []

    def _decl_dram(self):
        nc = self.nc
        dt = lambda n, s: nc.dram_tensor(n, s, F32, kind="ExternalInput").ap()
        self.xs = dt("xs", [L, D])
        self.wq_pk = dt("wq_pk", [KC, 128, D])
        self.wo_pk = dt("wo_pk", [KC, 128, D])
        self.wk_nat = dt("wk_nat", [D, D])
        self.wv_nat = dt("wv_nat", [D, D])
        self.w1_pk = dt("w1_pk", [FFC, 128, D])
        self.w2_pk = dt("w2_pk", [NGRP, KC, 128, GH * 128])
        self.ew1_pk = dt("ew1_pk", [NEXP, FFC, 128, D])
        self.ew2_pk = dt("ew2_pk", [NEXP, NGRP, KC, 128, GH * 128])
        self.bq_pp = dt("bq_pp", [128, KC])
        self.bo_pp = dt("bo_pp", [128, KC])
        self.g1_pp = dt("g1_pp", [128, KC])
        self.be1_pp = dt("be1_pp", [128, KC])
        self.g2_pp = dt("g2_pp", [128, KC])
        self.be2_pp = dt("be2_pp", [128, KC])
        self.b2a_pp = dt("b2a_pp", [128, KC])
        self.b1_pp = dt("b1_pp", [128, FFC])
        self.eb1_pp = dt("eb1_pp", [128, NEXP * FFC])
        self.bk_row = dt("bk_row", [1, D])
        self.bv_row = dt("bv_row", [1, D])
        self.eb2_rows = dt("eb2_rows", [NEXP, D])
        self.ident_c = dt("ident_c", [128, 128])
        self.onescol_c = dt("onescol_c", [128, 1])
        self.onesrow_c = dt("onesrow_c", [1, 128])
        self.e2_c = dt("e2_c", [16, NPAIR * 128])
        self.mkc_c = dt("mkc_c", [NEXP, 2])
        self.nwc_c = dt("nwc_c", [128, 2])
        self.ecol_c = dt("ecol_c", [NEXP, 1])
        self.out = nc.dram_tensor("out", [TOWN, D], F32, kind="ExternalOutput").ap()
        self.dbg = {}
        for name, shape in (("xnbf", [128, KC * TOWN]), ("msb", [128, NPAIR * 128]),
                            ("ksbd", [128, NPAIR * 16]), ("wrow", [NEXP, TOWN]),
                            ("accm", [128, KC * TOWN]), ("x1n", [128, KC * TOWN]),
                            ("accf", [128, KC * TOWN]), ("hall", [128, NGOWN]),
                            ("ktm", [128, 2 * D]), ("mstat", [128, NGOWN]),
                            ("rstat", [128, NGOWN]), ("x1_pre", [128, KC * TOWN]),
                            ("qsb", [128, KC * TOWN]), ("cbf", [128, KC * TOWN])):
            if name in self.debug:
                self.dbg[name] = nc.dram_tensor(
                    "dbg_" + name, shape, F32, kind="ExternalOutput").ap()

    def _sb(self, name, shape, dtype, off, region_end=None):
        per = int(np.prod(shape[1:])) * mybir.dt.size(dtype)
        off = self.base + off
        assert off % 32 == 0, (name, off)
        assert off + per <= SBUF_TOP, (name, off, per, off + per)
        if region_end is not None:
            assert off + per <= self.base + region_end, (name, off + per, region_end)
        return self.nc.alloc_sbuf_tensor_at(name, list(shape), dtype, offset=off)

    def _decl_sbuf(self):
        sb = self._sb
        self.wk_sb = sb("wk_sb", [128, KC * D], BF16, 0, 65536)
        self.wv_sb = sb("wv_sb", [128, KC * D], BF16, 16384, 65536)
        self.kvbias = sb("kvbias", [1, 2 * D], BF16, 32768, 65536)
        self.xnp = sb("xnp", [128, 2 * KC * 128], BF16, 36864, 65536)
        self.stat48 = sb("stat48", [48, 128], F32, 40960, 65536)
        self.kvacc = sb("kvacc", [128, NPAIR * 128], F32, 45056, 65536)
        self.ksacc = sb("ksacc", [128, NPAIR], F32, 49600, 65536)
        self.h_m = [sb("h_lo", [128, GH * TOWN], BF16, 0, 65536),
                    sb("h_hi", [128, GH * TOWN], BF16, 32768, 65536)]
        self.x1_sb = sb("x1_sb", [128, KC * TOWN], F32, 0, 65536)
        self.acc = sb("acc", [128, KC * TOWN], F32, 65536, 131072)
        XB = 131072
        self.xnbf = sb("xnbf", [128, KC * TOWN], BF16, XB, 163840)
        self.xnfm = sb("xnfm", [128, KC * 1024], F32, XB, 163840)
        self.c_bf = sb("c_bf", [128, KC * TOWN], BF16, XB, 163840)
        TR = 163840
        self.xtm_a = sb("xtm_a", [128, 2 * D], F32, TR, 196608)
        self.ktm = sb("ktm", [128, 2 * D], BF16, TR + 8192, 196608)
        self.vtm = sb("vtm", [128, 2 * D], BF16, TR + 12288, 196608)
        self.elu_a = sb("elu_a", [128, 2 * D], F32, TR + 16384, 196608)
        self.hrow = sb("hrow", [1, TOWN], F32, TR, 196608)
        self.h8 = sb("h8", [NEXP, TOWN], F32, TR + 8192, 196608)
        self.wscr = sb("wscr", [NEXP, TOWN], F32, TR + 16384, 196608)
        self.wrow_f = sb("wrow_f", [NEXP, TOWN], F32, TR + 24576, 196608)
        self.wb = sb("wb", [128, 2 * TOWN], BF16, TR, 196608)
        self.wrow_bf = sb("wrow_bf", [NEXP, TOWN], BF16, TR + 8192, 196608)
        self.wtmp = sb("wtmp", [1, 2 * TOWN], BF16, TR + 12288, 196608)
        self.eb2cur = sb("eb2cur", [1, 2 * D], BF16, TR + 20480, 196608)
        self.q_sb = sb("q_sb", [128, KC * TOWN], BF16, TR, 196608)
        self.ln_m = sb("ln_m", [1, TOWN], F32, TR, 196608)
        self.ln_r = sb("ln_r", [1, TOWN], F32, TR + 8192, 196608)
        self.ln_bf = sb("ln_bf", [128, 4 * 512], BF16, TR + 16384, 196608)
        self.ln_t = sb("ln_t", [128, 2 * 512], F32, TR + 20480, 196608)
        self.ln_s1 = sb("ln_s1", [1, 512], F32, TR + 24576, 196608)
        self.ln_s2 = sb("ln_s2", [1, 512], F32, TR + 26624, 196608)
        self.ln_s3 = sb("ln_s3", [1, 512], F32, TR + 28672, 196608)
        self.h_b2 = sb("h_b2", [128, GH * TOWN], BF16, TR, 196608)
        self.xtm_b4 = sb("xtm_b4", [128, 4 * D], F32, TR, 196608)
        self.otm_b4 = sb("otm_b4", [128, 4 * D], F32, TR + 16384, 196608)
        ST = 196608
        self.w1s = sb("w1s", [128, 2 * KC * 128], BF16, ST, 204800)
        self.w2s = sb("w2s", [128, 2 * GH * 128], BF16, ST + 4096, 204800)
        self.zbfb = sb("zbfb", [16, 512], BF16, ST + 4096, 204800)
        self.zb = sb("zb", [128, 2 * 512], BF16, ST + 5120, 204800)
        self.xt_b1 = sb("xt_b1", [128, 512], F32, ST + 6144, 204800)
        SM = 204800
        self.ident = sb("ident", [128, 128], F32, SM)
        self.onescol_bf = sb("onescol_bf", [128, 1], BF16, SM + 512)
        self.onesrow_bf = sb("onesrow_bf", [1, 128], BF16, SM + 544)
        self.onesrow_f = sb("onesrow_f", [1, 128], F32, SM + 800)
        self.bq_sb = sb("bq_sb", [128, KC], F32, SM + 1312)
        self.bo_sb = sb("bo_sb", [128, KC], F32, SM + 1344)
        self.g1_sb = sb("g1_sb", [128, KC], F32, SM + 1376)
        self.be1_sb = sb("be1_sb", [128, KC], F32, SM + 1408)
        self.g2_sb = sb("g2_sb", [128, KC], F32, SM + 1440)
        self.be2_sb = sb("be2_sb", [128, KC], F32, SM + 1472)
        self.b2a_sb = sb("b2a_sb", [128, KC], F32, SM + 1504)
        self.b1_sb = sb("b1_sb", [128, FFC], BF16, SM + 1536)
        self.eb1_sb = sb("eb1_sb", [128, NEXP * FFC], BF16, SM + 1600)
        self.mstat = sb("mstat", [128, NGOWN], F32, SM + 2112)
        self.rstat = sb("rstat", [128, NGOWN], F32, SM + 2176)
        self.ksbd = sb("ksbd", [128, NPAIR * 16], BF16, SM + 2240)
        self.m_sb = sb("m_sb", [128, NPAIR * 128], BF16, SM + 2496)
        self.e2_sb = sb("e2_sb", [16, NPAIR * 128], BF16, SM + 4544)
        self.dum = sb("dum", [128, 8], F32, SM + 6624)
        SZ = SM + 6656
        self.elu_r = sb("elu_r", [128, 512], BF16, SZ)
        self.tpack = sb("tpack", [128, 128], F32, SZ)
        self.xn01 = sb("xn01", [128, 2 * NGOWN], F32, SZ + 512)
        self.hall = sb("hall", [128, NGOWN], F32, SZ + 704)
        self.scol = sb("scol", [128, 4], F32, SZ + 768)
        self.ecol_sb = sb("ecol_sb", [NEXP, 1], F32, SZ + 800)
        self.mkc_sb = sb("mkc_sb", [NEXP, 2], F32, SZ + 832)
        self.nwc_sb = sb("nwc_sb", [128, 2], F32, SM + 6592)
        assert SZ + 1024 <= SBUF_TOP - self.base, (SZ + 1024, SBUF_TOP - self.base)

    def em(self, e, fn):
        self.L[e].append(fn)
        self.meta[e].append(("op", None, 0))

    def wait(self, e, sem, val):
        if val > 0:
            self.L[e].append(lambda x, s=sem.h, v=val: x.wait_ge(s, v))
            self.meta[e].append(("wait", sem.name, val))

    def inc(self, e, sem, n=1):
        prev = self.L[e].pop()
        sem.n += n
        def fn(x, prev=prev, s=sem.h, n=n):
            prev(x).then_inc(s, n)
        self.L[e].append(fn)
        self.meta[e][-1] = ("op", sem.name, n)
        return sem.n

    def static_check(self):
        pos = {e: 0 for e in "TVSGY"}
        sems = {}
        while True:
            progress = False
            done = True
            for e in "TVSGY":
                while pos[e] < len(self.meta[e]):
                    kind, sem, val = self.meta[e][pos[e]]
                    if kind == "wait" and sems.get(sem, 0) < val:
                        break
                    if kind == "op" and sem is not None:
                        sems[sem] = sems.get(sem, 0) + val
                    pos[e] += 1
                    progress = True
                if pos[e] < len(self.meta[e]):
                    done = False
            if done:
                return
            if not progress:
                msg = []
                for e in "TVSGY":
                    if pos[e] < len(self.meta[e]):
                        k, sem, val = self.meta[e][pos[e]]
                        msg.append(f"{e}@{pos[e]}: wait {sem}>={val} (cur {sems.get(sem, 0)})")
                raise RuntimeError("STATIC DEADLOCK: " + "; ".join(msg))

    def chain(self, e, sem):
        v = self.inc(e, sem, 1)
        self.wait(e, sem, v)
        return v

    def vbar(self):
        return self.chain("V", self.s_ve)

    def dma(self, e, out, in_, sem):
        self.em(e, lambda x, o=out, i=in_: x.dma_start(out=o, in_=i))
        return self.inc(e, sem, 16)

    def barrier(self):
        self.em("T", lambda x: x.nop())
        self.inc("T", self.s_bar, 1)
        self.em("V", lambda x: x.memset(self.dum[:, 0:2], 0.0))
        self.inc("V", self.s_bar, 1)
        self.em("S", lambda x: x.memzero(self.dum[:, 2:4]))
        self.inc("S", self.s_bar, 1)
        self.em("G", lambda x: x.memset(self.dum[:, 4:6], 0.0))
        self.inc("G", self.s_bar, 1)
        tgt = self.s_bar.n
        for e in "TVSGY":
            self.wait(e, self.s_bar, tgt)

    def w1_load(self, src):
        i = self.w1_idx
        self.w1_idx += 1
        slot = i % 2
        if i >= 2:
            sem, val = self.w1_ev[i - 2]
            self.wait("G", sem, val)
        if self.gser:
            self.wait("G", self.s_w1[slot], self.s_w1[slot].n)
        v = self.dma("G", self.w1s[:, slot * 1024:(slot + 1) * 1024], src, self.s_w1[slot])
        return slot, v

    def w1t(self, slot, k):
        return self.w1s[:, slot * 1024 + k * 128: slot * 1024 + (k + 1) * 128]

    def w2_load(self, src):
        i = self.w2_idx
        self.w2_idx += 1
        slot = i % 2
        if i >= 2:
            sem, val = self.w2_ev[i - 2]
            self.wait("G", sem, val)
        v = self.dma("G", self.w2s[:, slot * 1024:(slot + 1) * 1024], src, self.s_w2[slot])
        return slot, v

    def w2t(self, slot, j):
        return self.w2s[:, slot * 1024 + j * 128: slot * 1024 + (j + 1) * 128]

    def _dump(self, name, src, cast=False):
        if name not in self.dbg:
            return
        if cast:
            sem = self.s_dbg
            self.dma("G", self.dbg[name], src[:], sem)
        else:
            sem = self.s_sd
            self.wait("Y", sem, sem.n)
            self.dma("Y", self.dbg[name], src[:], sem)
        for e in "TVSG":
            self.wait(e, sem, sem.n)

    def emit_aux(self):
        d = self.dma
        for k in range(KC):
            d("G", self.wk_sb[:, k * D:(k + 1) * D],
              self.wk_nat[k * 128:(k + 1) * 128, :], self.s_aux)
            d("G", self.wv_sb[:, k * D:(k + 1) * D],
              self.wv_nat[k * 128:(k + 1) * 128, :], self.s_aux)
        d("G", self.kvbias[0:1, 0:D], self.bk_row, self.s_aux)
        d("G", self.kvbias[0:1, D:2 * D], self.bv_row, self.s_aux)
        for dst, src in ((self.ident, self.ident_c), (self.onescol_bf, self.onescol_c),
                         (self.onesrow_bf, self.onesrow_c), (self.onesrow_f, self.onesrow_c),
                         (self.bq_sb, self.bq_pp), (self.bo_sb, self.bo_pp),
                         (self.g1_sb, self.g1_pp), (self.be1_sb, self.be1_pp),
                         (self.g2_sb, self.g2_pp), (self.be2_sb, self.be2_pp),
                         (self.b2a_sb, self.b2a_pp), (self.b1_sb, self.b1_pp),
                         (self.eb1_sb, self.eb1_pp), (self.ecol_sb, self.ecol_c),
                         (self.e2_sb, self.e2_c), (self.mkc_sb, self.mkc_c),
                         (self.nwc_sb, self.nwc_c)):
            d("G", dst[:], src, self.s_aux)
        self.aux_tgt = self.s_aux.n
        for e in "TVS":
            self.wait(e, self.s_aux, self.aux_tgt)
        self.em("V", lambda v: v.memset(self.m_sb[:], 0.0))
        self.em("V", lambda v: v.memset(self.ksbd[:], 0.0))
        self.inc("V", self.s_ve, 1)

    def emit_phase_a(self):
        ps = self.ps
        NGL = self.ngl
        evT = {}; evKV = {}; evKVmm = {}; evA = {}; evXN = {}
        evK_s = {}; evK_v = {}; evV_v = {}; evDMA = {}
        evXNw = 0
        tpieces, tailguard = self.tail_pieces(lambda: evA[NGOWN - 1])
        tp_sched = {17: 0, 18: 1, 19: 2, 20: 3, 21: 4}
        tp_done = [0]

        def lhs(c, g):
            own = g < NGOWN
            if own:
                return self.xnbf[:, c * TOWN + g * 128:c * TOWN + g * 128 + 128]
            return self.xnp[:, ((g % 2) * KC + c) * 128:((g % 2) * KC + c + 1) * 128]

        def emit_stats(g, mid_v=None, mid_s=None):
            own = g < NGOWN
            par = g % 2
            xt = self.xtm_a[:, par * D:(par + 1) * D]
            if g >= 2:
                self.wait("Y", self.s_mm, evT[g - 2])
            evDMA[g] = self.dma("Y", xt, self.xs[g * 128:(g + 1) * 128, :], self.s_lx[par])
            mc = self.mstat[:, g:g + 1] if own else self.scol[:, 0:1]
            rc = self.rstat[:, g:g + 1] if own else self.scol[:, 1:2]
            ac = self.scol[:, 2:3]
            sq = self.elu_a[:, par * D:(par + 1) * D]
            self.wait("V", self.s_lx[par], evDMA[g])
            self.em("V", lambda v, xt=xt, mc=mc: v.reduce_sum(mc, xt, axis=AXX))
            self.vbar()
            self.em("V", lambda v, mc=mc: v.tensor_scalar_mul(mc, mc, -INV_D))
            v_negm = self.inc("V", self.s_ve, 1)
            self.wait("S", self.s_ve, v_negm)
            if g >= 2:
                self.wait("S", self.s_mm, evKVmm.get(g - 2, 0))
            self.em("S", lambda s, xt=xt, sq=sq, mc=mc: s.activation(sq, xt, AF.Square, bias=mc))
            self.chain("S", self.s_sc)
            self.em("S", lambda s, xt=xt, mc=mc: s.add(xt, xt, mc))
            s_sub = self.inc("S", self.s_sc, 1)
            if mid_v is not None:
                mid_v()
            self.wait("V", self.s_sc, s_sub)
            self.em("V", lambda v, sq=sq, ac=ac: v.reduce_sum(ac, sq, axis=AXX))
            self.vbar()
            self.em("V", lambda v, ac=ac: v.tensor_scalar(ac, ac, INV_D, EPS_LN, op0=OP.mult, op1=OP.add))
            v_ac = self.inc("V", self.s_ve, 1)
            if mid_s is not None:
                mid_s()
            r0 = self.scol[:, 3:4]
            self.wait("S", self.s_ve, v_ac)
            self.em("S", lambda s, ac=ac, r0=r0: s.activation(r0, ac, AF.Ln))
            self.chain("S", self.s_sc)
            self.em("S", lambda s, r0=r0: s.activation(r0, r0, AF.Exp, scale=-0.5))
            self.chain("S", self.s_sc)
            self.em("S", lambda s, rc=rc, r0=r0: s.mul(rc, r0, r0))
            self.chain("S", self.s_sc)
            self.em("S", lambda s, rc=rc, ac=ac: s.mul(rc, rc, ac))
            self.chain("S", self.s_sc)
            self.em("S", lambda s, rc=rc: s.mul(rc, rc, self.nwc_sb[:, 0:1]))
            self.chain("S", self.s_sc)
            self.em("S", lambda s, rc=rc: s.add(rc, rc, self.nwc_sb[:, 1:2]))
            self.chain("S", self.s_sc)
            self.em("S", lambda s, rc=rc, r0=r0: s.mul(rc, rc, r0))
            self.chain("S", self.s_sc)
            self.em("S", lambda s, xt=xt, rc=rc: s.mul(xt, xt, rc))
            if own:
                self.chain("S", self.s_sc)
                self.em("S", lambda s, xt=xt, g=g: s.copy(self.xn01[:, g:g + 1], xt[:, 0:1]))
                self.chain("S", self.s_sc)
                self.em("S", lambda s, xt=xt, g=g: s.copy(
                    self.xn01[:, NGOWN + g:NGOWN + g + 1], xt[:, 1:2]))
            evA[g] = self.inc("S", self.s_sc, 1)

        def emit_T(g):
            nonlocal evXNw
            own = g < NGOWN
            par = g % 2
            xt = self.xtm_a[:, par * D:(par + 1) * D]
            self.wait("T", self.s_sc, evA[g])
            for wave in range(2):
                self.wait("T", self.s_sc, evXNw)
                if tailguard[0]:
                    self.wait("T", self.s_ve, tailguard[0])
                    tailguard[0] = 0
                for j in range(4):
                    c = wave * 4 + j
                    self.em("T", lambda t, xt=xt, c=c, j=j: t.transpose(
                        ps[4][:, j * 128:(j + 1) * 128], xt[:, c * 128:(c + 1) * 128],
                        self.ident[:]))
                mmv = self.inc("T", self.s_mm, 1)
                if wave == 1:
                    evT[g] = mmv
                self.wait("S", self.s_mm, mmv)
                if (not own) and g >= 2:
                    self.wait("S", self.s_mm, evKVmm.get(g - 2, 0))
                for j in range(4):
                    c = wave * 4 + j
                    self.em("S", lambda s, dst=lhs(c, g), j=j: s.copy(dst, ps[4][:, j * 128:(j + 1) * 128]))
                evXNw = self.inc("S", self.s_sc, 1)
            evXN[g] = evXNw
            self.wait("T", self.s_sc, evXN[g])
            if g >= 1:
                self.wait("T", self.s_sc, evK_s[g - 1])
                self.wait("T", self.s_ve, evK_v[g - 1])
            for k in range(KC):
                for hh in range(2):
                    self.em("T", lambda t, k=k, hh=hh, lh=lhs(k, g): t.matmul(
                        ps[hh][:], lh, self.wk_sb[:, k * D + hh * 512:k * D + (hh + 1) * 512],
                        start=(k == 0), stop=False))
            for hh in range(2):
                self.em("T", lambda t, hh=hh: t.matmul(
                    ps[hh][:], self.onesrow_bf[:], self.kvbias[0:1, hh * 512:(hh + 1) * 512],
                    start=False, stop=True))
            evKmm = self.inc("T", self.s_mm, 1)
            if g >= 1:
                self.wait("T", self.s_ve, evV_v[g - 1])
            for k in range(KC):
                for hh in range(2):
                    self.em("T", lambda t, k=k, hh=hh, lh=lhs(k, g): t.matmul(
                        ps[2 + hh][:], lh, self.wv_sb[:, k * D + hh * 512:k * D + (hh + 1) * 512],
                        start=(k == 0), stop=False))
            for hh in range(2):
                self.em("T", lambda t, hh=hh: t.matmul(
                    ps[2 + hh][:], self.onesrow_bf[:],
                    self.kvbias[0:1, D + hh * 512:D + (hh + 1) * 512],
                    start=False, stop=True))
            evVmm = self.inc("T", self.s_mm, 1)
            return evKmm, evVmm

        def elu_vmax(g, evKmm):
            par = g % 2
            rt = self.elu_a[:, par * D:(par + 1) * D]
            self.wait("V", self.s_mm, evKmm)
            if g >= 2:
                self.wait("V", self.s_mm, evKVmm.get(g - 2, 0))
            for hh in range(2):
                self.em("V", lambda v, hh=hh, rt=rt: v.tensor_scalar(
                    rt[:, hh * 512:(hh + 1) * 512], ps[hh][:], 0.0, None, op0=OP.max))
            evK_v[g] = self.inc("V", self.s_ve, 1)

        def elu_sexp(g, evKmm):
            par = g % 2
            kslot = self.ktm[:, par * D:(par + 1) * D]
            self.wait("S", self.s_mm, evKmm)
            self.wait("S", self.s_ve, evK_v[g])
            if g >= 2:
                self.wait("S", self.s_mm, evKVmm.get(g - 2, 0))
            for hh in range(2):
                self.em("S", lambda s, hh=hh, kslot=kslot: s.activation(
                    kslot[:, hh * 512:(hh + 1) * 512], ps[hh][:], AF.Exp))
            evK_s[g] = self.inc("S", self.s_sc, 1)

        def elu_vtail(g, evVmm):
            par = g % 2
            kslot = self.ktm[:, par * D:(par + 1) * D]
            vslot = self.vtm[:, par * D:(par + 1) * D]
            rt = self.elu_a[:, par * D:(par + 1) * D]
            self.wait("V", self.s_sc, evK_s[g])
            for hh in range(2):
                self.em("V", lambda v, hh=hh, kslot=kslot, rt=rt: v.scalar_tensor_tensor(
                    kslot[:, hh * 512:(hh + 1) * 512], kslot[:, hh * 512:(hh + 1) * 512],
                    1.0, rt[:, hh * 512:(hh + 1) * 512], op0=OP.min, op1=OP.add))
            self.vbar()
            self.wait("V", self.s_mm, evVmm)
            for hh in range(2):
                self.em("V", lambda v, hh=hh, vslot=vslot: v.tensor_copy(
                    vslot[:, hh * 512:(hh + 1) * 512], ps[2 + hh][:]))
            evV_v[g] = self.inc("V", self.s_ve, 1)

        def emit_kv(g):
            par = g % 2
            kslot = self.ktm[:, par * D:(par + 1) * D]
            vslot = self.vtm[:, par * D:(par + 1) * D]
            self.wait("T", self.s_ve, evV_v[g])
            if g >= 1:
                self.wait("T", self.s_ve, evKV[g - 1])
            for pr in range(NPAIR):
                b = 5 + pr // 4
                o = pr % 4
                self.em("T", lambda t, pr=pr, b=b, o=o, kslot=kslot, vslot=vslot: t.matmul(
                    ps[b][:, o * 128:(o + 1) * 128], kslot[:, pr * 128:(pr + 1) * 128],
                    vslot[:, pr * 128:(pr + 1) * 128], start=True, stop=True))
                self.em("T", lambda t, pr=pr, kslot=kslot: t.matmul(
                    ps[7][:, pr:pr + 1], kslot[:, pr * 128:(pr + 1) * 128],
                    self.onescol_bf[:], start=True, stop=True))
            evKVmm[g] = self.inc("T", self.s_mm, 1)
            self.wait("V", self.s_mm, evKVmm[g])
            if g >= 1:
                self.wait("V", self.s_ve, evKV[g - 1])
            if g == 0:
                self.em("V", lambda v: v.tensor_copy(self.kvacc[:, 0:512], ps[5][:]))
                self.em("V", lambda v: v.tensor_copy(self.kvacc[:, 512:1024], ps[6][:]))
                self.em("V", lambda v: v.tensor_copy(self.ksacc[:], ps[7][:, 0:NPAIR]))
            else:
                self.em("V", lambda v: v.tensor_add(self.kvacc[:, 0:512], self.kvacc[:, 0:512], ps[5][:]))
                self.em("V", lambda v: v.tensor_add(self.kvacc[:, 512:1024], self.kvacc[:, 512:1024], ps[6][:]))
                self.em("V", lambda v: v.tensor_add(self.ksacc[:], self.ksacc[:], ps[7][:, 0:NPAIR]))
            evKV[g] = self.inc("V", self.s_ve, 1)

        pend = []
        for g in range(NGL):
            if pend:
                pg, ek, ev = pend.pop(0)
                emit_stats(g, mid_v=lambda pg=pg, ek=ek: elu_vmax(pg, ek),
                           mid_s=lambda pg=pg, ek=ek, ev=ev: (elu_sexp(pg, ek),
                                                             elu_vtail(pg, ev)))
                emit_kv(pg)
            else:
                emit_stats(g)
            ek, ev = emit_T(g)
            pend.append((g, ek, ev))
            if g >= NGOWN + 1 and tp_done[0] < len(tpieces) and tp_done[0] <= tp_sched.get(g, -1):
                tpieces[tp_done[0]]()
                tp_done[0] += 1
        pg, ek, ev = pend.pop(0)
        elu_vmax(pg, ek)
        elu_sexp(pg, ek)
        elu_vtail(pg, ev)
        emit_kv(NGL - 1)
        self.evKV_last = evKV[NGL - 1]
        self.wait("Y", self.s_mm, evKVmm[NGL - 1])
        self.wait("Y", self.s_ve, evV_v[NGL - 1])
        self.wait("Y", self.s_sc, evA[NGL - 1])
        while tp_done[0] < len(tpieces):
            tpieces[tp_done[0]]()
            tp_done[0] += 1

    def tail_pieces(self, evA15f):
        ps = self.ps
        u = self.hall[:]
        t1 = self.xn01[:, 0:NGOWN]
        t2 = self.xn01[:, NGOWN:2 * NGOWN]
        guard = [0]
        ev = {}

        def p_hash1():
            self.wait("V", self.s_sc, evA15f())
            self.em("V", lambda v: v.tensor_add(u, t1, t2))
            self.vbar()
            for bnd in (64.0, 32.0):
                self.em("V", lambda v, b=bnd: v.tensor_single_scalar(t1, u, b, OP.is_ge))
                self.vbar()
                self.em("V", lambda v, b=bnd: v.scalar_tensor_tensor(u, t1, -b, u, op0=OP.mult, op1=OP.add))
                self.vbar()
                self.em("V", lambda v, b=bnd: v.tensor_single_scalar(t1, u, -b, OP.is_le))
                self.vbar()
                self.em("V", lambda v, b=bnd: v.scalar_tensor_tensor(u, t1, b, u, op0=OP.mult, op1=OP.add))
                self.vbar()

        def p_hash2():
            for bnd in (16.0, 8.0):
                self.em("V", lambda v, b=bnd: v.tensor_single_scalar(t1, u, b, OP.is_ge))
                self.vbar()
                self.em("V", lambda v, b=bnd: v.scalar_tensor_tensor(u, t1, -b, u, op0=OP.mult, op1=OP.add))
                self.vbar()
                self.em("V", lambda v, b=bnd: v.tensor_single_scalar(t1, u, -b, OP.is_le))
                self.vbar()
                self.em("V", lambda v, b=bnd: v.scalar_tensor_tensor(u, t1, b, u, op0=OP.mult, op1=OP.add))
                self.vbar()

        def p_hash3():
            self.em("V", lambda v: v.tensor_single_scalar(t1, u, 1.0, OP.is_ge))
            self.vbar()
            for kk in range(2, 8):
                self.em("V", lambda v, k=float(kk): v.tensor_single_scalar(t2, u, k, OP.is_ge))
                self.vbar()
                self.em("V", lambda v: v.tensor_add(t1, t1, t2))
                self.vbar()

        def p_hash4():
            for kk in range(1, 8):
                self.em("V", lambda v, k=float(-kk): v.tensor_single_scalar(t2, u, k, OP.is_le))
                self.vbar()
                self.em("V", lambda v: v.tensor_sub(t1, t1, t2))
                self.vbar()
            self.em("V", lambda v: v.tensor_single_scalar(t2, t1, 0.0, OP.is_lt))
            self.vbar()
            self.em("V", lambda v: v.scalar_tensor_tensor(u, t2, 8.0, t1, op0=OP.mult, op1=OP.add))
            self.vbar()

        def p_pack():
            self.em("V", lambda v: v.tensor_copy(self.tpack[:, 0:NGOWN], self.mstat[:]))
            self.em("V", lambda v: v.tensor_copy(self.tpack[:, NGOWN:2 * NGOWN], self.rstat[:]))
            self.em("V", lambda v: v.tensor_copy(self.tpack[:, 2 * NGOWN:3 * NGOWN], self.hall[:]))
            vp = self.inc("V", self.s_ve, 1)
            self.wait("T", self.s_ve, vp)
            self.em("T", lambda t: t.transpose(ps[4][:, 0:128], self.tpack[:], self.ident[:]))
            mt = self.inc("T", self.s_mm, 1)
            self.wait("V", self.s_mm, mt)
            self.em("V", lambda v: v.tensor_copy(self.stat48[0:16, :], ps[4][32:48, 0:128]))
            ev["stat48"] = self.inc("V", self.s_ve, 1)
            guard[0] = ev["stat48"]

        def p_bcast():
            self.wait("Y", self.s_ve, ev["stat48"])
            self.dma("Y", self.hrow[0:1, :], self.stat48[0:16, :], self.s_sd)
            for e in range(NEXP):
                self.wait("Y", self.s_sd, self.s_sd.n)
                self.dma("Y", self.h8[e:e + 1, :], self.hrow[0:1, :], self.s_sd)
            ev["sd"] = self.s_sd.n

        def p_wrow():
            self.wait("V", self.s_sd, ev["sd"])
            self.em("V", lambda v: v.tensor_scalar(self.h8[:], self.h8[:], self.ecol_sb[:], None,
                                                   op0=OP.subtract))
            vd = self.inc("V", self.s_ve, 1)
            self.wait("S", self.s_ve, vd)
            self.em("S", lambda s: s.activation(self.wscr[:], self.h8[:], AF.Abs))
            self.chain("S", self.s_sc)
            self.em("S", lambda s: s.activation(self.wrow_f[:], self.wscr[:], AF.Relu, scale=-1.0, bias=self.mkc_sb[:, 0:1]))
            self.chain("S", self.s_sc)
            self.em("S", lambda s: s.activation(self.wscr[:], self.h8[:], AF.Abs, bias=self.mkc_sb[:, 0:1]))
            self.chain("S", self.s_sc)
            self.em("S", lambda s: s.activation(self.wscr[:], self.wscr[:], AF.Relu, scale=-1.0, bias=self.mkc_sb[:, 0:1]))
            s1 = self.inc("S", self.s_sc, 1)
            self.wait("V", self.s_sc, s1)
            self.em("V", lambda v: v.tensor_add(self.wrow_f[:], self.wrow_f[:], self.wscr[:]))
            v1 = self.inc("V", self.s_ve, 1)
            self.wait("S", self.s_ve, v1)
            self.em("S", lambda s: s.activation(self.wscr[:], self.h8[:], AF.Abs, bias=self.mkc_sb[:, 1:2]))
            self.chain("S", self.s_sc)
            self.em("S", lambda s: s.activation(self.wscr[:], self.wscr[:], AF.Relu, scale=-1.0, bias=self.mkc_sb[:, 0:1]))
            s2 = self.inc("S", self.s_sc, 1)
            self.wait("V", self.s_sc, s2)
            self.em("V", lambda v: v.tensor_add(self.wrow_f[:], self.wrow_f[:], self.wscr[:]))
            self.vbar()
            self.em("V", lambda v: v.tensor_scalar_mul(self.wrow_f[:], self.wrow_f[:], 0.5))
            self.vbar()
            self.em("V", lambda v: v.tensor_copy(self.wrow_bf[:], self.wrow_f[:]))
            self.wrow_ev = self.inc("V", self.s_ve, 1)

        return [p_hash1, p_hash2, p_hash3, p_hash4, p_pack, p_bcast, p_wrow], guard

    def emit_phase_a_tail(self):
        ps = self.ps
        self.wait("V", self.s_ve, self.evKV_last)
        for pr in range(NPAIR):
            self.em("V", lambda v, pr=pr: v.tensor_copy(
                self.m_sb[0:64, pr * 128:pr * 128 + 64],
                self.kvacc[0:64, pr * 128:pr * 128 + 64]))
            self.em("V", lambda v, pr=pr: v.tensor_copy(
                self.m_sb[64:128, pr * 128 + 64:pr * 128 + 128],
                self.kvacc[64:128, pr * 128 + 64:pr * 128 + 128]))
            self.em("V", lambda v, pr=pr: v.tensor_copy(
                self.ksbd[0:64, pr * 16 + 2 * pr:pr * 16 + 2 * pr + 1],
                self.ksacc[0:64, pr:pr + 1]))
            self.em("V", lambda v, pr=pr: v.tensor_copy(
                self.ksbd[64:128, pr * 16 + 2 * pr + 1:pr * 16 + 2 * pr + 2],
                self.ksacc[64:128, pr:pr + 1]))

    def emit_moe(self):
        ps = self.ps
        regA_ev = [0, 0]
        hEv_prev = [0]
        wbEv = [0, 0]
        wtmpRead = [0, 0]
        hW_last = {}
        yEv_prev = [0]
        ebv_of = {}

        def emit_wb(e):
            sl = e % 2
            self.wait("Y", self.s_ve, self.wrow_ev)
            if e >= 2:
                self.wait("Y", self.s_mm, wtmpRead[sl])
            self.wait("Y", self.s_sd, self.s_sd.n)
            self.dma("Y", self.wtmp[0:1, sl * TOWN:(sl + 1) * TOWN],
                     self.wrow_bf[e:e + 1, :], self.s_sd)
            sdv = self.s_sd.n
            if e >= 2:
                self.wait("G", self.s_mm, wtmpRead[sl])
            ebv_of[e] = self.dma("G", self.eb2cur[0:1, sl * D:(sl + 1) * D],
                                 self.eb2_rows[e:e + 1, :], self.s_eb[sl])
            self.wait("T", self.s_sd, sdv)
            self.wait("T", self.s_sc, hEv_prev[0])
            for tt in range(4):
                self.em("T", lambda t, tt=tt, sl=sl: t.matmul(
                    ps[tt][:], self.onesrow_bf[:],
                    self.wtmp[0:1, sl * TOWN + tt * 512:sl * TOWN + (tt + 1) * 512],
                    start=True, stop=True))
            mmwb = self.inc("T", self.s_mm, 1)
            self.wait("V", self.s_mm, mmwb)
            for tt in range(4):
                self.em("V", lambda v, tt=tt, sl=sl: v.tensor_copy(
                    self.wb[:, sl * TOWN + tt * 512:sl * TOWN + (tt + 1) * 512], ps[tt][:]))
            wbEv[sl] = self.inc("V", self.s_ve, 1)

        def emit_h_block(e, gg, i):
            n = gg * GH + i
            r = gg % 2
            hreg = self.h_m[r]
            slot, wv = self.w1_load(self.ew1_pk[e % self.emod, n])
            self.wait("T", self.s_w1[slot], wv)
            self.wait("T", self.s_sc, hEv_prev[0])
            if gg == 0 and i == 0:
                self.wait("T", self.s_ve, wbEv[e % 2])
            for k in range(KC):
                for tt in range(4):
                    self.em("T", lambda t, k=k, tt=tt, slot=slot: t.matmul(
                        ps[tt][:], self.w1t(slot, k),
                        self.xnbf[:, k * TOWN + tt * 512:k * TOWN + (tt + 1) * 512],
                        start=(k == 0), stop=(k == KC - 1)))
            mmH = self.inc("T", self.s_mm, 1)
            self.w1_ev.append((self.s_mm, mmH))
            self.wait("S", self.s_mm, mmH)
            if regA_ev[r]:
                self.wait("S", self.s_mm, regA_ev[r])
            for tt in range(4):
                self.em("S", lambda s, tt=tt, i=i, hreg=hreg, e=e, n=n: s.activation(
                    hreg[:, i * TOWN + tt * 512:i * TOWN + (tt + 1) * 512], ps[tt][:],
                    AF.Relu, bias=self.eb1_sb[:, e * FFC + n:e * FFC + n + 1]))
            hEv_prev[0] = self.inc("S", self.s_sc, 1)
            self.wait("V", self.s_sc, hEv_prev[0])
            if self.moe_wb:
                self.wait("V", self.s_ve, wbEv[e % 2])
                for tt in range(4):
                    self.em("V", lambda v, tt=tt, i=i, hreg=hreg, e=e: v.tensor_mul(
                        hreg[:, i * TOWN + tt * 512:i * TOWN + (tt + 1) * 512],
                        hreg[:, i * TOWN + tt * 512:i * TOWN + (tt + 1) * 512],
                        self.wb[:, (e % 2) * TOWN + tt * 512:(e % 2) * TOWN + (tt + 1) * 512]))
            else:
                self.em("V", lambda v: v.memset(self.dum[:, 0:2], 0.0))
            hW_last[(e, gg)] = self.inc("V", self.s_ve, 1)

        def emit_y_sub(e, gg, dout):
            r = gg % 2
            hreg = self.h_m[r]
            slot, wv = self.w2_load(self.ew2_pk[e, gg, dout])
            self.wait("T", self.s_w2[slot], wv)
            self.wait("T", self.s_ve, hW_last[(e, gg)])
            self.wait("T", self.s_ve, yEv_prev[0])
            if gg == NGRP - 1 and dout == 0:
                self.wait("T", self.s_eb[e % 2], ebv_of[e])
            last_plain = (gg != NGRP - 1)
            for j in range(GH):
                for tt in range(4):
                    self.em("T", lambda t, j=j, tt=tt, slot=slot, hreg=hreg, lp=last_plain: t.matmul(
                        ps[4 + tt][:], self.w2t(slot, j),
                        hreg[:, j * TOWN + tt * 512:j * TOWN + (tt + 1) * 512],
                        start=(j == 0), stop=(lp and j == GH - 1)))
            if gg == NGRP - 1:
                for tt in range(4):
                    self.em("T", lambda t, tt=tt, e=e, dout=dout: t.matmul(
                        ps[4 + tt][:],
                        self.eb2cur[0:1, (e % 2) * D + dout * 128:(e % 2) * D + (dout + 1) * 128],
                        self.wtmp[0:1, (e % 2) * TOWN + tt * 512:(e % 2) * TOWN + (tt + 1) * 512],
                        start=False, stop=True))
            mmY = self.inc("T", self.s_mm, 1)
            self.w2_ev.append((self.s_mm, mmY))
            if dout == KC - 1:
                regA_ev[r] = mmY
            if gg == NGRP - 1 and dout == GH - 1:
                wtmpRead[e % 2] = mmY
            self.wait("V", self.s_mm, mmY)
            self.wait("V", self.s_ve, yEv_prev[0])
            for tt in range(4):
                dst = self.acc[:, dout * TOWN + tt * 512:dout * TOWN + (tt + 1) * 512]
                if e == 0 and gg == 0:
                    self.em("V", lambda v, dst=dst, tt=tt: v.tensor_copy(dst, ps[4 + tt][:]))
                else:
                    self.em("V", lambda v, dst=dst, tt=tt: v.tensor_add(dst, dst, ps[4 + tt][:]))
            yEv_prev[0] = self.inc("V", self.s_ve, 1)

        prev = None
        nh = [0]
        for e in range(self.moe_e):
            if self.moe_wb:
                emit_wb(e)
            for gg in range(NGRP):
                for i in range(GH):
                    if nh[0] >= self.hblk:
                        break
                    emit_h_block(e, gg, i)
                    nh[0] += 1
                    if prev is not None and self.moe_y:
                        emit_y_sub(prev[0], prev[1], i)
                if nh[0] >= self.hblk:
                    break
                prev = (e, gg)
            if nh[0] >= self.hblk:
                break
        if self.moe_y:
            for i in range(GH):
                emit_y_sub(prev[0], prev[1], i)

    def emit_b1(self):
        ps = self.ps
        evQpar = [0, 0]
        evQs = [0, 0]
        evDen = [0]
        evZB = [0]
        evZBp = [0, 0]
        evAOZ = [0, 0]
        evAOZ_last = [0]
        aoz_done = [0, 0]
        evX1par = [0, 0]
        xf_prev = [0, 0]
        xf_evac = [0, 0]
        xn_ready = [0, 0]

        def qproj_n(s, n):
            slot, wv = self.w1_load(self.wq_pk[n])
            self.wait("T", self.s_w1[slot], wv)
            self.wait("T", self.s_ve, evQpar[n % 2])
            for k in range(KC):
                for sub in range(2):
                    self.em("T", lambda t, k=k, sub=sub, n=n, s=s, slot=slot: t.matmul(
                        ps[2 * (n % 2) + sub][:], self.w1t(slot, k),
                        self.xnbf[:, k * TOWN + s * 1024 + sub * 512:
                                  k * TOWN + s * 1024 + (sub + 1) * 512],
                        start=(k == 0), stop=(k == KC - 1)))
            mmQ = self.inc("T", self.s_mm, 1)
            self.w1_ev.append((self.s_mm, mmQ))
            for sub in range(2):
                pstile = ps[2 * (n % 2) + sub]
                qsl = self.q_sb[:, n * TOWN + s * 1024 + sub * 512:
                                n * TOWN + s * 1024 + (sub + 1) * 512]
                self.wait("V", self.s_mm, mmQ)
                self.wait("V", self.s_ve, self.s_ve.n)
                self.em("V", lambda v, p=pstile, n=n: v.tensor_scalar(
                    self.elu_r[:], p[:], self.bq_sb[:, n:n + 1], 0.0, op0=OP.add, op1=OP.max))
                vmax = self.vbar()
                self.wait("S", self.s_mm, mmQ)
                self.wait("S", self.s_ve, vmax)
                self.em("S", lambda x, p=pstile, q=qsl, n=n: x.activation(
                    q, p[:], AF.Exp, bias=self.bq_sb[:, n:n + 1]))
                sv = self.inc("S", self.s_sc, 1)
                self.wait("V", self.s_sc, sv)
                self.em("V", lambda v, q=qsl: v.scalar_tensor_tensor(
                    q, q, 1.0, self.elu_r[:], op0=OP.min, op1=OP.add))
                evQpar[n % 2] = self.inc("V", self.s_ve, 1)
            evQs[s] = evQpar[n % 2]

        def mix_sub(s, sub):
            qcol = s * 1024 + sub * 512
            self.wait("T", self.s_ve, evQs[s])
            self.wait("T", self.s_ve, evDen[0])
            if s == 1 and sub == 0:
                self.wait("T", self.s_sc, max(xf_evac))
                self.wait("S", self.s_mm, max(xf_prev))
            for p in range(NPAIR):
                self.em("T", lambda t, p=p, qcol=qcol: t.matmul(
                    ps[4][0:16, :], self.ksbd[:, p * 16:(p + 1) * 16],
                    self.q_sb[:, p * TOWN + qcol:p * TOWN + qcol + 512],
                    start=(p == 0), stop=(p == NPAIR - 1)))
            mmDen = self.inc("T", self.s_mm, 1)
            self.wait("V", self.s_mm, mmDen)
            self.em("V", lambda v: v.tensor_scalar(ps[4][0:16, :], ps[4][0:16, :],
                                                   EPS_ATTN, None, op0=OP.add))
            self.vbar()

            def _recip(v):
                with self.nc.allow_low_precision("z in bf16 (matmul operand)"):
                    return v.reciprocal(self.zbfb[:], ps[4][0:16, :])
            self.em("V", _recip)
            evDen[0] = self.inc("V", self.s_ve, 1)
            for p in range(NPAIR):
                zsl = self.zb[:, (p % 2) * 512:(p % 2 + 1) * 512]
                self.wait("T", self.s_ve, evDen[0])
                self.wait("T", self.s_sc, evZB[0])
                self.em("T", lambda t, p=p: t.matmul(
                    ps[5][:], self.e2_sb[:, p * 128:(p + 1) * 128], self.zbfb[:],
                    start=True, stop=True))
                mmZB = self.inc("T", self.s_mm, 1)
                self.wait("S", self.s_mm, mmZB)
                self.wait("S", self.s_ve, evAOZ[p % 2] if p > 1 else evAOZ_last[0])
                self.em("S", lambda x, zsl=zsl: x.copy(zsl, ps[5][:]))
                evZB[0] = self.inc("S", self.s_sc, 1)
                evZBp[p % 2] = evZB[0]
                self.wait("T", self.s_ve, evAOZ[p % 2])
                self.em("T", lambda t, p=p, qcol=qcol: t.matmul(
                    ps[6 + p % 2][:], self.m_sb[:, p * 128:(p + 1) * 128],
                    self.q_sb[:, p * TOWN + qcol:p * TOWN + qcol + 512],
                    start=True, stop=True))
                mmQM = self.inc("T", self.s_mm, 1)
                self.wait("V", self.s_mm, mmQM)
                self.wait("V", self.s_sc, evZBp[p % 2])
                self.em("V", lambda v, p=p, qcol=qcol, zsl=zsl: v.tensor_mul(
                    self.q_sb[:, p * TOWN + qcol:p * TOWN + qcol + 512],
                    ps[6 + p % 2][:], zsl))
                evAOZ[p % 2] = self.inc("V", self.s_ve, 1)
            evAOZ_last[0] = evAOZ[(NPAIR - 1) % 2]
            aoz_done[s] = max(evAOZ)

        def xnrec(s):
            self.wait("T", self.s_sc, evZB[0])
            self.wait("Y", self.s_ve, aoz_done[s])
            if s == 1:
                self.wait("S", self.s_ve, max(evX1par))
            for j in range(8):
                gown = s * 8 + j
                for fh in range(2):
                    sl = fh
                    xt = self.xt_b1[:]
                    if xf_prev[0] or xf_prev[1]:
                        self.wait("Y", self.s_mm, max(xf_prev))
                    lxv = self.dma("Y", xt,
                                   self.xs[gown * 128:(gown + 1) * 128, fh * 512:(fh + 1) * 512],
                                   self.s_lx[sl])
                    self.wait("V", self.s_lx[sl], lxv)
                    self.em("V", lambda v, xt=xt, gown=gown: v.tensor_scalar(
                        xt, xt, self.mstat[:, gown:gown + 1], self.rstat[:, gown:gown + 1],
                        op0=OP.add, op1=OP.mult))
                    vn = self.inc("V", self.s_ve, 1)
                    self.wait("T", self.s_ve, vn)
                    self.wait("T", self.s_sc, xf_evac[sl])
                    pb = 4 + fh
                    for cc in range(4):
                        self.em("T", lambda t, xt=xt, cc=cc, pb=pb: t.transpose(
                            ps[pb][:, cc * 128:(cc + 1) * 128], xt[:, cc * 128:(cc + 1) * 128],
                            self.ident[:]))
                    xf_prev[sl] = self.inc("T", self.s_mm, 1)
                    self.wait("S", self.s_mm, xf_prev[sl])
                    for cc in range(4):
                        c = fh * 4 + cc
                        self.em("S", lambda x, c=c, j=j, cc=cc, pb=pb: x.copy(
                            self.xnfm[:, c * 1024 + j * 128:c * 1024 + (j + 1) * 128],
                            ps[pb][:, cc * 128:(cc + 1) * 128]))
                    xf_evac[sl] = self.inc("S", self.s_sc, 1)
            xn_ready[s] = self.s_sc.n

        def wo_n(s, n):
            slot, wv = self.w1_load(self.wo_pk[n])
            self.wait("T", self.s_w1[slot], wv)
            self.wait("T", self.s_ve, evX1par[n % 2])
            if n == 0:
                self.wait("T", self.s_ve, aoz_done[s])
            for k in range(KC):
                for sub in range(2):
                    self.em("T", lambda t, k=k, sub=sub, n=n, s=s, slot=slot: t.matmul(
                        ps[2 * (n % 2) + sub][:], self.w1t(slot, k),
                        self.q_sb[:, k * TOWN + s * 1024 + sub * 512:
                                  k * TOWN + s * 1024 + (sub + 1) * 512],
                        start=(k == 0), stop=(k == KC - 1)))
            mmWo = self.inc("T", self.s_mm, 1)
            self.w1_ev.append((self.s_mm, mmWo))
            self.wait("V", self.s_mm, mmWo)
            self.wait("V", self.s_sc, xn_ready[s])
            for sub in range(2):
                self.em("V", lambda v, n=n, s=s, sub=sub: v.scalar_tensor_tensor(
                    self.x1_sb[:, n * TOWN + s * 1024 + sub * 512:
                               n * TOWN + s * 1024 + (sub + 1) * 512],
                    ps[2 * (n % 2) + sub][:], self.bo_sb[:, n:n + 1],
                    self.xnfm[:, n * 1024 + sub * 512:n * 1024 + (sub + 1) * 512],
                    op0=OP.add, op1=OP.add))
            evX1par[n % 2] = self.inc("V", self.s_ve, 1)

        for n in range(KC):
            qproj_n(0, n)
        for n in range(KC):
            qproj_n(1, n)
            if n == 3:
                mix_sub(0, 0)
            if n == 7:
                mix_sub(0, 1)
        xnrec(0)
        for n in range(KC):
            wo_n(0, n)
            if n == 3:
                mix_sub(1, 0)
            if n == 7:
                mix_sub(1, 1)
        xnrec(1)
        for n in range(KC):
            wo_n(1, n)

    def emit_ln(self, which):
        if which == 2:
            self.ln2_bc = {}
            self.ln2_apply = {}
        """LN over x1 (feature-major). Stats via bf16 ones-matmuls; apply with
        2-slot ln_t, sq/copies offloaded to ScalarE."""
        ps = self.ps
        g_sb = self.g1_sb if which == 1 else self.g2_sb
        be_sb = self.be1_sb if which == 1 else self.be2_sb
        evMMpar = [0, 0]
        evRowEvac = 0
        evApply = {}
        evBF = [0, 0]
        for tt in range(4):
            for c in range(KC):
                sl = c % 2
                xsl = self.x1_sb[:, c * TOWN + tt * 512:c * TOWN + (tt + 1) * 512]
                b1 = self.ln_bf[:, sl * 512:(sl + 1) * 512]
                b2 = self.ln_bf[:, (2 + sl) * 512:(3 + sl) * 512]
                self.wait("S", self.s_mm, evMMpar[sl])
                self.em("S", lambda s, b1=b1, x=xsl: s.copy(b1, x))
                self.chain("S", self.s_sc)
                self.em("S", lambda s, b2=b2, x=xsl: s.activation(b2, x, AF.Square))
                vb = self.inc("S", self.s_sc, 1)
                self.wait("T", self.s_sc, vb)
                if c == 0:
                    self.wait("T", self.s_ve, evRowEvac)
                self.em("T", lambda t, b1=b1, c=c: t.matmul(
                    ps[0][0:1, :], self.onescol_bf[:], b1, start=(c == 0), stop=(c == KC - 1)))
                self.em("T", lambda t, b2=b2, c=c: t.matmul(
                    ps[1][0:1, :], self.onescol_bf[:], b2, start=(c == 0), stop=(c == KC - 1)))
                evMMpar[sl] = self.inc("T", self.s_mm, 1)
            mmStat = evMMpar[(KC - 1) % 2]
            mrow = self.ln_m[0:1, tt * 512:(tt + 1) * 512]
            rrow = self.ln_r[0:1, tt * 512:(tt + 1) * 512]
            self.wait("V", self.s_mm, mmStat)
            self.em("V", lambda v, m=mrow: v.tensor_scalar_mul(m, ps[0][0:1, :], INV_D))
            self.em("V", lambda v: v.tensor_scalar(self.ln_s1[:], ps[1][0:1, :], INV_D, EPS_LN,
                                                   op0=OP.mult, op1=OP.add))
            self.vbar()
            self.em("V", lambda v, m=mrow: v.tensor_mul(self.ln_s2[:], m, m))
            self.vbar()
            self.em("V", lambda v: v.tensor_sub(self.ln_s1[:], self.ln_s1[:], self.ln_s2[:]))
            evRowEvac = self.inc("V", self.s_ve, 1)
            self.wait("S", self.s_ve, evRowEvac)
            self.em("S", lambda s: s.activation(self.ln_s2[:], self.ln_s1[:], AF.Ln))
            self.chain("S", self.s_sc)
            self.em("S", lambda s: s.activation(self.ln_s2[:], self.ln_s2[:], AF.Exp, scale=-0.5))
            sv = self.inc("S", self.s_sc, 1)
            self.wait("V", self.s_sc, sv)
            self.em("V", lambda v: v.tensor_mul(self.ln_s3[:], self.ln_s2[:], self.ln_s2[:]))
            self.vbar()
            self.em("V", lambda v: v.tensor_mul(self.ln_s3[:], self.ln_s3[:], self.ln_s1[:]))
            self.vbar()
            self.em("V", lambda v: v.tensor_scalar(self.ln_s3[:], self.ln_s3[:], -0.5, 1.5,
                                                   op0=OP.mult, op1=OP.add))
            self.vbar()
            self.em("V", lambda v, r=rrow: v.tensor_mul(r, self.ln_s3[:], self.ln_s2[:]))
            vr = self.inc("V", self.s_ve, 1)
            self.wait("T", self.s_ve, vr)
            self.wait("T", self.s_ve, evApply.get((tt - 1, KC - 1), 0))
            self.em("T", lambda t, m=mrow: t.matmul(ps[2][:], self.onesrow_f[:], m,
                                                    start=True, stop=True))
            self.em("T", lambda t, r=rrow: t.matmul(ps[3][:], self.onesrow_f[:], r,
                                                    start=True, stop=True))
            mmB = self.inc("T", self.s_mm, 1)
            if which == 2:
                self.ln2_bc[tt] = mmB
            self.wait("V", self.s_mm, mmB)
            stages = []
            for c in range(KC):
                lt = lambda c: self.ln_t[:, (c % 2) * 512:(c % 2 + 1) * 512]
                xs = self.x1_sb[:, c * TOWN + tt * 512:c * TOWN + (tt + 1) * 512]
                stages.append(("fin", c - 2))
                stages.append(("mul", c - 1))
                stages.append(("sub", c))
            stages += [("fin", KC - 2), ("mul", KC - 1), ("fin", KC - 1)]
            for step, c in stages:
                if c < 0:
                    continue
                sl = c % 2
                lts = self.ln_t[:, sl * 512:(sl + 1) * 512]
                xsl = self.x1_sb[:, c * TOWN + tt * 512:c * TOWN + (tt + 1) * 512]
                if step == "sub":
                    prev = (tt, c - 2) if c >= 2 else (tt - 1, KC - 2 + c)
                    self.wait("V", self.s_ve, evApply.get(prev, 0))
                    self.em("V", lambda v, x=xsl, lts=lts: v.tensor_sub(lts, x, ps[2][:]))
                    self.inc("V", self.s_ve, 1)
                elif step == "mul":
                    self.wait("V", self.s_ve, self.s_ve.n)
                    self.em("V", lambda v, lts=lts: v.tensor_mul(lts, lts, ps[3][:]))
                    self.inc("V", self.s_ve, 1)
                else:
                    self.wait("V", self.s_ve, self.s_ve.n)
                    if which == 1:
                        self.em("V", lambda v, x=xsl, lts=lts, c=c: v.tensor_scalar(
                            x, lts, g_sb[:, c:c + 1], be_sb[:, c:c + 1],
                            op0=OP.mult, op1=OP.add))
                        vx = self.inc("V", self.s_ve, 1)
                        csl = self.c_bf[:, c * TOWN + tt * 512:c * TOWN + (tt + 1) * 512]
                        self.wait("S", self.s_ve, vx)
                        self.em("S", lambda s, csl=csl, x=xsl: s.copy(csl, x))
                        self.chain("S", self.s_sc)
                        evApply[(tt, c)] = vx
                    else:
                        self.em("V", lambda v, lts=lts, c=c: v.tensor_scalar(
                            lts, lts, g_sb[:, c:c + 1], be_sb[:, c:c + 1],
                            op0=OP.mult, op1=OP.add))
                        self.vbar()
                        asl = self.acc[:, c * TOWN + tt * 512:c * TOWN + (tt + 1) * 512]
                        self.em("V", lambda v, a=asl, lts=lts: v.tensor_add(a, a, lts))
                        evApply[(tt, c)] = self.inc("V", self.s_ve, 1)
                        if c == KC - 1:
                            self.ln2_apply[tt] = evApply[(tt, c)]

    def emit_b2(self):
        ps = self.ps
        hprev_tt = [0] * 4
        hlast_tt = [0] * 4
        mmY_tt = [0] * 4
        yadd_tt = [0] * 4
        for gg in range(NGRP):
            for i in range(GH):
                n = gg * GH + i
                slot, wv = self.w1_load(self.w1_pk[n])
                self.wait("T", self.s_w1[slot], wv)
                mmH_tt = [0] * 4
                for tt in range(4):
                    self.wait("T", self.s_sc, hprev_tt[tt])
                    if i == 0:
                        self.wait("T", self.s_mm, mmY_tt[tt])
                    for k in range(KC):
                        self.em("T", lambda t, k=k, tt=tt, slot=slot: t.matmul(
                            ps[tt][:], self.w1t(slot, k),
                            self.c_bf[:, k * TOWN + tt * 512:k * TOWN + (tt + 1) * 512],
                            start=(k == 0), stop=(k == KC - 1)))
                    mmH_tt[tt] = self.inc("T", self.s_mm, 1)
                self.w1_ev.append((self.s_mm, mmH_tt[3]))
                for tt in range(4):
                    self.wait("S", self.s_mm, mmH_tt[tt])
                    self.em("S", lambda s, tt=tt, i=i, n=n: s.activation(
                        self.h_b2[:, i * TOWN + tt * 512:i * TOWN + (tt + 1) * 512],
                        ps[tt][:], AF.Relu, bias=self.b1_sb[:, n:n + 1]))
                    hprev_tt[tt] = self.inc("S", self.s_sc, 1)
                hlast_tt = list(hprev_tt)
            for dout in range(KC):
                slot, wv = self.w2_load(self.w2_pk[gg, dout])
                self.wait("T", self.s_w2[slot], wv)
                mmYc = [0] * 4
                for tt in range(4):
                    self.wait("T", self.s_sc, hlast_tt[tt])
                    self.wait("T", self.s_ve, yadd_tt[tt])
                    for j in range(GH):
                        self.em("T", lambda t, j=j, tt=tt, slot=slot: t.matmul(
                            ps[4 + tt][:], self.w2t(slot, j),
                            self.h_b2[:, j * TOWN + tt * 512:j * TOWN + (tt + 1) * 512],
                            start=(j == 0), stop=(j == GH - 1)))
                    mmYc[tt] = self.inc("T", self.s_mm, 1)
                self.w2_ev.append((self.s_mm, mmYc[3]))
                if dout == KC - 1:
                    mmY_tt = mmYc
                for tt in range(4):
                    xsl = self.x1_sb[:, dout * TOWN + tt * 512:dout * TOWN + (tt + 1) * 512]
                    self.wait("V", self.s_mm, mmYc[tt])
                    self.wait("V", self.s_ve, yadd_tt[tt])
                    if gg == 0:
                        self.em("V", lambda v, x=xsl, tt=tt, dout=dout: v.scalar_tensor_tensor(
                            x, ps[4 + tt][:], self.b2a_sb[:, dout:dout + 1], x,
                            op0=OP.add, op1=OP.add))
                    else:
                        self.em("V", lambda v, x=xsl, tt=tt: v.tensor_add(x, x, ps[4 + tt][:]))
                    yadd_tt[tt] = self.inc("V", self.s_ve, 1)

    def emit_b4(self):
        ps = self.ps
        evAdd = {}
        ldsem = [self.s_eb[0], self.s_eb[1], self.s_w1[0], self.s_w1[1]]
        stsem = [self.s_st[0], self.s_st[1], self.s_lx[0], self.s_lx[1]]
        lxv = {}
        stv = {}
        for j in range(NGOWN):
            sl = j % 4
            par = j % 2
            xt = self.xtm_b4[:, sl * D:(sl + 1) * D]
            ot = self.otm_b4[:, sl * D:(sl + 1) * D]
            if j == 0:
                self.wait("G", self.s_mm, self.ln2_bc[3])
            if j >= 4:
                self.wait("G", self.s_ve, evAdd[j - 4])
            lxv[j] = self.dma("G", xt, self.xs[j * 128:(j + 1) * 128, :], ldsem[sl])
            pbase = 4 if par == 0 else 6
            self.wait("T", self.s_ve, max(evAdd.get(j - 2, 0), self.ln2_apply[j // 4]))
            for c in range(KC):
                self.em("T", lambda t, c=c, j=j, pbase=pbase: t.transpose(
                    ps[pbase + c // 4][:, (c % 4) * 128:(c % 4 + 1) * 128],
                    self.acc[:, c * TOWN + j * 128:c * TOWN + (j + 1) * 128], self.ident[:]))
            mmT = self.inc("T", self.s_mm, 1)
            self.wait("V", self.s_mm, mmT)
            if j == 0:
                self.wait("V", self.s_ve, self.ln2_apply[3])
            self.wait("V", ldsem[sl], lxv[j])
            if j >= 4:
                self.wait("V", stsem[sl], stv[j - 4])
            for hh in range(2):
                self.em("V", lambda v, ot=ot, xt=xt, hh=hh, pbase=pbase: v.tensor_add(
                    ot[:, hh * 512:(hh + 1) * 512], xt[:, hh * 512:(hh + 1) * 512],
                    ps[pbase + hh][:]))
            evAdd[j] = self.inc("V", self.s_ve, 1)
            self.wait("Y", self.s_ve, evAdd[j])
            stv[j] = self.dma("Y", self.out[j * 128:(j + 1) * 128, :], ot, stsem[sl])
        for s in stsem:
            self.wait("Y", s, s.n)
        for s in ldsem:
            self.wait("Y", s, s.n)
        if self.s_sd.n:
            self.wait("Y", self.s_sd, self.s_sd.n)
        if self.s_dbg.n:
            self.wait("Y", self.s_dbg, self.s_dbg.n)

    def build(self, phases=("A", "MOE", "B1", "LN1", "B2", "LN2", "B4")):
        self.emit_aux()
        if "A0" in phases:
            self.emit_phase_a()
            self.barrier()
        if "A" in phases:
            self.emit_phase_a()
            self.emit_phase_a_tail()
            self.barrier()
            self._dump("xnbf", self.xnbf, cast=True)
            self._dump("msb", self.m_sb, cast=True)
            self._dump("ksbd", self.ksbd, cast=True)
            self._dump("wrow", self.wrow_f)
            self._dump("hall", self.hall)
            self._dump("mstat", self.mstat)
            self._dump("rstat", self.rstat)
            self._dump("ktm", self.ktm, cast=True)
        if "MOE" in phases:
            self.emit_moe()
            self.barrier()
            self._dump("accm", self.acc)
        if "B1" in phases:
            self.emit_b1()
            self.barrier()
            self._dump("x1_pre", self.x1_sb)
            self._dump("qsb", self.q_sb, cast=True)
        if "LN1" in phases:
            self.emit_ln(1)
            self.barrier()
            self._dump("cbf", self.c_bf, cast=True)
        if "B2" in phases:
            self.emit_b2()
            self.barrier()
            self._dump("x1n", self.x1_sb)
        if "LN2" in phases:
            self.emit_ln(2)
            if "accf" in self.debug or "B4" not in phases:
                self.barrier()
                self._dump("accf", self.acc)
        if "B4" in phases:
            self.emit_b4()
        else:
            self.barrier()
            fb = self.acc[:, 0:D] if "MOE" in phases else self.kvacc[:]
            self.dma("Y", self.out[0:128, :], fb, self.s_st[0])
            self.wait("Y", self.s_st[0], self.s_st[0].n)
            if self.s_sd.n:
                self.wait("Y", self.s_sd, self.s_sd.n)
            if self.s_dbg.n:
                self.wait("Y", self.s_dbg, self.s_dbg.n)
        nc = self.nc
        with nc.Block() as block:
            @block.tensor
            def _(t):
                for fn in self.L["T"]:
                    fn(t)

            @block.vector
            def _(v):
                for fn in self.L["V"]:
                    fn(v)

            @block.scalar
            def _(s):
                for fn in self.L["S"]:
                    fn(s)

            @block.gpsimd
            def _(g):
                for fn in self.L["G"]:
                    fn(g)

            @block.sync
            def _(y):
                for fn in self.L["Y"]:
                    fn(y)
        return nc


def host_pack(inputs):
    f32 = lambda a: np.ascontiguousarray(np.asarray(a, np.float32))
    shared = {
        "wq_pk": pack_w1(f32(inputs["L0_attn_Wq"])),
        "wo_pk": pack_w1(f32(inputs["L0_attn_Wo"])),
        "wk_nat": f32(inputs["L0_attn_Wk"]),
        "wv_nat": f32(inputs["L0_attn_Wv"]),
        "w1_pk": pack_w1(f32(inputs["L0_attn_W1"])),
        "w2_pk": pack_w2(f32(inputs["L0_attn_W2"])),
        "ew1_pk": np.stack([pack_w1(f32(inputs["L0_ffn_W1"][e])) for e in range(NEXP)]),
        "ew2_pk": np.stack([pack_w2(f32(inputs["L0_ffn_W2"][e])) for e in range(NEXP)]),
        "bq_pp": pp(f32(inputs["L0_attn_bq"])),
        "bo_pp": pp(f32(inputs["L0_attn_bo"])),
        "g1_pp": pp(f32(inputs["L0_attn_g1"])),
        "be1_pp": pp(f32(inputs["L0_attn_beta1"])),
        "g2_pp": pp(f32(inputs["L0_attn_g2"])),
        "be2_pp": pp(f32(inputs["L0_attn_beta2"])),
        "b2a_pp": pp(f32(inputs["L0_attn_b2"])),
        "b1_pp": pp(f32(inputs["L0_attn_b1"])),
        "eb1_pp": np.concatenate([pp(f32(inputs["L0_ffn_b1"][e])) for e in range(NEXP)], axis=1),
        "bk_row": f32(inputs["L0_attn_bk"]).reshape(1, D),
        "bv_row": f32(inputs["L0_attn_bv"]).reshape(1, D),
        "eb2_rows": f32(inputs["L0_ffn_b2"]).reshape(NEXP, D),
        "ident_c": np.eye(128, dtype=np.float32),
        "onescol_c": np.ones((128, 1), np.float32),
        "onesrow_c": np.ones((1, 128), np.float32),
        "e2_c": make_e2(),
        "ecol_c": np.arange(NEXP, dtype=np.float32).reshape(NEXP, 1),
        "mkc_c": np.tile(np.array([[1.0, -7.0]], np.float32), (NEXP, 1)),
        "nwc_c": np.tile(np.array([[-0.5, 1.5]], np.float32), (128, 1)),
    }
    return {k: np.ascontiguousarray(v) for k, v in shared.items()}


def make_in_maps(inputs):
    x = np.ascontiguousarray(np.asarray(inputs["L0_attn_x"], np.float32))
    shared = host_pack(inputs)
    in_maps = []
    for c in range(8):
        seq, half = c // 2, c % 2
        own = x[seq, half * TOWN:(half + 1) * TOWN]
        other = x[seq, (1 - half) * TOWN:(2 - half) * TOWN]
        m = dict(shared)
        m["xs"] = np.ascontiguousarray(np.concatenate([own, other], axis=0))
        in_maps.append(m)
    return in_maps


def kernel(**inputs):
    nc = KB().build()
    in_maps = make_in_maps(inputs)
    res = run_bass_kernel_spmd(nc, in_maps, core_ids=list(range(8)))
    out = np.empty((4, 4096, 1024), np.float32)
    for c in range(8):
        seq, half = c // 2, c % 2
        out[seq, half * TOWN:(half + 1) * TOWN] = res.results[c]["out"]
    return out


# version 12
# speedup vs baseline: 1.0708x; 1.0029x over previous
import numpy as np
import ml_dtypes
import concourse.bass as bass
import concourse.mybir as mybir
from concourse.bass_utils import run_bass_kernel_spmd

F32 = mybir.dt.float32
BF16 = mybir.dt.bfloat16
I32 = mybir.dt.int32
AF = mybir.ActivationFunctionType
OP = mybir.AluOpType
AXX = mybir.AxisListType.X

D = 1024
KC = 8
NPAIR = 8
L = 4096
TOWN = 2048
NG = 32
NGOWN = 16
FF = 4096
FFC = 32
NEXP = 8
GH = 8
NGRP = FFC // GH
EPS_LN = 1e-5
EPS_ATTN = 1e-6
INV_D = 1.0 / D

DMA_SCRATCH = 16384
SBUF_TOP = 229344


def pack_w1(W):
    din, dout = W.shape
    return np.ascontiguousarray(
        W.reshape(din // 128, 128, dout // 128, 128).transpose(2, 1, 0, 3)
        .reshape(dout // 128, 128, din))


def pack_w2(W, G=GH):
    din, dout = W.shape
    ngb = din // (G * 128)
    return np.ascontiguousarray(
        W.reshape(ngb, G, 128, dout // 128, 128).transpose(0, 3, 2, 1, 4)
        .reshape(ngb, dout // 128, 128, G * 128))


def pp(vec):
    n = vec.shape[0] // 128
    return np.ascontiguousarray(vec.reshape(n, 128).T)


def make_e2():
    e2 = np.zeros((16, NPAIR * 128), np.float32)
    for p in range(NPAIR):
        e2[2 * p, p * 128:p * 128 + 64] = 1.0
        e2[2 * p + 1, p * 128 + 64:p * 128 + 128] = 1.0
    return e2


class Sem:
    def __init__(self, h, name):
        self.h = h
        self.name = name
        self.n = 0


class KB:
    def __init__(self, debug=(), ngl=NG, alim=6, moe_e=NEXP, moe_y=True, gser=False, moe_wb=True, emod=NEXP, hblk=10000):
        self.debug = set(debug)
        self.ngl = ngl
        self.alim = alim
        self.moe_e = moe_e
        self.moe_y = moe_y
        self.gser = gser
        self.moe_wb = moe_wb
        self.emod = emod
        self.hblk = hblk
        nc = bass.Bass(trn_type="TRN2", target_bir_lowering=False,
                       dynamic_dma_scratch_size=DMA_SCRATCH)
        self.nc = nc
        self.base = DMA_SCRATCH + 32
        self.L = {k: [] for k in "TVSGY"}
        self.meta = {k: [] for k in "TVSGY"}
        self._decl_dram()
        self._decl_sbuf()
        self.ps = [nc.alloc_psum_tensor(f"ps{i}", [128, 512], F32) for i in range(8)]
        mk = lambda n: Sem(nc.alloc_semaphore(n), n)
        self.s_lx = [mk("lx0"), mk("lx1")]
        self.s_w1 = [mk("w1a"), mk("w1b")]
        self.s_w2 = [mk("w2a"), mk("w2b")]
        self.s_aux = mk("aux")
        self.s_sd = mk("sd")
        self.s_dbg = mk("dbg")
        self.s_eb = [mk("eb0"), mk("eb1")]
        self.s_mm = mk("mm")
        self.s_sc = mk("sc")
        self.s_ve = mk("ve")
        self.s_st = [mk("st0"), mk("st1")]
        self.s_bar = mk("bar")
        self.w1_idx = 0
        self.w1_ev = []
        self.w2_idx = 0
        self.w2_ev = []

    def _decl_dram(self):
        nc = self.nc
        dt = lambda n, s: nc.dram_tensor(n, s, F32, kind="ExternalInput").ap()
        self.xs = dt("xs", [L, D])
        self.wq_pk = dt("wq_pk", [KC, 128, D])
        self.wo_pk = dt("wo_pk", [KC, 128, D])
        self.wk_nat = dt("wk_nat", [D, D])
        self.wv_nat = dt("wv_nat", [D, D])
        self.w1_pk = dt("w1_pk", [FFC, 128, D])
        self.w2_pk = dt("w2_pk", [NGRP, KC, 128, GH * 128])
        self.ew1_pk = dt("ew1_pk", [NEXP, FFC, 128, D])
        self.ew2_pk = dt("ew2_pk", [NEXP, NGRP, KC, 128, GH * 128])
        self.bq_pp = dt("bq_pp", [128, KC])
        self.bo_pp = dt("bo_pp", [128, KC])
        self.g1_pp = dt("g1_pp", [128, KC])
        self.be1_pp = dt("be1_pp", [128, KC])
        self.g2_pp = dt("g2_pp", [128, KC])
        self.be2_pp = dt("be2_pp", [128, KC])
        self.b2a_pp = dt("b2a_pp", [128, KC])
        self.b1_pp = dt("b1_pp", [128, FFC])
        self.eb1_pp = dt("eb1_pp", [128, NEXP * FFC])
        self.bk_row = dt("bk_row", [1, D])
        self.bv_row = dt("bv_row", [1, D])
        self.eb2_rows = dt("eb2_rows", [NEXP, D])
        self.ident_c = dt("ident_c", [128, 128])
        self.onescol_c = dt("onescol_c", [128, 1])
        self.onesrow_c = dt("onesrow_c", [1, 128])
        self.e2_c = dt("e2_c", [16, NPAIR * 128])
        self.mkc_c = dt("mkc_c", [NEXP, 2])
        self.nwc_c = dt("nwc_c", [128, 2])
        self.ecol_c = dt("ecol_c", [NEXP, 1])
        self.out = nc.dram_tensor("out", [TOWN, D], F32, kind="ExternalOutput").ap()
        self.dbg = {}
        for name, shape in (("xnbf", [128, KC * TOWN]), ("msb", [128, NPAIR * 128]),
                            ("ksbd", [128, NPAIR * 16]), ("wrow", [NEXP, TOWN]),
                            ("accm", [128, KC * TOWN]), ("x1n", [128, KC * TOWN]),
                            ("accf", [128, KC * TOWN]), ("hall", [128, NGOWN]),
                            ("ktm", [128, 2 * D]), ("mstat", [128, NGOWN]),
                            ("rstat", [128, NGOWN]), ("x1_pre", [128, KC * TOWN]),
                            ("qsb", [128, KC * TOWN]), ("cbf", [128, KC * TOWN])):
            if name in self.debug:
                self.dbg[name] = nc.dram_tensor(
                    "dbg_" + name, shape, F32, kind="ExternalOutput").ap()

    def _sb(self, name, shape, dtype, off, region_end=None):
        per = int(np.prod(shape[1:])) * mybir.dt.size(dtype)
        off = self.base + off
        assert off % 32 == 0, (name, off)
        assert off + per <= SBUF_TOP, (name, off, per, off + per)
        if region_end is not None:
            assert off + per <= self.base + region_end, (name, off + per, region_end)
        return self.nc.alloc_sbuf_tensor_at(name, list(shape), dtype, offset=off)

    def _decl_sbuf(self):
        sb = self._sb
        self.wk_sb = sb("wk_sb", [128, KC * D], BF16, 0, 65536)
        self.wv_sb = sb("wv_sb", [128, KC * D], BF16, 16384, 65536)
        self.kvbias = sb("kvbias", [1, 2 * D], BF16, 32768, 65536)
        self.xnp = sb("xnp", [128, 2 * KC * 128], BF16, 36864, 65536)
        self.stat48 = sb("stat48", [48, 128], F32, 40960, 65536)
        self.kvacc = sb("kvacc", [128, NPAIR * 128], F32, 45056, 65536)
        self.ksacc = sb("ksacc", [128, NPAIR], F32, 49600, 65536)
        self.h_m = [sb("h_lo", [128, GH * TOWN], BF16, 0, 65536),
                    sb("h_hi", [128, GH * TOWN], BF16, 32768, 65536)]
        self.x1_sb = sb("x1_sb", [128, KC * TOWN], F32, 0, 65536)
        self.acc = sb("acc", [128, KC * TOWN], F32, 65536, 131072)
        XB = 131072
        self.xnbf = sb("xnbf", [128, KC * TOWN], BF16, XB, 163840)
        self.xnfm = sb("xnfm", [128, KC * 1024], F32, XB, 163840)
        self.c_bf = sb("c_bf", [128, KC * TOWN], BF16, XB, 163840)
        TR = 163840
        self.xtm_a = sb("xtm_a", [128, 2 * D], F32, TR, 196608)
        self.ktm = sb("ktm", [128, 2 * D], BF16, TR + 8192, 196608)
        self.vtm = sb("vtm", [128, 2 * D], BF16, TR + 12288, 196608)
        self.elu_a = sb("elu_a", [128, 2 * D], F32, TR + 16384, 196608)
        self.hrow = sb("hrow", [1, TOWN], F32, TR, 196608)
        self.h8 = sb("h8", [NEXP, TOWN], F32, TR + 8192, 196608)
        self.wscr = sb("wscr", [NEXP, TOWN], F32, TR + 16384, 196608)
        self.wrow_f = sb("wrow_f", [NEXP, TOWN], F32, TR + 24576, 196608)
        self.wb = sb("wb", [128, 2 * TOWN], BF16, TR, 196608)
        self.wrow_bf = sb("wrow_bf", [NEXP, TOWN], BF16, TR + 8192, 196608)
        self.wtmp = sb("wtmp", [1, 2 * TOWN], BF16, TR + 12288, 196608)
        self.eb2cur = sb("eb2cur", [1, 2 * D], BF16, TR + 20480, 196608)
        self.q_sb = sb("q_sb", [128, KC * TOWN], BF16, TR, 196608)
        self.ln_m = sb("ln_m", [1, TOWN], F32, TR, 196608)
        self.ln_r = sb("ln_r", [1, TOWN], F32, TR + 8192, 196608)
        self.ln_bf = sb("ln_bf", [128, 4 * 512], BF16, TR + 16384, 196608)
        self.ln_t = sb("ln_t", [128, 2 * 512], F32, TR + 20480, 196608)
        self.ln_s1 = sb("ln_s1", [1, 512], F32, TR + 24576, 196608)
        self.ln_s2 = sb("ln_s2", [1, 512], F32, TR + 26624, 196608)
        self.ln_s3 = sb("ln_s3", [1, 512], F32, TR + 28672, 196608)
        self.h_b2 = sb("h_b2", [128, GH * TOWN], BF16, TR, 196608)
        self.xtm_b4 = sb("xtm_b4", [128, 4 * D], F32, TR, 196608)
        self.otm_b4 = sb("otm_b4", [128, 4 * D], F32, TR + 16384, 196608)
        ST = 196608
        self.w1s = sb("w1s", [128, 2 * KC * 128], BF16, ST, 204800)
        self.w2s = sb("w2s", [128, 2 * GH * 128], BF16, ST + 4096, 204800)
        self.zbfb = sb("zbfb", [16, 512], BF16, ST + 4096, 204800)
        self.zb = sb("zb", [128, 2 * 512], BF16, ST + 5120, 204800)
        self.xt_b1 = sb("xt_b1", [128, 512], F32, ST + 6144, 204800)
        SM = 204800
        self.ident = sb("ident", [128, 128], F32, SM)
        self.onescol_bf = sb("onescol_bf", [128, 1], BF16, SM + 512)
        self.onesrow_bf = sb("onesrow_bf", [1, 128], BF16, SM + 544)
        self.onesrow_f = sb("onesrow_f", [1, 128], F32, SM + 800)
        self.bq_sb = sb("bq_sb", [128, KC], F32, SM + 1312)
        self.bo_sb = sb("bo_sb", [128, KC], F32, SM + 1344)
        self.g1_sb = sb("g1_sb", [128, KC], F32, SM + 1376)
        self.be1_sb = sb("be1_sb", [128, KC], F32, SM + 1408)
        self.g2_sb = sb("g2_sb", [128, KC], F32, SM + 1440)
        self.be2_sb = sb("be2_sb", [128, KC], F32, SM + 1472)
        self.b2a_sb = sb("b2a_sb", [128, KC], F32, SM + 1504)
        self.b1_sb = sb("b1_sb", [128, FFC], BF16, SM + 1536)
        self.eb1_sb = sb("eb1_sb", [128, NEXP * FFC], BF16, SM + 1600)
        self.mstat = sb("mstat", [128, NGOWN], F32, SM + 2112)
        self.rstat = sb("rstat", [128, NGOWN], F32, SM + 2176)
        self.ksbd = sb("ksbd", [128, NPAIR * 16], BF16, SM + 2240)
        self.m_sb = sb("m_sb", [128, NPAIR * 128], BF16, SM + 2496)
        self.e2_sb = sb("e2_sb", [16, NPAIR * 128], BF16, SM + 4544)
        self.dum = sb("dum", [128, 8], F32, SM + 6624)
        SZ = SM + 6656
        self.elu_r = sb("elu_r", [128, 512], BF16, SZ)
        self.tpack = sb("tpack", [128, 128], F32, SZ)
        self.xn01 = sb("xn01", [128, 2 * NGOWN], F32, SZ + 512)
        self.hall = sb("hall", [128, NGOWN], F32, SZ + 704)
        self.scol = sb("scol", [128, 4], F32, SZ + 768)
        self.ecol_sb = sb("ecol_sb", [NEXP, 1], F32, SZ + 800)
        self.mkc_sb = sb("mkc_sb", [NEXP, 2], F32, SZ + 832)
        self.nwc_sb = sb("nwc_sb", [128, 2], F32, SM + 6592)
        assert SZ + 1024 <= SBUF_TOP - self.base, (SZ + 1024, SBUF_TOP - self.base)

    def em(self, e, fn):
        self.L[e].append(fn)
        self.meta[e].append(("op", None, 0))

    def wait(self, e, sem, val):
        if val > 0:
            self.L[e].append(lambda x, s=sem.h, v=val: x.wait_ge(s, v))
            self.meta[e].append(("wait", sem.name, val))

    def inc(self, e, sem, n=1):
        prev = self.L[e].pop()
        sem.n += n
        def fn(x, prev=prev, s=sem.h, n=n):
            prev(x).then_inc(s, n)
        self.L[e].append(fn)
        self.meta[e][-1] = ("op", sem.name, n)
        return sem.n

    def static_check(self):
        pos = {e: 0 for e in "TVSGY"}
        sems = {}
        while True:
            progress = False
            done = True
            for e in "TVSGY":
                while pos[e] < len(self.meta[e]):
                    kind, sem, val = self.meta[e][pos[e]]
                    if kind == "wait" and sems.get(sem, 0) < val:
                        break
                    if kind == "op" and sem is not None:
                        sems[sem] = sems.get(sem, 0) + val
                    pos[e] += 1
                    progress = True
                if pos[e] < len(self.meta[e]):
                    done = False
            if done:
                return
            if not progress:
                msg = []
                for e in "TVSGY":
                    if pos[e] < len(self.meta[e]):
                        k, sem, val = self.meta[e][pos[e]]
                        msg.append(f"{e}@{pos[e]}: wait {sem}>={val} (cur {sems.get(sem, 0)})")
                raise RuntimeError("STATIC DEADLOCK: " + "; ".join(msg))

    def chain(self, e, sem):
        v = self.inc(e, sem, 1)
        self.wait(e, sem, v)
        return v

    def vbar(self):
        return self.chain("V", self.s_ve)

    def dma(self, e, out, in_, sem):
        self.em(e, lambda x, o=out, i=in_: x.dma_start(out=o, in_=i))
        return self.inc(e, sem, 16)

    def barrier(self):
        self.em("T", lambda x: x.nop())
        self.inc("T", self.s_bar, 1)
        self.em("V", lambda x: x.memset(self.dum[:, 0:2], 0.0))
        self.inc("V", self.s_bar, 1)
        self.em("S", lambda x: x.memzero(self.dum[:, 2:4]))
        self.inc("S", self.s_bar, 1)
        self.em("G", lambda x: x.memset(self.dum[:, 4:6], 0.0))
        self.inc("G", self.s_bar, 1)
        tgt = self.s_bar.n
        for e in "TVSGY":
            self.wait(e, self.s_bar, tgt)

    def w1_load(self, src):
        i = self.w1_idx
        self.w1_idx += 1
        slot = i % 2
        if i >= 2:
            sem, val = self.w1_ev[i - 2]
            self.wait("G", sem, val)
        if self.gser:
            self.wait("G", self.s_w1[slot], self.s_w1[slot].n)
        v = self.dma("G", self.w1s[:, slot * 1024:(slot + 1) * 1024], src, self.s_w1[slot])
        return slot, v

    def w1t(self, slot, k):
        return self.w1s[:, slot * 1024 + k * 128: slot * 1024 + (k + 1) * 128]

    def w2_load(self, src):
        i = self.w2_idx
        self.w2_idx += 1
        slot = i % 2
        if i >= 2:
            sem, val = self.w2_ev[i - 2]
            self.wait("G", sem, val)
        v = self.dma("G", self.w2s[:, slot * 1024:(slot + 1) * 1024], src, self.s_w2[slot])
        return slot, v

    def w2t(self, slot, j):
        return self.w2s[:, slot * 1024 + j * 128: slot * 1024 + (j + 1) * 128]

    def _dump(self, name, src, cast=False):
        if name not in self.dbg:
            return
        if cast:
            sem = self.s_dbg
            self.dma("G", self.dbg[name], src[:], sem)
        else:
            sem = self.s_sd
            self.wait("Y", sem, sem.n)
            self.dma("Y", self.dbg[name], src[:], sem)
        for e in "TVSG":
            self.wait(e, sem, sem.n)

    def emit_aux(self):
        d = self.dma
        for k in range(KC):
            d("G", self.wk_sb[:, k * D:(k + 1) * D],
              self.wk_nat[k * 128:(k + 1) * 128, :], self.s_aux)
            d("G", self.wv_sb[:, k * D:(k + 1) * D],
              self.wv_nat[k * 128:(k + 1) * 128, :], self.s_aux)
        d("G", self.kvbias[0:1, 0:D], self.bk_row, self.s_aux)
        d("G", self.kvbias[0:1, D:2 * D], self.bv_row, self.s_aux)
        for dst, src in ((self.ident, self.ident_c), (self.onescol_bf, self.onescol_c),
                         (self.onesrow_bf, self.onesrow_c), (self.onesrow_f, self.onesrow_c),
                         (self.bq_sb, self.bq_pp), (self.bo_sb, self.bo_pp),
                         (self.g1_sb, self.g1_pp), (self.be1_sb, self.be1_pp),
                         (self.g2_sb, self.g2_pp), (self.be2_sb, self.be2_pp),
                         (self.b2a_sb, self.b2a_pp), (self.b1_sb, self.b1_pp),
                         (self.eb1_sb, self.eb1_pp), (self.ecol_sb, self.ecol_c),
                         (self.e2_sb, self.e2_c), (self.mkc_sb, self.mkc_c),
                         (self.nwc_sb, self.nwc_c)):
            d("G", dst[:], src, self.s_aux)
        self.aux_tgt = self.s_aux.n
        for e in "TVS":
            self.wait(e, self.s_aux, self.aux_tgt)
        self.em("V", lambda v: v.memset(self.m_sb[:], 0.0))
        self.em("V", lambda v: v.memset(self.ksbd[:], 0.0))
        self.inc("V", self.s_ve, 1)

    def emit_phase_a(self):
        ps = self.ps
        NGL = self.ngl
        evT = {}; evKV = {}; evKVmm = {}; evA = {}; evXN = {}
        evK_s = {}; evK_v = {}; evV_v = {}; evDMA = {}
        evXNw = 0
        tpieces, tailguard = self.tail_pieces(lambda: evA[NGOWN - 1])
        tp_sched = {17: 0, 18: 1, 19: 2, 20: 3, 21: 4}
        tp_done = [0]

        def lhs(c, g):
            own = g < NGOWN
            if own:
                return self.xnbf[:, c * TOWN + g * 128:c * TOWN + g * 128 + 128]
            return self.xnp[:, ((g % 2) * KC + c) * 128:((g % 2) * KC + c + 1) * 128]

        def emit_stats(g, mid_v=None, mid_s=None):
            own = g < NGOWN
            par = g % 2
            xt = self.xtm_a[:, par * D:(par + 1) * D]
            if g >= 2:
                self.wait("Y", self.s_mm, evT[g - 2])
            evDMA[g] = self.dma("Y", xt, self.xs[g * 128:(g + 1) * 128, :], self.s_lx[par])
            mc = self.mstat[:, g:g + 1] if own else self.scol[:, 0:1]
            rc = self.rstat[:, g:g + 1] if own else self.scol[:, 1:2]
            ac = self.scol[:, 2:3]
            sq = self.elu_a[:, par * D:(par + 1) * D]
            self.wait("V", self.s_lx[par], evDMA[g])
            self.em("V", lambda v, xt=xt, mc=mc: v.reduce_sum(mc, xt, axis=AXX))
            self.vbar()
            self.em("V", lambda v, mc=mc: v.tensor_scalar_mul(mc, mc, -INV_D))
            v_negm = self.inc("V", self.s_ve, 1)
            self.wait("S", self.s_ve, v_negm)
            if g >= 2:
                self.wait("S", self.s_mm, evKVmm.get(g - 2, 0))
            self.em("S", lambda s, xt=xt, sq=sq, mc=mc: s.activation(sq, xt, AF.Square, bias=mc))
            self.chain("S", self.s_sc)
            self.em("S", lambda s, xt=xt, mc=mc: s.add(xt, xt, mc))
            s_sub = self.inc("S", self.s_sc, 1)
            if mid_v is not None:
                mid_v()
            self.wait("V", self.s_sc, s_sub)
            self.em("V", lambda v, sq=sq, ac=ac: v.reduce_sum(ac, sq, axis=AXX))
            self.vbar()
            self.em("V", lambda v, ac=ac: v.tensor_scalar(ac, ac, INV_D, EPS_LN, op0=OP.mult, op1=OP.add))
            v_ac = self.inc("V", self.s_ve, 1)
            if mid_s is not None:
                mid_s()
            r0 = self.scol[:, 3:4]
            self.wait("S", self.s_ve, v_ac)
            self.em("S", lambda s, ac=ac, r0=r0: s.activation(r0, ac, AF.Ln))
            self.chain("S", self.s_sc)
            self.em("S", lambda s, r0=r0: s.activation(r0, r0, AF.Exp, scale=-0.5))
            self.chain("S", self.s_sc)
            self.em("S", lambda s, rc=rc, r0=r0: s.mul(rc, r0, r0))
            self.chain("S", self.s_sc)
            self.em("S", lambda s, rc=rc, ac=ac: s.mul(rc, rc, ac))
            self.chain("S", self.s_sc)
            self.em("S", lambda s, rc=rc: s.mul(rc, rc, self.nwc_sb[:, 0:1]))
            self.chain("S", self.s_sc)
            self.em("S", lambda s, rc=rc: s.add(rc, rc, self.nwc_sb[:, 1:2]))
            self.chain("S", self.s_sc)
            self.em("S", lambda s, rc=rc, r0=r0: s.mul(rc, rc, r0))
            self.chain("S", self.s_sc)
            self.em("S", lambda s, xt=xt, rc=rc: s.mul(xt, xt, rc))
            if own:
                self.chain("S", self.s_sc)
                self.em("S", lambda s, xt=xt, g=g: s.copy(self.xn01[:, g:g + 1], xt[:, 0:1]))
                self.chain("S", self.s_sc)
                self.em("S", lambda s, xt=xt, g=g: s.copy(
                    self.xn01[:, NGOWN + g:NGOWN + g + 1], xt[:, 1:2]))
            evA[g] = self.inc("S", self.s_sc, 1)

        def emit_T(g):
            nonlocal evXNw
            own = g < NGOWN
            par = g % 2
            xt = self.xtm_a[:, par * D:(par + 1) * D]
            self.wait("T", self.s_sc, evA[g])
            for wave in range(2):
                self.wait("T", self.s_sc, evXNw)
                if tailguard[0]:
                    self.wait("T", self.s_ve, tailguard[0])
                    tailguard[0] = 0
                for j in range(4):
                    c = wave * 4 + j
                    self.em("T", lambda t, xt=xt, c=c, j=j: t.transpose(
                        ps[4][:, j * 128:(j + 1) * 128], xt[:, c * 128:(c + 1) * 128],
                        self.ident[:]))
                mmv = self.inc("T", self.s_mm, 1)
                if wave == 1:
                    evT[g] = mmv
                self.wait("S", self.s_mm, mmv)
                if (not own) and g >= 2:
                    self.wait("S", self.s_mm, evKVmm.get(g - 2, 0))
                for j in range(4):
                    c = wave * 4 + j
                    self.em("S", lambda s, dst=lhs(c, g), j=j: s.copy(dst, ps[4][:, j * 128:(j + 1) * 128]))
                evXNw = self.inc("S", self.s_sc, 1)
            evXN[g] = evXNw
            self.wait("T", self.s_sc, evXN[g])
            if g >= 1:
                self.wait("T", self.s_sc, evK_s[g - 1])
                self.wait("T", self.s_ve, evK_v[g - 1])
            for k in range(KC):
                for hh in range(2):
                    self.em("T", lambda t, k=k, hh=hh, lh=lhs(k, g): t.matmul(
                        ps[hh][:], lh, self.wk_sb[:, k * D + hh * 512:k * D + (hh + 1) * 512],
                        start=(k == 0), stop=False))
            for hh in range(2):
                self.em("T", lambda t, hh=hh: t.matmul(
                    ps[hh][:], self.onesrow_bf[:], self.kvbias[0:1, hh * 512:(hh + 1) * 512],
                    start=False, stop=True))
            evKmm = self.inc("T", self.s_mm, 1)
            if g >= 1:
                self.wait("T", self.s_ve, evV_v[g - 1])
            for k in range(KC):
                for hh in range(2):
                    self.em("T", lambda t, k=k, hh=hh, lh=lhs(k, g): t.matmul(
                        ps[2 + hh][:], lh, self.wv_sb[:, k * D + hh * 512:k * D + (hh + 1) * 512],
                        start=(k == 0), stop=False))
            for hh in range(2):
                self.em("T", lambda t, hh=hh: t.matmul(
                    ps[2 + hh][:], self.onesrow_bf[:],
                    self.kvbias[0:1, D + hh * 512:D + (hh + 1) * 512],
                    start=False, stop=True))
            evVmm = self.inc("T", self.s_mm, 1)
            return evKmm, evVmm

        def elu_vmax(g, evKmm):
            par = g % 2
            rt = self.elu_a[:, par * D:(par + 1) * D]
            self.wait("V", self.s_mm, evKmm)
            if g >= 2:
                self.wait("V", self.s_mm, evKVmm.get(g - 2, 0))
            for hh in range(2):
                self.em("V", lambda v, hh=hh, rt=rt: v.tensor_scalar(
                    rt[:, hh * 512:(hh + 1) * 512], ps[hh][:], 0.0, None, op0=OP.max))
            evK_v[g] = self.inc("V", self.s_ve, 1)

        def elu_sexp(g, evKmm):
            par = g % 2
            kslot = self.ktm[:, par * D:(par + 1) * D]
            self.wait("S", self.s_mm, evKmm)
            self.wait("S", self.s_ve, evK_v[g])
            if g >= 2:
                self.wait("S", self.s_mm, evKVmm.get(g - 2, 0))
            for hh in range(2):
                self.em("S", lambda s, hh=hh, kslot=kslot: s.activation(
                    kslot[:, hh * 512:(hh + 1) * 512], ps[hh][:], AF.Exp))
            evK_s[g] = self.inc("S", self.s_sc, 1)

        def elu_vtail(g, evVmm):
            par = g % 2
            kslot = self.ktm[:, par * D:(par + 1) * D]
            vslot = self.vtm[:, par * D:(par + 1) * D]
            rt = self.elu_a[:, par * D:(par + 1) * D]
            self.wait("V", self.s_sc, evK_s[g])
            for hh in range(2):
                self.em("V", lambda v, hh=hh, kslot=kslot, rt=rt: v.scalar_tensor_tensor(
                    kslot[:, hh * 512:(hh + 1) * 512], kslot[:, hh * 512:(hh + 1) * 512],
                    1.0, rt[:, hh * 512:(hh + 1) * 512], op0=OP.min, op1=OP.add))
            self.vbar()
            self.wait("V", self.s_mm, evVmm)
            for hh in range(2):
                self.em("V", lambda v, hh=hh, vslot=vslot: v.tensor_copy(
                    vslot[:, hh * 512:(hh + 1) * 512], ps[2 + hh][:]))
            evV_v[g] = self.inc("V", self.s_ve, 1)

        def emit_kv(g):
            par = g % 2
            kslot = self.ktm[:, par * D:(par + 1) * D]
            vslot = self.vtm[:, par * D:(par + 1) * D]
            self.wait("T", self.s_ve, evV_v[g])
            if g >= 1:
                self.wait("T", self.s_ve, evKV[g - 1])
            for pr in range(NPAIR):
                b = 5 + pr // 4
                o = pr % 4
                self.em("T", lambda t, pr=pr, b=b, o=o, kslot=kslot, vslot=vslot: t.matmul(
                    ps[b][:, o * 128:(o + 1) * 128], kslot[:, pr * 128:(pr + 1) * 128],
                    vslot[:, pr * 128:(pr + 1) * 128], start=True, stop=True))
                self.em("T", lambda t, pr=pr, kslot=kslot: t.matmul(
                    ps[7][:, pr:pr + 1], kslot[:, pr * 128:(pr + 1) * 128],
                    self.onescol_bf[:], start=True, stop=True))
            evKVmm[g] = self.inc("T", self.s_mm, 1)
            self.wait("V", self.s_mm, evKVmm[g])
            if g >= 1:
                self.wait("V", self.s_ve, evKV[g - 1])
            if g == 0:
                self.em("V", lambda v: v.tensor_copy(self.kvacc[:, 0:512], ps[5][:]))
                self.em("V", lambda v: v.tensor_copy(self.kvacc[:, 512:1024], ps[6][:]))
                self.em("V", lambda v: v.tensor_copy(self.ksacc[:], ps[7][:, 0:NPAIR]))
            else:
                self.em("V", lambda v: v.tensor_add(self.kvacc[:, 0:512], self.kvacc[:, 0:512], ps[5][:]))
                self.em("V", lambda v: v.tensor_add(self.kvacc[:, 512:1024], self.kvacc[:, 512:1024], ps[6][:]))
                self.em("V", lambda v: v.tensor_add(self.ksacc[:], self.ksacc[:], ps[7][:, 0:NPAIR]))
            evKV[g] = self.inc("V", self.s_ve, 1)

        pend = []
        for g in range(NGL):
            if pend:
                pg, ek, ev = pend.pop(0)
                emit_stats(g, mid_v=lambda pg=pg, ek=ek: elu_vmax(pg, ek),
                           mid_s=lambda pg=pg, ek=ek, ev=ev: (elu_sexp(pg, ek),
                                                             elu_vtail(pg, ev)))
                emit_kv(pg)
            else:
                emit_stats(g)
            ek, ev = emit_T(g)
            pend.append((g, ek, ev))
            if g >= NGOWN + 1 and tp_done[0] < len(tpieces) and tp_done[0] <= tp_sched.get(g, -1):
                tpieces[tp_done[0]]()
                tp_done[0] += 1
        pg, ek, ev = pend.pop(0)
        elu_vmax(pg, ek)
        elu_sexp(pg, ek)
        elu_vtail(pg, ev)
        emit_kv(NGL - 1)
        self.evKV_last = evKV[NGL - 1]
        self.wait("Y", self.s_mm, evKVmm[NGL - 1])
        self.wait("Y", self.s_ve, evV_v[NGL - 1])
        self.wait("Y", self.s_sc, evA[NGL - 1])
        while tp_done[0] < len(tpieces):
            tpieces[tp_done[0]]()
            tp_done[0] += 1

    def tail_pieces(self, evA15f):
        ps = self.ps
        u = self.hall[:]
        t1 = self.xn01[:, 0:NGOWN]
        t2 = self.xn01[:, NGOWN:2 * NGOWN]
        guard = [0]
        ev = {}

        def p_hash1():
            self.wait("V", self.s_sc, evA15f())
            self.em("V", lambda v: v.tensor_add(u, t1, t2))
            self.vbar()
            for bnd in (64.0, 32.0):
                self.em("V", lambda v, b=bnd: v.tensor_single_scalar(t1, u, b, OP.is_ge))
                self.vbar()
                self.em("V", lambda v, b=bnd: v.scalar_tensor_tensor(u, t1, -b, u, op0=OP.mult, op1=OP.add))
                self.vbar()
                self.em("V", lambda v, b=bnd: v.tensor_single_scalar(t1, u, -b, OP.is_le))
                self.vbar()
                self.em("V", lambda v, b=bnd: v.scalar_tensor_tensor(u, t1, b, u, op0=OP.mult, op1=OP.add))
                self.vbar()

        def p_hash2():
            for bnd in (16.0, 8.0):
                self.em("V", lambda v, b=bnd: v.tensor_single_scalar(t1, u, b, OP.is_ge))
                self.vbar()
                self.em("V", lambda v, b=bnd: v.scalar_tensor_tensor(u, t1, -b, u, op0=OP.mult, op1=OP.add))
                self.vbar()
                self.em("V", lambda v, b=bnd: v.tensor_single_scalar(t1, u, -b, OP.is_le))
                self.vbar()
                self.em("V", lambda v, b=bnd: v.scalar_tensor_tensor(u, t1, b, u, op0=OP.mult, op1=OP.add))
                self.vbar()

        def p_hash3():
            self.em("V", lambda v: v.tensor_single_scalar(t1, u, 1.0, OP.is_ge))
            self.vbar()
            for kk in range(2, 8):
                self.em("V", lambda v, k=float(kk): v.tensor_single_scalar(t2, u, k, OP.is_ge))
                self.vbar()
                self.em("V", lambda v: v.tensor_add(t1, t1, t2))
                self.vbar()

        def p_hash4():
            for kk in range(1, 8):
                self.em("V", lambda v, k=float(-kk): v.tensor_single_scalar(t2, u, k, OP.is_le))
                self.vbar()
                self.em("V", lambda v: v.tensor_sub(t1, t1, t2))
                self.vbar()
            self.em("V", lambda v: v.tensor_single_scalar(t2, t1, 0.0, OP.is_lt))
            self.vbar()
            self.em("V", lambda v: v.scalar_tensor_tensor(u, t2, 8.0, t1, op0=OP.mult, op1=OP.add))
            self.vbar()

        def p_pack():
            self.em("V", lambda v: v.tensor_copy(self.tpack[:, 0:NGOWN], self.mstat[:]))
            self.em("V", lambda v: v.tensor_copy(self.tpack[:, NGOWN:2 * NGOWN], self.rstat[:]))
            self.em("V", lambda v: v.tensor_copy(self.tpack[:, 2 * NGOWN:3 * NGOWN], self.hall[:]))
            vp = self.inc("V", self.s_ve, 1)
            self.wait("T", self.s_ve, vp)
            self.em("T", lambda t: t.transpose(ps[4][:, 0:128], self.tpack[:], self.ident[:]))
            mt = self.inc("T", self.s_mm, 1)
            self.wait("V", self.s_mm, mt)
            self.em("V", lambda v: v.tensor_copy(self.stat48[0:16, :], ps[4][32:48, 0:128]))
            ev["stat48"] = self.inc("V", self.s_ve, 1)
            guard[0] = ev["stat48"]

        def p_bcast():
            self.wait("Y", self.s_ve, ev["stat48"])
            self.dma("Y", self.hrow[0:1, :], self.stat48[0:16, :], self.s_sd)
            for e in range(NEXP):
                self.wait("Y", self.s_sd, self.s_sd.n)
                self.dma("Y", self.h8[e:e + 1, :], self.hrow[0:1, :], self.s_sd)
            ev["sd"] = self.s_sd.n

        def p_wrow():
            self.wait("V", self.s_sd, ev["sd"])
            self.em("V", lambda v: v.tensor_scalar(self.h8[:], self.h8[:], self.ecol_sb[:], None,
                                                   op0=OP.subtract))
            vd = self.inc("V", self.s_ve, 1)
            self.wait("S", self.s_ve, vd)
            self.em("S", lambda s: s.activation(self.wscr[:], self.h8[:], AF.Abs))
            self.chain("S", self.s_sc)
            self.em("S", lambda s: s.activation(self.wrow_f[:], self.wscr[:], AF.Relu, scale=-1.0, bias=self.mkc_sb[:, 0:1]))
            self.chain("S", self.s_sc)
            self.em("S", lambda s: s.activation(self.wscr[:], self.h8[:], AF.Abs, bias=self.mkc_sb[:, 0:1]))
            self.chain("S", self.s_sc)
            self.em("S", lambda s: s.activation(self.wscr[:], self.wscr[:], AF.Relu, scale=-1.0, bias=self.mkc_sb[:, 0:1]))
            s1 = self.inc("S", self.s_sc, 1)
            self.wait("V", self.s_sc, s1)
            self.em("V", lambda v: v.tensor_add(self.wrow_f[:], self.wrow_f[:], self.wscr[:]))
            v1 = self.inc("V", self.s_ve, 1)
            self.wait("S", self.s_ve, v1)
            self.em("S", lambda s: s.activation(self.wscr[:], self.h8[:], AF.Abs, bias=self.mkc_sb[:, 1:2]))
            self.chain("S", self.s_sc)
            self.em("S", lambda s: s.activation(self.wscr[:], self.wscr[:], AF.Relu, scale=-1.0, bias=self.mkc_sb[:, 0:1]))
            s2 = self.inc("S", self.s_sc, 1)
            self.wait("V", self.s_sc, s2)
            self.em("V", lambda v: v.tensor_add(self.wrow_f[:], self.wrow_f[:], self.wscr[:]))
            self.vbar()
            self.em("V", lambda v: v.tensor_scalar_mul(self.wrow_f[:], self.wrow_f[:], 0.5))
            self.vbar()
            self.em("V", lambda v: v.tensor_copy(self.wrow_bf[:], self.wrow_f[:]))
            self.wrow_ev = self.inc("V", self.s_ve, 1)

        return [p_hash1, p_hash2, p_hash3, p_hash4, p_pack, p_bcast, p_wrow], guard

    def emit_phase_a_tail(self):
        ps = self.ps
        self.wait("V", self.s_ve, self.evKV_last)
        for pr in range(NPAIR):
            self.em("V", lambda v, pr=pr: v.tensor_copy(
                self.m_sb[0:64, pr * 128:pr * 128 + 64],
                self.kvacc[0:64, pr * 128:pr * 128 + 64]))
            self.em("V", lambda v, pr=pr: v.tensor_copy(
                self.m_sb[64:128, pr * 128 + 64:pr * 128 + 128],
                self.kvacc[64:128, pr * 128 + 64:pr * 128 + 128]))
            self.em("V", lambda v, pr=pr: v.tensor_copy(
                self.ksbd[0:64, pr * 16 + 2 * pr:pr * 16 + 2 * pr + 1],
                self.ksacc[0:64, pr:pr + 1]))
            self.em("V", lambda v, pr=pr: v.tensor_copy(
                self.ksbd[64:128, pr * 16 + 2 * pr + 1:pr * 16 + 2 * pr + 2],
                self.ksacc[64:128, pr:pr + 1]))

    def emit_moe(self):
        ps = self.ps
        regA_ev = [0, 0]
        hEv_prev = [0]
        wbEv = [0, 0]
        wtmpRead = [0, 0]
        hW_last = {}
        yEv_prev = [0]
        ebv_of = {}

        def emit_wb(e):
            sl = e % 2
            self.wait("Y", self.s_ve, self.wrow_ev)
            if e >= 2:
                self.wait("Y", self.s_mm, wtmpRead[sl])
            self.wait("Y", self.s_sd, self.s_sd.n)
            self.dma("Y", self.wtmp[0:1, sl * TOWN:(sl + 1) * TOWN],
                     self.wrow_bf[e:e + 1, :], self.s_sd)
            sdv = self.s_sd.n
            if e >= 2:
                self.wait("G", self.s_mm, wtmpRead[sl])
            ebv_of[e] = self.dma("G", self.eb2cur[0:1, sl * D:(sl + 1) * D],
                                 self.eb2_rows[e:e + 1, :], self.s_eb[sl])
            self.wait("T", self.s_sd, sdv)
            self.wait("T", self.s_sc, hEv_prev[0])
            for tt in range(4):
                self.em("T", lambda t, tt=tt, sl=sl: t.matmul(
                    ps[tt][:], self.onesrow_bf[:],
                    self.wtmp[0:1, sl * TOWN + tt * 512:sl * TOWN + (tt + 1) * 512],
                    start=True, stop=True))
            mmwb = self.inc("T", self.s_mm, 1)
            self.wait("V", self.s_mm, mmwb)
            for tt in range(4):
                self.em("V", lambda v, tt=tt, sl=sl: v.tensor_copy(
                    self.wb[:, sl * TOWN + tt * 512:sl * TOWN + (tt + 1) * 512], ps[tt][:]))
            wbEv[sl] = self.inc("V", self.s_ve, 1)

        def emit_h_block(e, gg, i):
            n = gg * GH + i
            r = gg % 2
            hreg = self.h_m[r]
            slot, wv = self.w1_load(self.ew1_pk[e % self.emod, n])
            self.wait("T", self.s_w1[slot], wv)
            self.wait("T", self.s_sc, hEv_prev[0])
            if gg == 0 and i == 0:
                self.wait("T", self.s_ve, wbEv[e % 2])
            for k in range(KC):
                for tt in range(4):
                    self.em("T", lambda t, k=k, tt=tt, slot=slot: t.matmul(
                        ps[tt][:], self.w1t(slot, k),
                        self.xnbf[:, k * TOWN + tt * 512:k * TOWN + (tt + 1) * 512],
                        start=(k == 0), stop=(k == KC - 1)))
            mmH = self.inc("T", self.s_mm, 1)
            self.w1_ev.append((self.s_mm, mmH))
            self.wait("S", self.s_mm, mmH)
            if regA_ev[r]:
                self.wait("S", self.s_mm, regA_ev[r])
            for tt in range(4):
                self.em("S", lambda s, tt=tt, i=i, hreg=hreg, e=e, n=n: s.activation(
                    hreg[:, i * TOWN + tt * 512:i * TOWN + (tt + 1) * 512], ps[tt][:],
                    AF.Relu, bias=self.eb1_sb[:, e * FFC + n:e * FFC + n + 1]))
            hEv_prev[0] = self.inc("S", self.s_sc, 1)
            self.wait("V", self.s_sc, hEv_prev[0])
            if self.moe_wb:
                self.wait("V", self.s_ve, wbEv[e % 2])
                for tt in range(4):
                    self.em("V", lambda v, tt=tt, i=i, hreg=hreg, e=e: v.tensor_mul(
                        hreg[:, i * TOWN + tt * 512:i * TOWN + (tt + 1) * 512],
                        hreg[:, i * TOWN + tt * 512:i * TOWN + (tt + 1) * 512],
                        self.wb[:, (e % 2) * TOWN + tt * 512:(e % 2) * TOWN + (tt + 1) * 512]))
            else:
                self.em("V", lambda v: v.memset(self.dum[:, 0:2], 0.0))
            hW_last[(e, gg)] = self.inc("V", self.s_ve, 1)

        def emit_y_sub(e, gg, dout):
            r = gg % 2
            hreg = self.h_m[r]
            slot, wv = self.w2_load(self.ew2_pk[e, gg, dout])
            self.wait("T", self.s_w2[slot], wv)
            self.wait("T", self.s_ve, hW_last[(e, gg)])
            self.wait("T", self.s_ve, yEv_prev[0])
            if gg == NGRP - 1 and dout == 0:
                self.wait("T", self.s_eb[e % 2], ebv_of[e])
            last_plain = (gg != NGRP - 1)
            for j in range(GH):
                for tt in range(4):
                    self.em("T", lambda t, j=j, tt=tt, slot=slot, hreg=hreg, lp=last_plain: t.matmul(
                        ps[4 + tt][:], self.w2t(slot, j),
                        hreg[:, j * TOWN + tt * 512:j * TOWN + (tt + 1) * 512],
                        start=(j == 0), stop=(lp and j == GH - 1)))
            if gg == NGRP - 1:
                for tt in range(4):
                    self.em("T", lambda t, tt=tt, e=e, dout=dout: t.matmul(
                        ps[4 + tt][:],
                        self.eb2cur[0:1, (e % 2) * D + dout * 128:(e % 2) * D + (dout + 1) * 128],
                        self.wtmp[0:1, (e % 2) * TOWN + tt * 512:(e % 2) * TOWN + (tt + 1) * 512],
                        start=False, stop=True))
            mmY = self.inc("T", self.s_mm, 1)
            self.w2_ev.append((self.s_mm, mmY))
            if dout == KC - 1:
                regA_ev[r] = mmY
            if gg == NGRP - 1 and dout == GH - 1:
                wtmpRead[e % 2] = mmY
            self.wait("V", self.s_mm, mmY)
            self.wait("V", self.s_ve, yEv_prev[0])
            for tt in range(4):
                dst = self.acc[:, dout * TOWN + tt * 512:dout * TOWN + (tt + 1) * 512]
                if e == 0 and gg == 0:
                    self.em("V", lambda v, dst=dst, tt=tt: v.tensor_copy(dst, ps[4 + tt][:]))
                else:
                    self.em("V", lambda v, dst=dst, tt=tt: v.tensor_add(dst, dst, ps[4 + tt][:]))
            yEv_prev[0] = self.inc("V", self.s_ve, 1)

        prev = None
        nh = [0]
        for e in range(self.moe_e):
            if self.moe_wb:
                emit_wb(e)
            for gg in range(NGRP):
                for i in range(GH):
                    if nh[0] >= self.hblk:
                        break
                    emit_h_block(e, gg, i)
                    nh[0] += 1
                    if prev is not None and self.moe_y:
                        emit_y_sub(prev[0], prev[1], i)
                if nh[0] >= self.hblk:
                    break
                prev = (e, gg)
            if nh[0] >= self.hblk:
                break
        if self.moe_y:
            for i in range(GH):
                emit_y_sub(prev[0], prev[1], i)

    def emit_b1(self):
        ps = self.ps
        evQpar = [0, 0]
        evQs = [0, 0]
        evDen = [0]
        evZB = [0]
        evZBp = [0, 0]
        evAOZ = [0, 0]
        evAOZ_last = [0]
        aoz_done = [0, 0]
        evX1par = [0, 0]
        xf_prev = [0, 0]
        xf_evac = [0, 0]
        xn_ready = [0, 0]

        def qproj_n(s, n):
            slot, wv = self.w1_load(self.wq_pk[n])
            self.wait("T", self.s_w1[slot], wv)
            self.wait("T", self.s_ve, evQpar[n % 2])
            for k in range(KC):
                for sub in range(2):
                    self.em("T", lambda t, k=k, sub=sub, n=n, s=s, slot=slot: t.matmul(
                        ps[2 * (n % 2) + sub][:], self.w1t(slot, k),
                        self.xnbf[:, k * TOWN + s * 1024 + sub * 512:
                                  k * TOWN + s * 1024 + (sub + 1) * 512],
                        start=(k == 0), stop=(k == KC - 1)))
            mmQ = self.inc("T", self.s_mm, 1)
            self.w1_ev.append((self.s_mm, mmQ))
            for sub in range(2):
                pstile = ps[2 * (n % 2) + sub]
                qsl = self.q_sb[:, n * TOWN + s * 1024 + sub * 512:
                                n * TOWN + s * 1024 + (sub + 1) * 512]
                self.wait("V", self.s_mm, mmQ)
                self.wait("V", self.s_ve, self.s_ve.n)
                self.em("V", lambda v, p=pstile, n=n: v.tensor_scalar(
                    self.elu_r[:], p[:], self.bq_sb[:, n:n + 1], 0.0, op0=OP.add, op1=OP.max))
                vmax = self.vbar()
                self.wait("S", self.s_mm, mmQ)
                self.wait("S", self.s_ve, vmax)
                self.em("S", lambda x, p=pstile, q=qsl, n=n: x.activation(
                    q, p[:], AF.Exp, bias=self.bq_sb[:, n:n + 1]))
                sv = self.inc("S", self.s_sc, 1)
                self.wait("V", self.s_sc, sv)
                self.em("V", lambda v, q=qsl: v.scalar_tensor_tensor(
                    q, q, 1.0, self.elu_r[:], op0=OP.min, op1=OP.add))
                evQpar[n % 2] = self.inc("V", self.s_ve, 1)
            evQs[s] = evQpar[n % 2]

        def mix_sub(s, sub):
            qcol = s * 1024 + sub * 512
            self.wait("T", self.s_ve, evQs[s])
            self.wait("T", self.s_ve, evDen[0])
            if s == 1 and sub == 0:
                self.wait("T", self.s_sc, max(xf_evac))
                self.wait("S", self.s_mm, max(xf_prev))
            for p in range(NPAIR):
                self.em("T", lambda t, p=p, qcol=qcol: t.matmul(
                    ps[4][0:16, :], self.ksbd[:, p * 16:(p + 1) * 16],
                    self.q_sb[:, p * TOWN + qcol:p * TOWN + qcol + 512],
                    start=(p == 0), stop=(p == NPAIR - 1)))
            mmDen = self.inc("T", self.s_mm, 1)
            self.wait("V", self.s_mm, mmDen)
            self.em("V", lambda v: v.tensor_scalar(ps[4][0:16, :], ps[4][0:16, :],
                                                   EPS_ATTN, None, op0=OP.add))
            self.vbar()

            def _recip(v):
                with self.nc.allow_low_precision("z in bf16 (matmul operand)"):
                    return v.reciprocal(self.zbfb[:], ps[4][0:16, :])
            self.em("V", _recip)
            evDen[0] = self.inc("V", self.s_ve, 1)
            for p in range(NPAIR):
                zsl = self.zb[:, (p % 2) * 512:(p % 2 + 1) * 512]
                self.wait("T", self.s_ve, evDen[0])
                self.wait("T", self.s_sc, evZB[0])
                self.em("T", lambda t, p=p: t.matmul(
                    ps[5][:], self.e2_sb[:, p * 128:(p + 1) * 128], self.zbfb[:],
                    start=True, stop=True))
                mmZB = self.inc("T", self.s_mm, 1)
                self.wait("S", self.s_mm, mmZB)
                self.wait("S", self.s_ve, evAOZ[p % 2] if p > 1 else evAOZ_last[0])
                self.em("S", lambda x, zsl=zsl: x.copy(zsl, ps[5][:]))
                evZB[0] = self.inc("S", self.s_sc, 1)
                evZBp[p % 2] = evZB[0]
                self.wait("T", self.s_ve, evAOZ[p % 2])
                self.em("T", lambda t, p=p, qcol=qcol: t.matmul(
                    ps[6 + p % 2][:], self.m_sb[:, p * 128:(p + 1) * 128],
                    self.q_sb[:, p * TOWN + qcol:p * TOWN + qcol + 512],
                    start=True, stop=True))
                mmQM = self.inc("T", self.s_mm, 1)
                self.wait("V", self.s_mm, mmQM)
                self.wait("V", self.s_sc, evZBp[p % 2])
                self.em("V", lambda v, p=p, qcol=qcol, zsl=zsl: v.tensor_mul(
                    self.q_sb[:, p * TOWN + qcol:p * TOWN + qcol + 512],
                    ps[6 + p % 2][:], zsl))
                evAOZ[p % 2] = self.inc("V", self.s_ve, 1)
            evAOZ_last[0] = evAOZ[(NPAIR - 1) % 2]
            aoz_done[s] = max(evAOZ)

        def xnrec(s):
            self.wait("T", self.s_sc, evZB[0])
            self.wait("Y", self.s_ve, aoz_done[s])
            if s == 1:
                self.wait("S", self.s_ve, max(evX1par))
            for j in range(8):
                gown = s * 8 + j
                for fh in range(2):
                    sl = fh
                    xt = self.xt_b1[:]
                    if xf_prev[0] or xf_prev[1]:
                        self.wait("Y", self.s_mm, max(xf_prev))
                    lxv = self.dma("Y", xt,
                                   self.xs[gown * 128:(gown + 1) * 128, fh * 512:(fh + 1) * 512],
                                   self.s_lx[sl])
                    self.wait("V", self.s_lx[sl], lxv)
                    self.em("V", lambda v, xt=xt, gown=gown: v.tensor_scalar(
                        xt, xt, self.mstat[:, gown:gown + 1], self.rstat[:, gown:gown + 1],
                        op0=OP.add, op1=OP.mult))
                    vn = self.inc("V", self.s_ve, 1)
                    self.wait("T", self.s_ve, vn)
                    self.wait("T", self.s_sc, xf_evac[sl])
                    pb = 4 + fh
                    for cc in range(4):
                        self.em("T", lambda t, xt=xt, cc=cc, pb=pb: t.transpose(
                            ps[pb][:, cc * 128:(cc + 1) * 128], xt[:, cc * 128:(cc + 1) * 128],
                            self.ident[:]))
                    xf_prev[sl] = self.inc("T", self.s_mm, 1)
                    self.wait("S", self.s_mm, xf_prev[sl])
                    for cc in range(4):
                        c = fh * 4 + cc
                        self.em("S", lambda x, c=c, j=j, cc=cc, pb=pb: x.copy(
                            self.xnfm[:, c * 1024 + j * 128:c * 1024 + (j + 1) * 128],
                            ps[pb][:, cc * 128:(cc + 1) * 128]))
                    xf_evac[sl] = self.inc("S", self.s_sc, 1)
            xn_ready[s] = self.s_sc.n

        def wo_n(s, n):
            slot, wv = self.w1_load(self.wo_pk[n])
            self.wait("T", self.s_w1[slot], wv)
            self.wait("T", self.s_ve, evX1par[n % 2])
            if n == 0:
                self.wait("T", self.s_ve, aoz_done[s])
            for k in range(KC):
                for sub in range(2):
                    self.em("T", lambda t, k=k, sub=sub, n=n, s=s, slot=slot: t.matmul(
                        ps[2 * (n % 2) + sub][:], self.w1t(slot, k),
                        self.q_sb[:, k * TOWN + s * 1024 + sub * 512:
                                  k * TOWN + s * 1024 + (sub + 1) * 512],
                        start=(k == 0), stop=(k == KC - 1)))
            mmWo = self.inc("T", self.s_mm, 1)
            self.w1_ev.append((self.s_mm, mmWo))
            self.wait("V", self.s_mm, mmWo)
            self.wait("V", self.s_sc, xn_ready[s])
            for sub in range(2):
                self.em("V", lambda v, n=n, s=s, sub=sub: v.scalar_tensor_tensor(
                    self.x1_sb[:, n * TOWN + s * 1024 + sub * 512:
                               n * TOWN + s * 1024 + (sub + 1) * 512],
                    ps[2 * (n % 2) + sub][:], self.bo_sb[:, n:n + 1],
                    self.xnfm[:, n * 1024 + sub * 512:n * 1024 + (sub + 1) * 512],
                    op0=OP.add, op1=OP.add))
            evX1par[n % 2] = self.inc("V", self.s_ve, 1)

        for n in range(KC):
            qproj_n(0, n)
        for n in range(KC):
            qproj_n(1, n)
            if n == 3:
                mix_sub(0, 0)
            if n == 7:
                mix_sub(0, 1)
        xnrec(0)
        for n in range(KC):
            wo_n(0, n)
            if n == 3:
                mix_sub(1, 0)
            if n == 7:
                mix_sub(1, 1)
        xnrec(1)
        for n in range(KC):
            wo_n(1, n)

    def emit_ln(self, which):
        if which == 2:
            self.ln2_bc = {}
            self.ln2_apply = {}
        ps = self.ps
        g_sb = self.g1_sb if which == 1 else self.g2_sb
        be_sb = self.be1_sb if which == 1 else self.be2_sb
        evMMpar = [0, 0]
        evRowEvac = 0
        evApply = {}

        def tile_c(tt, c):
            sl = c % 2
            xsl = self.x1_sb[:, c * TOWN + tt * 512:c * TOWN + (tt + 1) * 512]
            b1 = self.ln_bf[:, sl * 512:(sl + 1) * 512]
            b2 = self.ln_bf[:, (2 + sl) * 512:(3 + sl) * 512]
            self.wait("S", self.s_mm, evMMpar[sl])
            self.em("S", lambda s, b1=b1, x=xsl: s.copy(b1, x))
            self.chain("S", self.s_sc)
            self.em("S", lambda s, b2=b2, x=xsl: s.activation(b2, x, AF.Square))
            return self.inc("S", self.s_sc, 1)

        seq = [(t2, c2) for t2 in range(4) for c2 in range(KC)]
        vb = {}
        vb[seq[0]] = tile_c(*seq[0])
        vb[seq[1]] = tile_c(*seq[1])
        ptr = 2
        for tt in range(4):
            for c in range(KC):
                sl = c % 2
                b1 = self.ln_bf[:, sl * 512:(sl + 1) * 512]
                b2 = self.ln_bf[:, (2 + sl) * 512:(3 + sl) * 512]
                self.wait("T", self.s_sc, vb[(tt, c)])
                if c == 0:
                    self.wait("T", self.s_ve, evRowEvac)
                self.em("T", lambda t, b1=b1, c=c: t.matmul(
                    ps[0][0:1, :], self.onescol_bf[:], b1, start=(c == 0), stop=(c == KC - 1)))
                self.em("T", lambda t, b2=b2, c=c: t.matmul(
                    ps[1][0:1, :], self.onescol_bf[:], b2, start=(c == 0), stop=(c == KC - 1)))
                evMMpar[sl] = self.inc("T", self.s_mm, 1)
                if ptr < len(seq):
                    vb[seq[ptr]] = tile_c(*seq[ptr])
                    ptr += 1
            mmStat = evMMpar[(KC - 1) % 2]
            mrow = self.ln_m[0:1, tt * 512:(tt + 1) * 512]
            rrow = self.ln_r[0:1, tt * 512:(tt + 1) * 512]
            self.wait("V", self.s_mm, mmStat)
            self.em("V", lambda v, m=mrow: v.tensor_scalar_mul(m, ps[0][0:1, :], INV_D))
            self.em("V", lambda v: v.tensor_scalar(self.ln_s1[:], ps[1][0:1, :], INV_D, EPS_LN,
                                                   op0=OP.mult, op1=OP.add))
            self.vbar()
            self.em("V", lambda v, m=mrow: v.tensor_mul(self.ln_s2[:], m, m))
            self.vbar()
            self.em("V", lambda v: v.tensor_sub(self.ln_s1[:], self.ln_s1[:], self.ln_s2[:]))
            evRowEvac = self.inc("V", self.s_ve, 1)
            self.wait("S", self.s_ve, evRowEvac)
            self.em("S", lambda s: s.activation(self.ln_s2[:], self.ln_s1[:], AF.Ln))
            self.chain("S", self.s_sc)
            self.em("S", lambda s: s.activation(self.ln_s2[:], self.ln_s2[:], AF.Exp, scale=-0.5))
            sv = self.inc("S", self.s_sc, 1)
            self.wait("V", self.s_sc, sv)
            self.em("V", lambda v: v.tensor_mul(self.ln_s3[:], self.ln_s2[:], self.ln_s2[:]))
            self.vbar()
            self.em("V", lambda v: v.tensor_mul(self.ln_s3[:], self.ln_s3[:], self.ln_s1[:]))
            self.vbar()
            self.em("V", lambda v: v.tensor_scalar(self.ln_s3[:], self.ln_s3[:], -0.5, 1.5,
                                                   op0=OP.mult, op1=OP.add))
            self.vbar()
            self.em("V", lambda v, r=rrow: v.tensor_mul(r, self.ln_s3[:], self.ln_s2[:]))
            vr = self.inc("V", self.s_ve, 1)
            self.wait("T", self.s_ve, vr)
            self.wait("T", self.s_ve, evApply.get((tt - 1, KC - 1), 0))
            self.em("T", lambda t, m=mrow: t.matmul(ps[2][:], self.onesrow_f[:], m,
                                                    start=True, stop=True))
            self.em("T", lambda t, r=rrow: t.matmul(ps[3][:], self.onesrow_f[:], r,
                                                    start=True, stop=True))
            mmB = self.inc("T", self.s_mm, 1)
            if which == 2:
                self.ln2_bc[tt] = mmB
            self.wait("V", self.s_mm, mmB)
            stages = []
            for c in range(KC):
                stages.append(("fin", c - 2))
                stages.append(("mul", c - 1))
                stages.append(("sub", c))
            stages += [("fin", KC - 2), ("mul", KC - 1), ("fin", KC - 1)]
            for step, c in stages:
                if c < 0:
                    continue
                sl = c % 2
                lts = self.ln_t[:, sl * 512:(sl + 1) * 512]
                xsl = self.x1_sb[:, c * TOWN + tt * 512:c * TOWN + (tt + 1) * 512]
                if step == "sub":
                    prev = (tt, c - 2) if c >= 2 else (tt - 1, KC - 2 + c)
                    self.wait("V", self.s_ve, evApply.get(prev, 0))
                    self.em("V", lambda v, x=xsl, lts=lts: v.tensor_sub(lts, x, ps[2][:]))
                    self.inc("V", self.s_ve, 1)
                elif step == "mul":
                    self.wait("V", self.s_ve, self.s_ve.n)
                    self.em("V", lambda v, lts=lts: v.tensor_mul(lts, lts, ps[3][:]))
                    self.inc("V", self.s_ve, 1)
                else:
                    self.wait("V", self.s_ve, self.s_ve.n)
                    if which == 1:
                        self.em("V", lambda v, x=xsl, lts=lts, c=c: v.tensor_scalar(
                            x, lts, g_sb[:, c:c + 1], be_sb[:, c:c + 1],
                            op0=OP.mult, op1=OP.add))
                        evApply[(tt, c)] = self.inc("V", self.s_ve, 1)
                    else:
                        self.em("V", lambda v, lts=lts, c=c: v.tensor_scalar(
                            lts, lts, g_sb[:, c:c + 1], be_sb[:, c:c + 1],
                            op0=OP.mult, op1=OP.add))
                        self.vbar()
                        asl = self.acc[:, c * TOWN + tt * 512:c * TOWN + (tt + 1) * 512]
                        self.em("V", lambda v, a=asl, lts=lts: v.tensor_add(a, a, lts))
                        evApply[(tt, c)] = self.inc("V", self.s_ve, 1)
                        if c == KC - 1:
                            self.ln2_apply[tt] = evApply[(tt, c)]
            if which == 1:
                for c in range(KC):
                    csl = self.c_bf[:, c * TOWN + tt * 512:c * TOWN + (tt + 1) * 512]
                    xsl = self.x1_sb[:, c * TOWN + tt * 512:c * TOWN + (tt + 1) * 512]
                    self.wait("S", self.s_ve, evApply[(tt, c)])
                    self.em("S", lambda s, csl=csl, x=xsl: s.copy(csl, x))
                    self.chain("S", self.s_sc)

    def emit_b2(self):
        ps = self.ps
        hprev_tt = [0] * 4
        hlast_tt = [0] * 4
        mmY_tt = [0] * 4
        yadd_tt = [0] * 4
        for gg in range(NGRP):
            for i in range(GH):
                n = gg * GH + i
                slot, wv = self.w1_load(self.w1_pk[n])
                self.wait("T", self.s_w1[slot], wv)
                mmH_tt = [0] * 4
                for tt in range(4):
                    self.wait("T", self.s_sc, hprev_tt[tt])
                    if i == 0:
                        self.wait("T", self.s_mm, mmY_tt[tt])
                    for k in range(KC):
                        self.em("T", lambda t, k=k, tt=tt, slot=slot: t.matmul(
                            ps[tt][:], self.w1t(slot, k),
                            self.c_bf[:, k * TOWN + tt * 512:k * TOWN + (tt + 1) * 512],
                            start=(k == 0), stop=(k == KC - 1)))
                    mmH_tt[tt] = self.inc("T", self.s_mm, 1)
                self.w1_ev.append((self.s_mm, mmH_tt[3]))
                for tt in range(4):
                    self.wait("S", self.s_mm, mmH_tt[tt])
                    self.em("S", lambda s, tt=tt, i=i, n=n: s.activation(
                        self.h_b2[:, i * TOWN + tt * 512:i * TOWN + (tt + 1) * 512],
                        ps[tt][:], AF.Relu, bias=self.b1_sb[:, n:n + 1]))
                    hprev_tt[tt] = self.inc("S", self.s_sc, 1)
                hlast_tt = list(hprev_tt)
            for dout in range(KC):
                slot, wv = self.w2_load(self.w2_pk[gg, dout])
                self.wait("T", self.s_w2[slot], wv)
                mmYc = [0] * 4
                for tt in range(4):
                    self.wait("T", self.s_sc, hlast_tt[tt])
                    self.wait("T", self.s_ve, yadd_tt[tt])
                    for j in range(GH):
                        self.em("T", lambda t, j=j, tt=tt, slot=slot: t.matmul(
                            ps[4 + tt][:], self.w2t(slot, j),
                            self.h_b2[:, j * TOWN + tt * 512:j * TOWN + (tt + 1) * 512],
                            start=(j == 0), stop=(j == GH - 1)))
                    mmYc[tt] = self.inc("T", self.s_mm, 1)
                self.w2_ev.append((self.s_mm, mmYc[3]))
                if dout == KC - 1:
                    mmY_tt = mmYc
                for tt in range(4):
                    xsl = self.x1_sb[:, dout * TOWN + tt * 512:dout * TOWN + (tt + 1) * 512]
                    self.wait("V", self.s_mm, mmYc[tt])
                    self.wait("V", self.s_ve, yadd_tt[tt])
                    if gg == 0:
                        self.em("V", lambda v, x=xsl, tt=tt, dout=dout: v.scalar_tensor_tensor(
                            x, ps[4 + tt][:], self.b2a_sb[:, dout:dout + 1], x,
                            op0=OP.add, op1=OP.add))
                    else:
                        self.em("V", lambda v, x=xsl, tt=tt: v.tensor_add(x, x, ps[4 + tt][:]))
                    yadd_tt[tt] = self.inc("V", self.s_ve, 1)

    def emit_b4(self):
        ps = self.ps
        evAdd = {}
        ldsem = [self.s_eb[0], self.s_eb[1], self.s_w1[0], self.s_w1[1]]
        stsem = [self.s_st[0], self.s_st[1], self.s_lx[0], self.s_lx[1]]
        lxv = {}
        stv = {}
        for j in range(NGOWN):
            sl = j % 4
            par = j % 2
            xt = self.xtm_b4[:, sl * D:(sl + 1) * D]
            ot = self.otm_b4[:, sl * D:(sl + 1) * D]
            if j == 0:
                self.wait("G", self.s_mm, self.ln2_bc[3])
            if j >= 4:
                self.wait("G", self.s_ve, evAdd[j - 4])
            lxv[j] = self.dma("G", xt, self.xs[j * 128:(j + 1) * 128, :], ldsem[sl])
            pbase = 4 if par == 0 else 6
            self.wait("T", self.s_ve, max(evAdd.get(j - 2, 0), self.ln2_apply[j // 4]))
            for c in range(KC):
                self.em("T", lambda t, c=c, j=j, pbase=pbase: t.transpose(
                    ps[pbase + c // 4][:, (c % 4) * 128:(c % 4 + 1) * 128],
                    self.acc[:, c * TOWN + j * 128:c * TOWN + (j + 1) * 128], self.ident[:]))
            mmT = self.inc("T", self.s_mm, 1)
            self.wait("V", self.s_mm, mmT)
            if j == 0:
                self.wait("V", self.s_ve, self.ln2_apply[3])
            self.wait("V", ldsem[sl], lxv[j])
            if j >= 4:
                self.wait("V", stsem[sl], stv[j - 4])
            for hh in range(2):
                self.em("V", lambda v, ot=ot, xt=xt, hh=hh, pbase=pbase: v.tensor_add(
                    ot[:, hh * 512:(hh + 1) * 512], xt[:, hh * 512:(hh + 1) * 512],
                    ps[pbase + hh][:]))
            evAdd[j] = self.inc("V", self.s_ve, 1)
            self.wait("Y", self.s_ve, evAdd[j])
            stv[j] = self.dma("Y", self.out[j * 128:(j + 1) * 128, :], ot, stsem[sl])
        for s in stsem:
            self.wait("Y", s, s.n)
        for s in ldsem:
            self.wait("Y", s, s.n)
        if self.s_sd.n:
            self.wait("Y", self.s_sd, self.s_sd.n)
        if self.s_dbg.n:
            self.wait("Y", self.s_dbg, self.s_dbg.n)

    def build(self, phases=("A", "MOE", "B1", "LN1", "B2", "LN2", "B4")):
        self.emit_aux()
        if "A0" in phases:
            self.emit_phase_a()
            self.barrier()
        if "A" in phases:
            self.emit_phase_a()
            self.emit_phase_a_tail()
            self.barrier()
            self._dump("xnbf", self.xnbf, cast=True)
            self._dump("msb", self.m_sb, cast=True)
            self._dump("ksbd", self.ksbd, cast=True)
            self._dump("wrow", self.wrow_f)
            self._dump("hall", self.hall)
            self._dump("mstat", self.mstat)
            self._dump("rstat", self.rstat)
            self._dump("ktm", self.ktm, cast=True)
        if "MOE" in phases:
            self.emit_moe()
            self.barrier()
            self._dump("accm", self.acc)
        if "B1" in phases:
            self.emit_b1()
            self.barrier()
            self._dump("x1_pre", self.x1_sb)
            self._dump("qsb", self.q_sb, cast=True)
        if "LN1" in phases:
            self.emit_ln(1)
            self.barrier()
            self._dump("cbf", self.c_bf, cast=True)
        if "B2" in phases:
            self.emit_b2()
            self.barrier()
            self._dump("x1n", self.x1_sb)
        if "LN2" in phases:
            self.emit_ln(2)
            if "accf" in self.debug or "B4" not in phases:
                self.barrier()
                self._dump("accf", self.acc)
        if "B4" in phases:
            self.emit_b4()
        else:
            self.barrier()
            fb = self.acc[:, 0:D] if "MOE" in phases else self.kvacc[:]
            self.dma("Y", self.out[0:128, :], fb, self.s_st[0])
            self.wait("Y", self.s_st[0], self.s_st[0].n)
            if self.s_sd.n:
                self.wait("Y", self.s_sd, self.s_sd.n)
            if self.s_dbg.n:
                self.wait("Y", self.s_dbg, self.s_dbg.n)
        nc = self.nc
        with nc.Block() as block:
            @block.tensor
            def _(t):
                for fn in self.L["T"]:
                    fn(t)

            @block.vector
            def _(v):
                for fn in self.L["V"]:
                    fn(v)

            @block.scalar
            def _(s):
                for fn in self.L["S"]:
                    fn(s)

            @block.gpsimd
            def _(g):
                for fn in self.L["G"]:
                    fn(g)

            @block.sync
            def _(y):
                for fn in self.L["Y"]:
                    fn(y)
        return nc


def host_pack(inputs):
    f32 = lambda a: np.ascontiguousarray(np.asarray(a, np.float32))
    shared = {
        "wq_pk": pack_w1(f32(inputs["L0_attn_Wq"])),
        "wo_pk": pack_w1(f32(inputs["L0_attn_Wo"])),
        "wk_nat": f32(inputs["L0_attn_Wk"]),
        "wv_nat": f32(inputs["L0_attn_Wv"]),
        "w1_pk": pack_w1(f32(inputs["L0_attn_W1"])),
        "w2_pk": pack_w2(f32(inputs["L0_attn_W2"])),
        "ew1_pk": np.stack([pack_w1(f32(inputs["L0_ffn_W1"][e])) for e in range(NEXP)]),
        "ew2_pk": np.stack([pack_w2(f32(inputs["L0_ffn_W2"][e])) for e in range(NEXP)]),
        "bq_pp": pp(f32(inputs["L0_attn_bq"])),
        "bo_pp": pp(f32(inputs["L0_attn_bo"])),
        "g1_pp": pp(f32(inputs["L0_attn_g1"])),
        "be1_pp": pp(f32(inputs["L0_attn_beta1"])),
        "g2_pp": pp(f32(inputs["L0_attn_g2"])),
        "be2_pp": pp(f32(inputs["L0_attn_beta2"])),
        "b2a_pp": pp(f32(inputs["L0_attn_b2"])),
        "b1_pp": pp(f32(inputs["L0_attn_b1"])),
        "eb1_pp": np.concatenate([pp(f32(inputs["L0_ffn_b1"][e])) for e in range(NEXP)], axis=1),
        "bk_row": f32(inputs["L0_attn_bk"]).reshape(1, D),
        "bv_row": f32(inputs["L0_attn_bv"]).reshape(1, D),
        "eb2_rows": f32(inputs["L0_ffn_b2"]).reshape(NEXP, D),
        "ident_c": np.eye(128, dtype=np.float32),
        "onescol_c": np.ones((128, 1), np.float32),
        "onesrow_c": np.ones((1, 128), np.float32),
        "e2_c": make_e2(),
        "ecol_c": np.arange(NEXP, dtype=np.float32).reshape(NEXP, 1),
        "mkc_c": np.tile(np.array([[1.0, -7.0]], np.float32), (NEXP, 1)),
        "nwc_c": np.tile(np.array([[-0.5, 1.5]], np.float32), (128, 1)),
    }
    return {k: np.ascontiguousarray(v) for k, v in shared.items()}


def make_in_maps(inputs):
    x = np.ascontiguousarray(np.asarray(inputs["L0_attn_x"], np.float32))
    shared = host_pack(inputs)
    in_maps = []
    for c in range(8):
        seq, half = c // 2, c % 2
        own = x[seq, half * TOWN:(half + 1) * TOWN]
        other = x[seq, (1 - half) * TOWN:(2 - half) * TOWN]
        m = dict(shared)
        m["xs"] = np.ascontiguousarray(np.concatenate([own, other], axis=0))
        in_maps.append(m)
    return in_maps


def kernel(**inputs):
    nc = KB().build()
    in_maps = make_in_maps(inputs)
    res = run_bass_kernel_spmd(nc, in_maps, core_ids=list(range(8)))
    out = np.empty((4, 4096, 1024), np.float32)
    for c in range(8):
        seq, half = c // 2, c % 2
        out[seq, half * TOWN:(half + 1) * TOWN] = res.results[c]["out"]
    return out
